# version 14
# baseline (speedup 1.0000x reference)
import contextlib
import os
import numpy as np

import concourse.bass as bass
import concourse.mybir as mybir
from concourse.bass_utils import run_bass_kernel_spmd

F32 = mybir.dt.float32
BF16 = mybir.dt.bfloat16
I32 = mybir.dt.int32
I8 = mybir.dt.int8
AF = mybir.ActivationFunctionType
ALU = mybir.AluOpType
AX = mybir.AxisListType.X

P = 128
B, T, D = 4, 8192, 1024
DK = D // P
F3 = 3 * D
FF = 4096
FK = FF // P
SUP = 256
NSUP = T // SUP
TC = 32
NCH = T // TC
CIRC = 2
YCIRC = 256
TILE = 128
NTILE = T // TILE
NTL = NTILE // 2
EPS = 1e-5
NEWT = 11


class Sem:
    def __init__(self, ctx, nc, name):
        self.h = ctx.enter_context(nc.semaphore(name))
        self.n = 0

    def inc(self, instr, k=1):
        instr.then_inc(self.h, k)
        self.n += k
        return instr

    def reserve(self, k=1):
        self.n += k
        return self.n

    def incr(self, instr, k=1):
        instr.then_inc(self.h, k)
        return instr


def build_nc():
    nc = bass.Bass("TRN2", target_bir_lowering=False)

    dx = nc.dram_tensor("x", [T, D], F32, kind="ExternalInput").ap()
    dxh = nc.dram_tensor("xh", [T // 2, D], F32, kind="ExternalInput").ap()
    dW = nc.dram_tensor("W", [F3, D], F32, kind="ExternalInput").ap()
    dbv = nc.dram_tensor("bvec", [1, F3], F32, kind="ExternalInput").ap()
    dgam = nc.dram_tensor("gamma", [1, D], F32, kind="ExternalInput").ap()
    dbet = nc.dram_tensor("beta", [1, D], F32, kind="ExternalInput").ap()
    du = nc.dram_tensor("u", [1, D], F32, kind="ExternalInput").ap()
    dw1 = nc.dram_tensor("w1", [FF, D], F32, kind="ExternalInput").ap()
    db1 = nc.dram_tensor("b1", [1, FF], F32, kind="ExternalInput").ap()
    dw2 = nc.dram_tensor("w2", [D, FF], F32, kind="ExternalInput").ap()
    db2 = nc.dram_tensor("b2", [1, D], F32, kind="ExternalInput").ap()
    didf = nc.dram_tensor("identf", [P, P], F32, kind="ExternalInput").ap()
    didb = nc.dram_tensor("identb", [P, P], BF16, kind="ExternalInput").ap()
    dyidx = nc.dram_tensor("yidx", [P, NTL], I32, kind="ExternalInput").ap()
    dyo = nc.dram_tensor("yo", [T // 2, D], F32, kind="ExternalOutput").ap()

    SCW = (T + 2) * DK
    _dbg = "Internal" if not os.environ.get("KDBG") else "ExternalOutput"
    sQ2 = nc.dram_tensor("sQ2", [P, T * 2 * DK], F32, kind=_dbg).ap()
    sP = nc.dram_tensor("sP", [P, SCW], F32, kind=_dbg).ap()
    sZ = nc.dram_tensor("sZ", [P, SCW], F32, kind=_dbg).ap()
    sC = nc.dram_tensor("sC", [P, SCW], F32, kind=_dbg).ap()
    sOZ = nc.dram_tensor("sOZ", [P, SCW], F32, kind="Internal").ap()
    ytm = nc.dram_tensor("ytm", [T, D], F32, kind=_dbg).ap()
    w1qt = nc.dram_tensor("w1qt", [P, DK * FF], BF16, kind="Internal").ap()
    w2qt = nc.dram_tensor("w2qt", [P, FK * D], BF16, kind="Internal").ap()

    ctx = contextlib.ExitStack()
    osb = lambda n_, s, d: ctx.enter_context(nc.sbuf_tensor(n_, s, d))
    psT = lambda n_, s, d: ctx.enter_context(nc.psum_tensor(n_, s, d))
    mksem = lambda n_: Sem(ctx, nc, n_)

    idf = osb("idf", [P, P], F32)
    idb = osb("idb", [P, P], BF16)
    ones1 = osb("ones1", [1, P], F32)
    onesc = osb("onesc", [P, 1], F32)
    btile = osb("btile", [P, 24], F32)
    utile = osb("utile", [P, DK], F32)
    bprime = osb("bprime", [P, 24], F32)
    b1rep = osb("b1rep", [P, FF], BF16)
    b2rep = osb("b2rep", [P, D], F32)
    swrep = osb("swrep", [P, 4], F32)
    yidxt = osb("yidxt", [P, NTL], I32)
    rowbuf = osb("rowbuf", [1, FF], F32)

    s_misc = mksem("s_misc")
    s_wld2 = [mksem("s_wld0"), mksem("s_wld1")]
    s_pe = mksem("s_pe")
    s_ve = mksem("s_ve")
    s_ae = mksem("s_ae")
    s_po = mksem("s_po")
    s_st = mksem("s_st")
    s_sg2 = [mksem("s_sg0"), mksem("s_sg1")]
    s_xl2 = [mksem("s_xl0"), mksem("s_xl1")]
    s_ldc2 = [mksem("s_ldc0"), mksem("s_ldc1")]
    s_ha = mksem("s_ha")
    s_vg = mksem("s_vg")
    s_ytr = mksem("s_ytr")
    s_yse = mksem("s_yse")
    s_yst2 = [mksem("s_yst0"), mksem("s_yst1"), mksem("s_yst2")]
    s_out = mksem("s_out")
    s_rt = mksem("s_rt")
    s_vs = mksem("s_vs")
    s_pes = mksem("s_pes")
    s_pz = mksem("s_pz")
    s_pm = mksem("s_pm")
    s_pb = mksem("s_pb")
    s_bar = mksem("s_bar")
    s_pq = mksem("s_pq")
    s_oh = mksem("s_oh")

    ps_a = psT("ps_a", [P, 512], F32)
    ps_b = psT("ps_b", [P, 512], F32)
    ps_c = psT("ps_c", [P, 512], F32)
    ps_d = psT("ps_d", [P, 512], F32)
    ps_tb = psT("ps_tb", [P, P], BF16)
    ps_tb2 = psT("ps_tb2", [P, P], BF16)
    ps_tf = psT("ps_tf", [P, P], F32)
    ps_tf2 = psT("ps_tf2", [P, P], F32)

    def rt(v):
        v.drain()

    def rsqrt_seed(v, dst, src, scr16):
        import math
        for k in range(16):
            v0 = 3.0e4 * (0.25 ** k)
            c2 = 1.5 / math.sqrt(v0)
            c1 = -0.5 / v0 ** 1.5
            v.tensor_scalar(scr16[:, k:k + 1], src, c1, c2, ALU.mult, ALU.add)
        v.drain()
        v.tensor_reduce(dst, scr16[:], AX, ALU.max)
        v.drain()

    def rsqrt_newton(gp, dst, src, scr):
        for _ in range(7):
            gp.tensor_tensor(scr, dst, dst, ALU.mult)
            gp.tensor_tensor(scr, src, scr, ALU.mult)
            gp.tensor_scalar(scr, scr, -0.5, 1.5, ALU.mult, ALU.add)
            gp.tensor_tensor(dst, dst, scr, ALU.mult)

    def heron_rsqrt(v, dst, src, scr, scr2):
        v.tensor_scalar(scr, src, 1.0, 0.5, ALU.add, ALU.mult)
        for _ in range(NEWT):
            v.drain()
            v.reciprocal(scr2, scr)
            v.drain()
            v.tensor_tensor(scr2, src, scr2, ALU.mult)
            v.drain()
            v.tensor_tensor(scr, scr, scr2, ALU.add)
            v.drain()
            v.tensor_scalar(scr, scr, 0.5, None, ALU.mult)
        v.drain()
        v.reciprocal(dst, scr)
        v.drain()

    wtctx = contextlib.ExitStack()
    wT = wtctx.enter_context(nc.sbuf_tensor("wT", [P, 24 * DK * P], BF16))
    actx = contextlib.ExitStack()
    asb = lambda n_, s, d: actx.enter_context(nc.sbuf_tensor(n_, s, d))
    wld = [asb(f"wld{i}", [P, FF], F32) for i in range(2)]
    wbuf = asb("wbuf", [P, D], BF16)
    grep = asb("grep", [P, D], F32)
    betrep = asb("betrep", [P, D], F32)
    urep = asb("urep", [P, D], F32)
    brep = asb("brep", [P, F3], F32)
    wtmp = asb("wtmp", [P, D], F32)
    wtmp2 = asb("wtmp2", [P, FF], F32)
    wsgn = asb("wsgn", [P, FF], F32)
    wstat = asb("wstat", [P, 48], F32)
    stg = [asb(f"stg{i}", [P, P], BF16) for i in range(2)]
    qint = asb("qint", [P, FF], I32)
    qtile = asb("qtile", [P, FF], BF16)
    sc1 = asb("sc1", [1, 8], F32)

    with nc.Block() as blk:
        def _s(f):
            blk.sync(f)

        def _v(f):
            blk.vector(f)

        def _a(f):
            blk.scalar(f)

        def _p(f):
            blk.tensor(f)

        def _g(f):
            blk.gpsimd(f)

        def sync0(sync):
            for dst, src in [(idf, didf), (idb, didb), (yidxt, dyidx)]:
                s_misc.inc(sync.dma_start(out=dst[:], in_=src[:]), 16)
            s_misc.inc(sync.dma_start(out=rowbuf[0:1, 0:D], in_=dgam[:]), 16)
        _s(sync0)

        def v0(v):
            v.memset(ones1[:], 1.0)
            v.memset(onesc[:], 1.0)
            s_ve.inc(v.engine_nop())
        _v(v0)

        def brow(dst, N):
            nch = (N + 511) // 512
            for j in range(nch):
                n0, n1 = j * 512, min(N, (j + 1) * 512)
                pss = [ps_a, ps_b][j % 2]
                mw, vw = s_misc.n, s_ve.n

                def pef(t, pss=pss, n0=n0, n1=n1, mw=mw, vw=vw):
                    t.wait_ge(s_misc.h, mw)
                    t.wait_ge(s_ve.h, vw)
                    s_pe.inc(t.matmul(pss[:, 0:n1 - n0], ones1[:],
                                      rowbuf[0:1, n0:n1], start=True, stop=True))
                _p(pef)
                pw = s_pe.n

                def vef(v, pss=pss, n0=n0, n1=n1, pw=pw, dst=dst):
                    v.wait_ge(s_pe.h, pw)
                    s_ve.inc(v.tensor_copy(dst[:, n0:n1], pss[:, 0:n1 - n0]))
                _v(vef)

        def reload_row(src, n):
            vw = s_ve.n

            def syr(sync, src=src, n=n, vw=vw):
                sync.wait_ge(s_ve.h, vw)
                s_misc.inc(sync.dma_start(out=rowbuf[0:1, 0:n], in_=src[:]), 16)
            _s(syr)

        brow(grep, D)
        reload_row(dbet, D)
        brow(betrep, D)
        reload_row(du, D)
        brow(urep, D)
        reload_row(dbv, F3)
        brow(brep, F3)
        reload_row(db1, FF)
        brow(b1rep, FF)
        reload_row(db2, D)
        brow(b2rep, D)

        def mk_tile(dst, rep, nk):
            for k in range(nk):
                pst = [ps_tf, ps_tf2][k % 2]
                vw = s_ve.n

                def pef(t, pst=pst, k=k, rep=rep, vw=vw):
                    t.wait_ge(s_ve.h, vw)
                    s_pe.inc(t.transpose(pst[:], rep[:, k * P:(k + 1) * P], idf[:]))
                _p(pef)
                pw = s_pe.n

                def vef(v, pst=pst, k=k, dst=dst, pw=pw):
                    v.wait_ge(s_pe.h, pw)
                    s_ve.inc(v.tensor_copy(dst[:, k:k + 1], pst[:, 0:1]))
                _v(vef)

        mk_tile(btile, brep, 24)
        mk_tile(utile, urep, DK)

        wev_marks = []
        for r in range(24):
            vw = s_ve.n

            def syw(sync, r=r, vw=vw):
                if r >= 2:
                    sync.wait_ge(s_ve.h, vw)
                s_wld2[r % 2].inc(sync.dma_start(
                    out=wld[r % 2][:, 0:D // 2],
                    in_=dW[r * P:(r + 1) * P, 0:D // 2]), 16)
            _s(syw)

            def syw2(a, r=r, vw=vw):
                if r >= 2:
                    a.wait_ge(s_ve.h, vw)
                s_wld2[r % 2].inc(a.dma_start(
                    out=wld[r % 2][:, D // 2:D],
                    in_=dW[r * P:(r + 1) * P, D // 2:D]), 16)
            _a(syw2)
            ww = s_wld2[r % 2].n

            pw0 = s_pe.n

            def vw_(v, r=r, ww=ww, pw0=pw0):
                v.wait_ge(s_wld2[r % 2].h, ww)
                v.wait_ge(s_pe.h, pw0)
                v.tensor_tensor(wbuf[:], wld[r % 2][:, 0:D], grep[:],
                                ALU.mult)
                v.tensor_tensor(wtmp[:], wld[r % 2][:, 0:D], betrep[:], ALU.mult)
                v.drain()
                v.tensor_reduce(wstat[:, 42:43], wtmp[:], AX, ALU.add)
                v.drain()
                v.tensor_tensor(bprime[:, r:r + 1], wstat[:, 42:43],
                                btile[:, r:r + 1], ALU.add)
                s_ve.inc(v.engine_nop())
            _v(vw_)
            tw = s_ve.n

            evmk = wev_marks
            for k in range(DK):
                pst = [ps_tb, ps_tb2][k % 2]
                bank_free = evmk[-2] if len(evmk) >= 2 else 0

                def pew(t, pst=pst, k=k, tw=tw, bank_free=bank_free):
                    t.wait_ge(s_ve.h, max(tw, bank_free))
                    s_pe.inc(t.transpose(pst[:], wbuf[:, k * P:(k + 1) * P], idb[:]))
                _p(pew)
                pw = s_pe.n

                def vew(v, pst=pst, k=k, r=r, pw=pw):
                    v.wait_ge(s_pe.h, pw)
                    col = (k * 24 + r) * P
                    s_ve.inc(v.tensor_copy(wT[:, col:col + P], pst[:]))
                _v(vew)
                evmk.append(s_ve.n)

        wtiles = [(dw1, r, D) for r in range(FF // P)] + \
                 [(dw2, r, FF) for r in range(D // P)]
        for i, (src, r, cols) in enumerate(wtiles):
            vw = s_ve.n

            def syl(sync, src=src, r=r, cols=cols, i=i, vw=vw):
                sync.wait_ge(s_ve.h, vw)
                s_wld2[i % 2].inc(sync.dma_start(
                    out=wld[i % 2][:, 0:cols // 2],
                    in_=src[r * P:(r + 1) * P, 0:cols // 2]), 16)
            _s(syl)

            def syl2(a, src=src, r=r, cols=cols, i=i, vw=vw):
                a.wait_ge(s_ve.h, vw)
                s_wld2[i % 2].inc(a.dma_start(
                    out=wld[i % 2][:, cols // 2:cols],
                    in_=src[r * P:(r + 1) * P, cols // 2:cols]), 16)
            _a(syl2)
            ww = s_wld2[i % 2].n

            def vst(v, i=i, cols=cols, ww=ww):
                v.wait_ge(s_wld2[i % 2].h, ww)
                v.tensor_reduce(wstat[:, i:i + 1], wld[i % 2][:, 0:cols], AX,
                                ALU.add, apply_absolute_value=True)
                s_ve.inc(v.engine_nop())
            _v(vst)

        def vsum(v):
            rt(v)
            v.tensor_reduce(wstat[:, 40:41], wstat[:, 0:32], AX, ALU.add)
            v.tensor_reduce(wstat[:, 41:42], wstat[:, 32:40], AX, ALU.add)
            rt(v)
            s_ve.inc(v.engine_nop())
        _v(vsum)

        for which in range(2):
            vw = s_ve.n

            def pesum(t, which=which, vw=vw):
                t.wait_ge(s_ve.h, vw)
                s_pe.inc(t.matmul(ps_c[0:1, 0:1], wstat[:, 40 + which:41 + which],
                                  onesc[:], start=True, stop=True))
            _p(pesum)
            pw = s_pe.n

            def vscal(v, which=which, pw=pw):
                v.wait_ge(s_pe.h, pw)
                denom = 1.0 / (FF * D)
                v.tensor_scalar(sc1[0:1, 0:1], ps_c[0:1, 0:1], denom, EPS,
                                ALU.mult, ALU.max)
                rt(v)
                v.reciprocal(sc1[0:1, 1:2], sc1[0:1, 0:1])
                rt(v)
                s_ve.inc(v.engine_nop())
            _v(vscal)

            for c2 in range(2):
                vw2 = s_ve.n

                def perep(t, c2=c2, vw2=vw2):
                    t.wait_ge(s_ve.h, vw2)
                    s_pe.inc(t.matmul(ps_c[:, 1:2], ones1[:],
                                      sc1[0:1, c2:c2 + 1], start=True, stop=True))
                _p(perep)
                pw2 = s_pe.n

                def vrep(v, which=which, c2=c2, pw2=pw2):
                    v.wait_ge(s_pe.h, pw2)
                    s_ve.inc(v.tensor_copy(
                        swrep[:, which * 2 + c2:which * 2 + c2 + 1], ps_c[:, 1:2]))
                _v(vrep)

        qev_marks = []
        qst_marks = []
        for i, (src, r, cols) in enumerate(wtiles):
            which = 0 if cols == D else 1
            vw = s_ve.n

            def syq(sync, src=src, r=r, cols=cols, i=i, vw=vw):
                sync.wait_ge(s_ve.h, vw)
                s_wld2[i % 2].inc(sync.dma_start(
                    out=wld[i % 2][:, 0:cols // 2],
                    in_=src[r * P:(r + 1) * P, 0:cols // 2]), 16)
            _s(syq)

            def syq2(a, src=src, r=r, cols=cols, i=i, vw=vw):
                a.wait_ge(s_ve.h, vw)
                s_wld2[i % 2].inc(a.dma_start(
                    out=wld[i % 2][:, cols // 2:cols],
                    in_=src[r * P:(r + 1) * P, cols // 2:cols]), 16)
            _a(syq2)
            ww = s_wld2[i % 2].n

            pw0 = s_pe.n

            def vq(v, i=i, cols=cols, which=which, ww=ww, pw0=pw0):
                v.wait_ge(s_wld2[i % 2].h, ww)
                v.wait_ge(s_pe.h, pw0)
                c = slice(0, cols)
                v.tensor_scalar(wtmp2[:, c], wld[i % 2][:, c],
                                swrep[:, which * 2 + 1:which * 2 + 2], 0.5,
                                ALU.mult, ALU.add)
                v.drain()
                v.tensor_scalar(qint[:, c], wtmp2[:, c], 1.0, None, ALU.mult)
                v.drain()
                v.tensor_scalar(wsgn[:, c], qint[:, c], 1, None, ALU.mult)
                v.drain()
                v.tensor_tensor(wtmp2[:, c], wtmp2[:, c], wsgn[:, c],
                                ALU.subtract)
                v.drain()
                v.tensor_scalar(wtmp2[:, c], wtmp2[:, c], 0.0, 1.0,
                                ALU.is_ge, ALU.subtract)
                v.drain()
                v.tensor_tensor(wsgn[:, c], wsgn[:, c], wtmp2[:, c], ALU.add)
                v.drain()
                v.tensor_scalar(qint[:, c], wsgn[:, c], 1.0, None, ALU.mult)
                v.drain()
                v.tensor_scalar(qint[:, c], qint[:, c], -1, 1,
                                ALU.max, ALU.min)
                v.drain()
                s_ve.inc(v.tensor_scalar(qtile[:, c], qint[:, c], 1, None,
                                         ALU.mult))
            _v(vq)
            qdone = s_ve.n

            nk = cols // P
            for k in range(nk):
                pst = [ps_tb, ps_tb2][k % 2]
                slot = len(qev_marks) % 2
                bank_free = qev_marks[-2] if len(qev_marks) >= 2 else 0
                stg_free = s_sg2[slot].n

                def peq(t, pst=pst, k=k, qdone=qdone, bank_free=bank_free):
                    t.wait_ge(s_ve.h, max(qdone, bank_free))
                    s_pe.inc(t.transpose(pst[:], qtile[:, k * P:(k + 1) * P],
                                         idb[:]))
                _p(peq)
                pw = s_pe.n

                def veq(v, pst=pst, k=k, pw=pw, slot=slot, stg_free=stg_free):
                    v.wait_ge(s_pe.h, pw)
                    v.wait_ge(s_sg2[slot].h, stg_free)
                    s_ve.inc(v.tensor_copy(stg[slot][:], pst[:]))
                _v(veq)
                ew = s_ve.n
                qev_marks.append(ew)

                def syst(sync, k=k, r=r, cols=cols, ew=ew, slot=slot):
                    sync.wait_ge(s_ve.h, ew)
                    if cols == D:
                        dst = w1qt[:, k * FF + r * P: k * FF + (r + 1) * P]
                    else:
                        dst = w2qt[:, k * D + r * P: k * D + (r + 1) * P]
                    s_sg2[slot].inc(sync.dma_start(out=dst, in_=stg[slot][:]), 16)
                _s(syst)

        stw = s_st.n
        sg0w, sg1w = s_sg2[0].n, s_sg2[1].n

        def vbarA(v, stw=stw, sg0w=sg0w, sg1w=sg1w):
            v.wait_ge(s_st.h, stw)
            v.wait_ge(s_sg2[0].h, sg0w)
            v.wait_ge(s_sg2[1].h, sg1w)
            s_bar.inc(v.engine_nop())
        _v(vbarA)
    barA = s_bar.n
    actx.close()

    bctx = contextlib.ExitStack()
    bsb = lambda n_, s, d: bctx.enter_context(nc.sbuf_tensor(n_, s, d))
    xld = [bsb(f"xld{i}", [P, D], F32) for i in range(2)]
    vdump = bsb("vdump", [P, D], F32)
    xnb = bsb("xnb", [P, D], BF16)
    xnT = bsb("xnT", [P, DK * SUP], BF16)
    stat = bsb("stat", [P, 16], F32)
    nsc = bsb("nsc", [P, 4], F32)
    nsc2 = bsb("nsc2", [P, 4], F32)
    asup = [bsb(f"asup{i}", [P, SUP * DK], F32) for i in range(2)]
    zsup = [bsb(f"zsup{i}", [P, SUP * DK], F32) for i in range(2)]
    csup = bsb("csup", [P, SUP * DK], F32)
    q2sup = bsb("q2sup", [P, 2 * SUP * DK], F32)
    psup = bsb("psup", [P, SUP * DK], F32)
    rscr = bsb("rscr", [P, SUP], F32)
    zpad = bsb("zpad", [P, 2 * DK], F32)
    ozsup = bsb("ozsup", [P, SUP * DK], F32)
    b16s = bsb("b16s", [P, 16], F32)

    with nc.Block() as blk:
        act_defer = []

        def _s(f):
            blk.sync(f)

        def _v(f):
            blk.vector(f)

        def _a(f):
            blk.scalar(f)

        def _p(f):
            blk.tensor(f)

        def vz(v):
            v.wait_ge(s_bar.h, barA)
            v.memset(zpad[:], 0.0)
            v.memset(q2sup[:], 0.0)
            s_ve.inc(v.engine_nop())
        _v(vz)

        def stream_build(v, sp, halo):
            n = SUP * DK
            v.tensor_tensor(psup[:, 0:n - DK], asup[sp % 2][:, DK:n],
                            zsup[sp % 2][:, 0:n - DK], ALU.mult)
            v.tensor_tensor(psup[:, n - DK:n], halo,
                            zsup[sp % 2][:, n - DK:n], ALU.mult)
            v.tensor_scalar(ozsup[:], zsup[sp % 2][:], -1.0, 1.0,
                            ALU.mult, ALU.add)
            v.drain()
            q2a = bass.AP(q2sup, 1, [[2 * n, P], [2, n - DK]])
            q2b = bass.AP(q2sup, 1 + 2 * (n - DK), [[2 * n, P], [2, DK]])
            v.tensor_tensor(q2a, asup[sp % 2][:, DK:n],
                            psup[:, 0:n - DK], ALU.subtract)
            v.tensor_tensor(q2b, halo,
                            psup[:, n - DK:n], ALU.subtract)
            s_ve.inc(v.engine_nop())

        xnb_marks = {}
        for s in range(NSUP):
            for half in range(2):
                tau = 2 * s + half
                vw = s_ve.n

                def syx(sync, tau=tau, vw=vw):
                    sync.wait_ge(s_bar.h, barA)
                    if tau >= 2:
                        sync.wait_ge(s_ve.h, vw)
                    s_xl2[tau % 2].inc(sync.dma_start(
                        out=xld[tau % 2][:, 0:D // 2],
                        in_=dx[tau * P:(tau + 1) * P, 0:D // 2]), 16)
                _s(syx)

                def syxg(gp, tau=tau, vw=vw):
                    gp.wait_ge(s_bar.h, barA)
                    if tau >= 2:
                        gp.wait_ge(s_ve.h, vw)
                    s_xl2[tau % 2].inc(gp.dma_start(
                        out=xld[tau % 2][:, D // 2:D],
                        in_=dx[tau * P:(tau + 1) * P, D // 2:D]), 16)
                blk.gpsimd(syxg)
                xw = s_xl2[tau % 2].n

                vw_prev = s_ve.n
                s_ae.reserve()

                def act_sq(a, tau=tau, xw=xw, vw_prev=vw_prev):
                    a.wait_ge(s_xl2[tau % 2].h, xw)
                    a.wait_ge(s_ve.h, vw_prev)
                    s_ae.incr(a.activation(vdump[:], xld[tau % 2][:], AF.Square,
                                           accum_out=stat[:, 1:2]))
                act_defer.append(act_sq)
                aw = s_ae.n

                pw0 = xnb_marks.get(tau - 1, 0)

                def vln_a(v, tau=tau, xw=xw):
                    v.wait_ge(s_xl2[tau % 2].h, xw)
                    s_ve.inc(v.tensor_reduce(stat[:, 0:1], xld[tau % 2][:],
                                             AX, ALU.add))
                _v(vln_a)
                vred = s_ve.n

                def p_bln(gp, aw=aw, vred=vred):
                    gp.wait_ge(s_ae.h, aw)
                    gp.wait_ge(s_ve.h, vred)
                    gp.tensor_scalar(stat[:, 2:3], stat[:, 0:1], 1.0 / D, None,
                                     ALU.mult)
                    gp.tensor_scalar(stat[:, 3:4], stat[:, 1:2], 1.0 / D, None,
                                     ALU.mult)
                    gp.tensor_tensor(nsc[:, 0:1], stat[:, 2:3], stat[:, 2:3],
                                     ALU.mult)
                    gp.tensor_tensor(stat[:, 3:4], stat[:, 3:4], nsc[:, 0:1],
                                     ALU.subtract)
                    s_pq.inc(gp.tensor_scalar(stat[:, 3:4], stat[:, 3:4], EPS,
                                              None, ALU.add))
                blk.gpsimd(p_bln)
                bpq = s_pq.n

                def v_bsd(v, bpq=bpq):
                    v.wait_ge(s_pq.h, bpq)
                    rsqrt_seed(v, stat[:, 4:5], stat[:, 3:4], b16s)
                    s_ve.inc(v.engine_nop())
                _v(v_bsd)
                bsd = s_ve.n

                def p_blnb(gp, bsd=bsd):
                    gp.wait_ge(s_ve.h, bsd)
                    rsqrt_newton(gp, stat[:, 4:5], stat[:, 3:4], nsc[:, 1:2])
                    gp.tensor_tensor(nsc2[:, 0:1], stat[:, 2:3], stat[:, 4:5],
                                     ALU.mult)
                    s_pq.inc(gp.tensor_scalar(nsc2[:, 0:1], nsc2[:, 0:1], -1.0,
                                              None, ALU.mult))
                blk.gpsimd(p_blnb)
                bpq2 = s_pq.n

                def vln(v, tau=tau, bpq2=bpq2, pw0=pw0):
                    v.wait_ge(s_pq.h, bpq2)
                    v.wait_ge(s_pe.h, pw0)
                    v.tensor_scalar(xnb[:], xld[tau % 2][:], stat[:, 4:5],
                                    nsc2[:, 0:1], ALU.mult, ALU.add)
                    s_ve.inc(v.engine_nop())
                _v(vln)
                lnw = s_ve.n

                xev_marks = []
                for k in range(DK):
                    pst = [ps_tb, ps_tb2][k % 2]
                    bank_free = xev_marks[k - 2] if k >= 2 else None

                    def pex(t, pst=pst, k=k, bank_free=bank_free, lnw=lnw):
                        t.wait_ge(s_ve.h, lnw)
                        if bank_free is not None:
                            t.wait_ge(s_ae.h, bank_free)
                        s_pe.inc(t.transpose(pst[:], xnb[:, k * P:(k + 1) * P],
                                             idb[:]))
                    _p(pex)
                    pw = s_pe.n
                    xnb_marks[tau] = s_pe.n
                    s_ae.reserve()

                    def aex(a, pst=pst, k=k, half=half, pw=pw):
                        a.wait_ge(s_pe.h, pw)
                        col = k * SUP + half * P
                        s_ae.incr(a.copy(xnT[:, col:col + P], pst[:]))
                    act_defer.append(aex)
                    xev_marks.append(s_ae.n)

            xnT_ready = s_ae.n
            evac_marks = {}

            for m in range(24):
                pss = [ps_a, ps_b, ps_c, ps_d][m % 4]
                bank_free = evac_marks.get(m - 4)

                def pem(t, m=m, pss=pss, xr=xnT_ready, bank_free=bank_free):
                    t.wait_ge(s_ae.h, xr)
                    if bank_free is not None:
                        t.wait_ge(bank_free[0].h, bank_free[1])
                    ins = None
                    for k in range(DK):
                        col = (k * 24 + m) * P
                        ins = t.matmul(pss[:, 0:SUP], wT[:, col:col + P],
                                       xnT[:, k * SUP:(k + 1) * SUP],
                                       start=(k == 0), stop=(k == DK - 1))
                    s_pe.inc(ins)
                _p(pem)
                pw = s_pe.n
                j = m % 8

                if m < 8:
                    def vevc(v, m=m, j=j, pss=pss, pw=pw):
                        v.wait_ge(s_pe.h, pw)
                        dst = bass.AP(csup, j, [[SUP * DK, P], [DK, SUP]])
                        s_ve.inc(v.tensor_scalar(dst, pss[:, 0:SUP],
                                                 bprime[:, m:m + 1], None,
                                                 ALU.add))
                    _v(vevc)
                    evac_marks[m] = (s_ve, s_ve.n)
                elif m < 16:
                    vbw = s_ve.n
                    s_ae.reserve()

                    def aevz(a, m=m, j=j, s=s, pss=pss, pw=pw, vbw=vbw):
                        a.wait_ge(s_pe.h, pw)
                        a.wait_ge(s_ve.h, vbw)
                        dst = bass.AP(zsup[s % 2], j, [[SUP * DK, P], [DK, SUP]])
                        s_ae.incr(a.activation(dst, pss[:, 0:SUP], AF.Sigmoid,
                                               bias=bprime[:, m:m + 1]))
                    act_defer.append(aevz)
                    evac_marks[m] = (s_ae, s_ae.n)
                else:
                    vaw = s_ve.n
                    s_ae.reserve()

                    def aevr(a, m=m, pss=pss, pw=pw, vaw=vaw):
                        a.wait_ge(s_pe.h, pw)
                        a.wait_ge(s_ve.h, vaw)
                        s_ae.incr(a.activation(rscr[:], pss[:, 0:SUP], AF.Sigmoid,
                                               bias=bprime[:, m:m + 1]))
                    act_defer.append(aevr)
                    evac_marks[m] = (s_ae, s_ae.n)
                    aw2 = s_ae.n

                    def veva(v, m=m, j=j, s=s, aw2=aw2):
                        v.wait_ge(s_ae.h, aw2)
                        dst = bass.AP(asup[s % 2], j, [[SUP * DK, P], [DK, SUP]])
                        s_ve.inc(v.tensor_scalar(dst, rscr[:],
                                                 utile[:, j:j + 1], None,
                                                 ALU.mult))
                    _v(veva)

            vws, aws = s_ve.n, s_ae.n

            def syzc(sync, s=s, vws=vws, aws=aws):
                sync.wait_ge(s_ve.h, vws)
                sync.wait_ge(s_ae.h, aws)
                c0 = s * SUP * DK
                s_st.inc(sync.dma_start(out=sZ[:, c0:c0 + SUP * DK],
                                        in_=zsup[s % 2][:]), 16)
                s_st.inc(sync.dma_start(out=sC[:, c0:c0 + SUP * DK],
                                        in_=csup[:]), 16)
            _s(syzc)
            zcw = s_st.n

            def vcw(v, zcw=zcw):
                v.wait_ge(s_st.h, zcw)
                s_ve.inc(v.engine_nop())
            _v(vcw)

            if s >= 1:
                stm = s_st.n

                def vbuild(v, s=s, stm=stm):
                    v.wait_ge(s_st.h, stm)
                    stream_build(v, s - 1, asup[s % 2][:, 0:DK])
                _v(vbuild)
                bw = s_ve.n

                def syqp(sync, s=s, bw=bw):
                    sync.wait_ge(s_ve.h, bw)
                    c0 = (s - 1) * SUP * DK
                    s_st.inc(sync.dma_start(
                        out=sQ2[:, 2 * c0:2 * c0 + 2 * SUP * DK],
                        in_=q2sup[:]), 16)
                    s_st.inc(sync.dma_start(out=sP[:, c0:c0 + SUP * DK],
                                            in_=psup[:]), 16)
                    s_st.inc(sync.dma_start(out=sOZ[:, c0:c0 + SUP * DK],
                                            in_=ozsup[:]), 16)
                _s(syqp)

        stm = s_st.n

        def vbuildl(v, stm=stm):
            v.wait_ge(s_st.h, stm)
            stream_build(v, NSUP - 1, zpad[:, 0:DK])
        _v(vbuildl)
        bw = s_ve.n

        def syl(sync, bw=bw):
            sync.wait_ge(s_ve.h, bw)
            c0 = (NSUP - 1) * SUP * DK
            s_st.inc(sync.dma_start(
                out=sQ2[:, 2 * c0:2 * c0 + 2 * SUP * DK], in_=q2sup[:]), 16)
            s_st.inc(sync.dma_start(out=sP[:, c0:c0 + SUP * DK],
                                    in_=psup[:]), 16)
            s_st.inc(sync.dma_start(out=sOZ[:, c0:c0 + SUP * DK],
                                    in_=ozsup[:]), 16)
            s_st.inc(sync.dma_start(out=sC[:, T * DK:(T + 2) * DK],
                                    in_=zpad[:]), 16)
        _s(syl)
        stw = s_st.n

        def act_flush(a):
            a.preload_act_table(AF.Sigmoid)
            for fn in act_defer:
                fn(a)
        blk.scalar(act_flush)

        def vbarB(v, stw=stw):
            v.wait_ge(s_st.h, stw)
            s_bar.inc(v.engine_nop())
        _v(vbarB)
    barB = s_bar.n
    bctx.close()
    wtctx.close()

    cctx = contextlib.ExitStack()
    csb = lambda n_, s, d: cctx.enter_context(nc.sbuf_tensor(n_, s, d))
    fw1 = csb("fw1", [P, DK * FF], BF16)
    fw2 = csb("fw2", [P, FK * D], BF16)

    chctx = contextlib.ExitStack()
    hsb = lambda n_, s, d: chctx.enter_context(nc.sbuf_tensor(n_, s, d))
    W = CIRC * TC
    W16 = W * 2 * DK
    std0 = hsb("std0", [P, W16], F32)
    stp = hsb("stp", [P, CIRC * TC * DK], F32)
    stz = hsb("stz", [P, CIRC * TC * DK], F32)
    stc = hsb("stc", [P, CIRC * TC * DK], F32)
    stoz = hsb("stoz", [P, CIRC * TC * DK], F32)
    cinit = hsb("cinit", [P, 2 * DK], F32)
    ysc = hsb("ysc", [P, YCIRC * DK], F32)
    hd1 = hsb("hd1", [P, W16], F32)
    garg = hsb("garg", [P, 32], F32)
    zinit = hsb("zinit", [P, 1], F32)
    ohb = hsb("ohb", [P, 2 * DK], F32)
    zsb = hsb("zsb", [P, DK], F32)
    nbb = hsb("nbb", [P, DK], F32)
    ysT = [hsb(f"ysT{i}", [P, D], F32) for i in range(3)]

    with nc.Block() as blk:
        def sy_w(sync):
            sync.wait_ge(s_bar.h, barB)
            for jj in range(4):
                s_misc.inc(sync.dma_start(
                    out=fw1[:, jj * 8192:(jj + 1) * 8192],
                    in_=w1qt[:, jj * 8192:(jj + 1) * 8192]), 16)
                s_misc.inc(sync.dma_start(
                    out=fw2[:, jj * 8192:(jj + 1) * 8192],
                    in_=w2qt[:, jj * 8192:(jj + 1) * 8192]), 16)
            s_misc.inc(sync.dma_start(out=cinit[:], in_=sC[:, 0:2 * DK]), 16)
        blk.sync(sy_w)
        w_loaded = s_misc.n

        def sy_loader(sync):
            for c in range(NCH):
                if c >= CIRC:
                    sync.wait_ge(s_vg.h, (c - 1) * TC)
                    sync.wait_ge(s_vs.h, (c - 1) * TC)
                lo = (c % CIRC) * TC * DK
                c0 = c * TC * DK
                sl = s_ldc2[c % 2]
                sl.inc(sync.dma_start(out=std0[:, 2 * lo:2 * lo + 2 * TC * DK],
                                      in_=sQ2[:, 2 * c0:2 * c0 + 2 * TC * DK]),
                       16)
                sl.inc(sync.dma_start(out=stp[:, lo:lo + TC * DK],
                                      in_=sP[:, c0:c0 + TC * DK]), 16)
                sl.inc(sync.dma_start(out=stoz[:, lo:lo + TC * DK],
                                      in_=sOZ[:, c0:c0 + TC * DK]), 16)
                sl.inc(sync.dma_start(out=stz[:, lo:lo + TC * DK],
                                      in_=sZ[:, c0:c0 + TC * DK]), 16)
                sl.inc(sync.dma_start(out=stc[:, lo:lo + TC * DK],
                                      in_=sC[:, c0 + DK:c0 + (TC + 1) * DK]), 16)
                if c % 4 == 3 and c >= 11:
                    tau = (c - 11) // 4
                    sync.wait_ge(s_yse.h, 8 * (tau + 1))
                    s_yst2[tau % 3].inc(sync.dma_start(
                        out=ytm[tau * P:(tau + 1) * P, :],
                        in_=ysT[tau % 3][:]), 16)
            for tau in range(62, NTILE):
                sync.wait_ge(s_yse.h, 8 * (tau + 1))
                s_yst2[tau % 3].inc(sync.dma_start(
                    out=ytm[tau * P:(tau + 1) * P, :],
                    in_=ysT[tau % 3][:]), 16)
        blk.sync(sy_loader)
        assert sum(s.n for s in s_yst2) == 16 * NTILE

        def v_scan(v):
            v.memset(zinit[:], 0.0)
            v.drain()
            for t in range(T):
                if t % TC == 0:
                    ch = t // TC
                    v.wait_ge(s_ldc2[ch % 2].h, 80 * (ch // 2 + 1))
                v.wait_ge(s_ha.h, t + 1)
                v.wait_ge(s_pm.h, t + 1)
                sl16 = (t % W) * 2 * DK
                out = garg[:, (t % 2) * 16:(t % 2) * 16 + 16]
                s_vg.inc(v.tensor_tensor_scan(
                    out, std0[:, sl16:sl16 + 16], hd1[:, sl16:sl16 + 16],
                    zinit[:], ALU.mult, ALU.add))
                hin = bass.AP(hd1, sl16, [[W16, P], [2, DK]])
                cs = (t % (CIRC * TC)) * DK
                s_oh.inc(v.tensor_tensor(ohb[:, (t % 2) * DK:(t % 2) * DK + DK],
                                         stoz[:, cs:cs + DK], hin, ALU.mult))
        blk.vector(v_scan)
        assert s_vg.n == T
        assert s_oh.n == T

        def act_chain(a):
            a.preload_act_table(AF.Tanh)
            a.wait_ge(s_misc.h, w_loaded)
            for t in range(T):
                sl16 = (t % W) * 2 * DK
                hout = bass.AP(hd1, sl16, [[W16, P], [2, DK]])
                if t == 0:
                    gin = cinit[:, 0:DK]
                else:
                    a.wait_ge(s_vg.h, t)
                    gin = bass.AP(garg, ((t - 1) % 2) * 16 + 1,
                                  [[32, P], [2, DK]])
                s_ha.inc(a.activation(hout, gin, AF.Tanh))
                if t >= TILE and t % 16 == 8:
                    tau = t // TILE - 1
                    k = (t % TILE) // 16
                    a.wait_ge(s_ytr.h, 8 * tau + k + 1)
                    if tau >= 3 and k == 0:
                        a.wait_ge(s_yst2[tau % 3].h, 16 * ((tau - 3) // 3 + 1))
                    pst = [ps_tf, ps_tf2][k % 2]
                    s_yse.inc(a.copy(ysT[tau % 3][:, k * P:(k + 1) * P], pst[:]))
            for tau in (62, 63):
                for k in range(DK):
                    done = 8 * tau + k + 1
                    if tau == 62 and k < 8:
                        if s_yse.n >= done:
                            continue
                    a.wait_ge(s_ytr.h, done)
                    a.wait_ge(s_yst2[tau % 3].h, 16 * ((tau - 3) // 3 + 1))
                    pst = [ps_tf, ps_tf2][k % 2]
                    s_yse.inc(a.copy(ysT[tau % 3][:, k * P:(k + 1) * P], pst[:]))
        blk.scalar(act_chain)
        assert s_yse.n == 8 * NTILE, s_yse.n
        assert s_ha.n == T

        def pool_chain(gp):
            gp.wait_ge(s_misc.h, w_loaded)
            gp.memset(ysc[:, (YCIRC - 1) * DK:YCIRC * DK], 0.0)
            for t in range(T):
                cs = (t % (CIRC * TC)) * DK
                sl16 = (t % W) * 2 * DK
                sprev = ((t - 1) % YCIRC) * DK
                scur = (t % YCIRC) * DK
                if t % TC == 0:
                    ch = t // TC
                    gp.wait_ge(s_ldc2[ch % 2].h, 80 * (ch // 2 + 1))
                if t % TILE == 0 and t >= 2 * TILE:
                    gp.wait_ge(s_ytr.h, 8 * (t // TILE - 1))
                gp.tensor_tensor(nbb[:], stp[:, cs:cs + DK],
                                 ysc[:, sprev:sprev + DK], ALU.mult)
                nout = bass.AP(hd1, sl16 + 1, [[W16, P], [2, DK]])
                s_pm.inc(gp.tensor_tensor(nout, nbb[:], stc[:, cs:cs + DK],
                                          ALU.add))
                gp.tensor_tensor(zsb[:], stz[:, cs:cs + DK],
                                 ysc[:, sprev:sprev + DK], ALU.mult)
                gp.wait_ge(s_oh.h, t + 1)
                s_vs.inc(gp.tensor_tensor(
                    ysc[:, scur:scur + DK], zsb[:],
                    ohb[:, (t % 2) * DK:(t % 2) * DK + DK], ALU.add))
        blk.gpsimd(pool_chain)
        assert s_pm.n == T and s_vs.n == T

        def pe_ys(t):
            for tau in range(NTILE):
                for k in range(DK):
                    pst = [ps_tf, ps_tf2][k % 2]
                    if k == 0:
                        t.wait_ge(s_vs.h, TILE * (tau + 1))
                    cnt = 8 * tau + k
                    if cnt >= 2:
                        t.wait_ge(s_yse.h, cnt - 1)
                    src = bass.AP(ysc, ((tau * TILE) % YCIRC) * DK + k,
                                  [[YCIRC * DK, P], [DK, TILE]])
                    s_ytr.inc(t.transpose(pst[:], src, idf[:]))
        blk.tensor(pe_ys)
        assert s_ytr.n == 8 * NTILE

    chctx.close()

    dctx = contextlib.ExitStack()
    dsb = lambda n_, s, d: dctx.enter_context(nc.sbuf_tensor(n_, s, d))
    fys = dsb("fys", [P, D], F32)
    fxh = dsb("fxh", [P, D], F32)
    fxm = dsb("fxm", [P, D], F32)
    flnb = dsb("flnb", [P, D], F32)
    fst = dsb("fst", [P, 32], F32)
    fns = dsb("fns", [P, 4], F32)
    fns2 = dsb("fns2", [P, 4], F32)
    fi32 = dsb("fi32", [P, D], I32)
    fxqb = dsb("fxqb", [P, D], BF16)
    fxqT = dsb("fxqT", [P, DK * P], BF16)
    fvb = dsb("fvb", [P, 512], F32)
    feb = dsb("feb", [P, 512], F32)
    fyb = dsb("fyb", [P, FF], BF16)
    fhqT = dsb("fhqT", [P, FK * P], BF16)
    fy2 = dsb("fy2", [P, D], F32)
    f16s = dsb("f16s", [P, 16], F32)

    with nc.Block() as blk:
        act_defer = []

        def _v(f):
            blk.vector(f)

        def _a(f):
            act_defer.append(f)

        def _p(f):
            blk.tensor(f)

        def _g(f):
            blk.gpsimd(f)

        for i in range(NTL):
            vprev = s_ve.n

            def g_load(gp, i=i, vprev=vprev):
                if i == 0:
                    for _sl in range(3):
                        gp.wait_ge(s_yst2[_sl].h, 16 * len(
                            [t_ for t_ in range(NTILE) if t_ % 3 == _sl]))
                    gp.wait_ge(s_misc.h, s_misc.n)
                else:
                    gp.wait_ge(s_ve.h, vprev)
                    gp.wait_ge(s_out.h, 16 * i)
                s_po.inc(gp.indirect_dma_start(
                    fys[:], None, ytm[0:T, :],
                    bass.IndirectOffsetOnAxis(yidxt[:, i:i + 1], 0)), 16)
                s_po.inc(gp.dma_start(out=fxh[:],
                                      in_=dxh[i * P:(i + 1) * P, :]), 16)
            _g(g_load)
            gw = s_po.n

            s_ae.reserve()

            def a_st1(a, gw=gw):
                a.wait_ge(s_po.h, gw)
                a.activation(flnb[:], fys[:], AF.Copy, accum_out=fst[:, 0:1])
                a.drain()
                s_ae.incr(a.activation(flnb[:], fys[:], AF.Square,
                                       accum_out=fst[:, 1:2]))
            _a(a_st1)
            aw = s_ae.n

            def p_ln1(gp, aw=aw):
                gp.wait_ge(s_ae.h, aw)
                gp.tensor_scalar(fst[:, 2:3], fst[:, 0:1], 1.0 / D, None,
                                 ALU.mult)
                gp.tensor_scalar(fst[:, 3:4], fst[:, 1:2], 1.0 / D, None,
                                 ALU.mult)
                gp.tensor_tensor(fns[:, 0:1], fst[:, 2:3], fst[:, 2:3],
                                 ALU.mult)
                gp.tensor_tensor(fst[:, 3:4], fst[:, 3:4], fns[:, 0:1],
                                 ALU.subtract)
                s_pq.inc(gp.tensor_scalar(fst[:, 3:4], fst[:, 3:4], EPS, None,
                                          ALU.add))
            _g(p_ln1)
            pqa = s_pq.n

            def v_sd1(v, pqa=pqa):
                v.wait_ge(s_pq.h, pqa)
                rsqrt_seed(v, fst[:, 4:5], fst[:, 3:4], f16s)
                s_ve.inc(v.engine_nop())
            _v(v_sd1)
            vsd = s_ve.n

            def p_ln1b(gp, vsd=vsd):
                gp.wait_ge(s_ve.h, vsd)
                rsqrt_newton(gp, fst[:, 4:5], fst[:, 3:4], fns[:, 2:3])
                gp.tensor_tensor(fns[:, 1:2], fst[:, 2:3], fst[:, 4:5],
                                 ALU.mult)
                s_pq.inc(gp.tensor_scalar(fns[:, 1:2], fns[:, 1:2], -1.0,
                                          None, ALU.mult))
            _g(p_ln1b)
            pqw = s_pq.n

            def v_ln1(v, pqw=pqw):
                v.wait_ge(s_pq.h, pqw)
                v.tensor_scalar(flnb[:], fys[:], fst[:, 4:5], fns[:, 1:2],
                                ALU.mult, ALU.add)
                v.drain()
                s_ve.inc(v.tensor_tensor(fxm[:], flnb[:], fxh[:], ALU.add))
            _v(v_ln1)
            vw = s_ve.n

            s_ae.reserve()

            def a_st2(a, vw=vw):
                a.wait_ge(s_ve.h, vw)
                a.activation(flnb[:], fxm[:], AF.Copy, accum_out=fst[:, 5:6])
                a.drain()
                s_ae.incr(a.activation(flnb[:], fxm[:], AF.Square,
                                       accum_out=fst[:, 6:7]))
            _a(a_st2)
            aw = s_ae.n

            pwq = s_pe.n

            def p_q1(gp, aw=aw):
                gp.wait_ge(s_ae.h, aw)
                gp.tensor_scalar(fst[:, 7:8], fst[:, 5:6], 1.0 / D, None,
                                 ALU.mult)
                gp.tensor_scalar(fst[:, 8:9], fst[:, 6:7], 1.0 / D, None,
                                 ALU.mult)
                gp.tensor_tensor(fns[:, 0:1], fst[:, 7:8], fst[:, 7:8],
                                 ALU.mult)
                gp.tensor_tensor(fst[:, 8:9], fst[:, 8:9], fns[:, 0:1],
                                 ALU.subtract)
                gp.tensor_scalar(fst[:, 9:10], fst[:, 8:9], 1.0 + EPS,
                                 EPS * EPS, ALU.mult, ALU.add)
                s_pq.inc(gp.tensor_scalar(fst[:, 7:8], fst[:, 7:8], -1.0,
                                          None, ALU.mult))
            _g(p_q1)
            pq1a = s_pq.n

            def v_sdq1(v, pq1a=pq1a):
                v.wait_ge(s_pq.h, pq1a)
                rsqrt_seed(v, fst[:, 10:11], fst[:, 9:10], f16s)
                s_ve.inc(v.engine_nop())
            _v(v_sdq1)
            vsdq = s_ve.n

            def p_q1b(gp, vsdq=vsdq):
                gp.wait_ge(s_ve.h, vsdq)
                rsqrt_newton(gp, fst[:, 10:11], fst[:, 9:10], fns[:, 2:3])
                s_pq.inc(gp.engine_nop())
            _g(p_q1b)
            pq1 = s_pq.n

            def v_quant1(v, pq1=pq1, pwq=pwq):
                v.wait_ge(s_pq.h, pq1)
                v.wait_ge(s_pe.h, pwq)
                v.tensor_scalar(fys[:], fxm[:], fst[:, 7:8], None, ALU.add)
                v.drain()
                v.tensor_reduce(fst[:, 12:13], fys[:], AX, ALU.max,
                                apply_absolute_value=True)
                v.drain()
                v.tensor_tensor(fns[:, 1:2], fst[:, 12:13], fst[:, 10:11],
                                ALU.mult)
                v.drain()
                v.tensor_scalar(fns[:, 1:2], fns[:, 1:2], EPS, None, ALU.max)
                rt(v)
                v.drain()
                v.reciprocal(fns[:, 2:3], fns[:, 1:2])
                rt(v)
                v.drain()
                v.tensor_scalar(fns[:, 3:4], fns[:, 2:3], 127.0, None, ALU.mult)
                v.drain()
                v.tensor_tensor(fst[:, 13:14], fst[:, 10:11], fns[:, 3:4],
                                ALU.mult)
                v.drain()
                v.tensor_scalar(fns[:, 1:2], fns[:, 1:2], 1.0 / 127.0, None,
                                ALU.mult)
                v.drain()
                v.tensor_tensor(fst[:, 14:15], fns[:, 1:2], swrep[:, 0:1],
                                ALU.mult)
                v.drain()
                v.tensor_scalar(flnb[:], fys[:], fst[:, 13:14], 0.5,
                                ALU.mult, ALU.add)
                v.drain()
                v.tensor_scalar(fi32[:], flnb[:], 1.0, None, ALU.mult)
                v.drain()
                v.tensor_scalar(fys[:], fi32[:], 1, None, ALU.mult)
                v.drain()
                v.tensor_tensor(flnb[:], flnb[:], fys[:], ALU.subtract)
                v.drain()
                v.tensor_scalar(flnb[:], flnb[:], 0.0, 1.0, ALU.is_ge,
                                ALU.subtract)
                v.drain()
                v.tensor_tensor(fys[:], fys[:], flnb[:], ALU.add)
                v.drain()
                s_ve.inc(v.tensor_copy(fxqb[:], fys[:]))
            _v(v_quant1)
            qw = s_ve.n

            xqev_marks = []
            for k in range(DK):
                pst = [ps_tb, ps_tb2][k % 2]
                bank_free = xqev_marks[k - 2] if k >= 2 else qw

                def pext(t, pst=pst, k=k, bank_free=bank_free):
                    t.wait_ge(s_ve.h, bank_free)
                    s_pe.inc(t.transpose(pst[:], fxqb[:, k * P:(k + 1) * P],
                                         idb[:]))
                _p(pext)
                pw = s_pe.n

                def vext(v, pst=pst, k=k, pw=pw):
                    v.wait_ge(s_pe.h, pw)
                    s_ve.inc(v.tensor_copy(fxqT[:, k * P:(k + 1) * P], pst[:]))
                _v(vext)
                xqev_marks.append(s_ve.n)
            xtw = s_ve.n

            sq_cols = []
            mx_cols = []
            for n in range(8):
                pss = [ps_a, ps_b][n % 2]
                vw2 = s_ve.n

                def pmm1(t, n=n, pss=pss, xtw=xtw, vw2=vw2):
                    t.wait_ge(s_ve.h, vw2 if n >= 2 else xtw)
                    ins = None
                    for k in range(DK):
                        ins = t.matmul(pss[:],
                                       fxqT[:, k * P:(k + 1) * P],
                                       fw1[:, k * FF + n * 512:k * FF + (n + 1) * 512],
                                       start=(k == 0), stop=(k == DK - 1))
                    s_pe.inc(ins)
                _p(pmm1)
                pw = s_pe.n

                def vev1(v, n=n, pss=pss, pw=pw):
                    v.wait_ge(s_pe.h, pw)
                    s_ve.inc(v.scalar_tensor_tensor(
                        fvb[:], pss[:], fst[:, 14:15],
                        b1rep[:, n * 512:(n + 1) * 512], ALU.mult, ALU.add))
                _v(vev1)
                vw3 = s_ve.n

                s_ae.reserve()

                def agelu(a, n=n, vw3=vw3):
                    a.wait_ge(s_ve.h, vw3)
                    s_ae.incr(a.activation(fyb[:, n * 512:(n + 1) * 512], fvb[:],
                                           AF.Gelu))
                _a(agelu)
                aw3 = s_ae.n

                s_ae.reserve()

                def asq(a, n=n, aw3=aw3):
                    a.drain()
                    s_ae.incr(a.activation(feb[:], fyb[:, n * 512:(n + 1) * 512],
                                           AF.Square, accum_out=fst[:, 16 + n:17 + n]))
                _a(asq)
                sq_cols.append(16 + n)

                def vmx(v, n=n, aw3=aw3):
                    v.wait_ge(s_ae.h, aw3)
                    v.tensor_reduce(fst[:, 24 + n:25 + n],
                                    fyb[:, n * 512:(n + 1) * 512], AX, ALU.max,
                                    apply_absolute_value=True)
                    s_ve.inc(v.engine_nop())
                _v(vmx)
                mx_cols.append(24 + n)

            aw4 = s_ae.n

            def v_red8(v, aw4=aw4):
                v.wait_ge(s_ae.h, aw4)
                v.drain()
                v.tensor_reduce(fns[:, 0:1], fst[:, 16:24], AX, ALU.add)
                v.tensor_reduce(fns[:, 1:2], fst[:, 24:32], AX, ALU.max)
                v.drain()
                v.tensor_scalar(fst[:, 9:10], fns[:, 0:1], 1.0 / FF, EPS,
                                ALU.mult, ALU.add)
                v.drain()
                rsqrt_seed(v, fst[:, 15:16], fst[:, 9:10], f16s)
                s_ve.inc(v.engine_nop())
            _v(v_red8)
            vr8 = s_ve.n

            def p_q2(gp, vr8=vr8):
                gp.wait_ge(s_ve.h, vr8)
                rsqrt_newton(gp, fst[:, 15:16], fst[:, 9:10], fns[:, 2:3])
                gp.tensor_scalar(fns2[:, 0:1], fst[:, 15:16], 1.0, None,
                                 ALU.mult)
                s_pq.inc(gp.tensor_tensor(fns2[:, 1:2], fns2[:, 0:1],
                                          fns[:, 1:2], ALU.mult))
            _g(p_q2)
            pq2 = s_pq.n

            def v_quant2(v, pq2=pq2):
                v.wait_ge(s_pq.h, pq2)
                v.tensor_scalar(fns2[:, 1:2], fns2[:, 1:2], EPS, None,
                                ALU.max)
                rt(v)
                v.drain()
                v.reciprocal(fns2[:, 2:3], fns2[:, 1:2])
                rt(v)
                v.drain()
                v.tensor_scalar(fns2[:, 3:4], fns2[:, 2:3], 127.0, None, ALU.mult)
                v.drain()
                v.tensor_tensor(fns[:, 2:3], fns2[:, 0:1], fns2[:, 3:4], ALU.mult)
                v.drain()
                v.tensor_scalar(fns[:, 3:4], fns2[:, 1:2], 1.0 / 127.0, None,
                                ALU.mult)
                v.drain()
                v.tensor_tensor(fns[:, 3:4], fns[:, 3:4], swrep[:, 2:3],
                                ALU.mult)
                v.drain()
                for g in range(4):
                    ys_ = [fys[:, 0:512], fys[:, 512:1024]]
                    fs_ = [feb[:], fvb[:]]
                    is_ = [fi32[:, j * 512:(j + 1) * 512] for j in range(2)]
                    segs = [slice((2 * g + j) * 512, (2 * g + j + 1) * 512)
                            for j in range(2)]
                    for j in range(2):
                        v.tensor_scalar(ys_[j], fyb[:, segs[j]], fns[:, 2:3],
                                        0.5, ALU.mult, ALU.add)
                    v.drain()
                    for j in range(2):
                        v.tensor_scalar(is_[j], ys_[j], 1.0, None, ALU.mult)
                    v.drain()
                    for j in range(2):
                        v.tensor_scalar(fs_[j], is_[j], 1, None, ALU.mult)
                    v.drain()
                    for j in range(2):
                        v.tensor_tensor(ys_[j], ys_[j], fs_[j], ALU.subtract)
                    v.drain()
                    for j in range(2):
                        v.tensor_scalar(ys_[j], ys_[j], 0.0, 1.0, ALU.is_ge,
                                        ALU.subtract)
                    v.drain()
                    for j in range(2):
                        v.tensor_tensor(fs_[j], fs_[j], ys_[j], ALU.add)
                    v.drain()
                    for j in range(2):
                        v.tensor_copy(fyb[:, segs[j]], fs_[j])
                    v.drain()
                s_ve.inc(v.engine_nop())
            _v(v_quant2)
            q2w = s_ve.n

            hqev_marks = []
            for k in range(FK):
                pst = [ps_tb, ps_tb2][k % 2]
                bank_free = hqev_marks[k - 2] if k >= 2 else q2w

                def peht(t, pst=pst, k=k, bank_free=bank_free):
                    t.wait_ge(s_ve.h, bank_free)
                    s_pe.inc(t.transpose(pst[:], fyb[:, k * P:(k + 1) * P], idb[:]))
                _p(peht)
                pw = s_pe.n

                def veht(v, pst=pst, k=k, pw=pw):
                    v.wait_ge(s_pe.h, pw)
                    s_ve.inc(v.tensor_copy(fhqT[:, k * P:(k + 1) * P], pst[:]))
                _v(veht)
                hqev_marks.append(s_ve.n)
            htw = s_ve.n

            for n in range(2):
                pss = [ps_c, ps_d][n]

                def pmm2(t, n=n, pss=pss, htw=htw):
                    t.wait_ge(s_ve.h, htw)
                    ins = None
                    for k in range(FK):
                        ins = t.matmul(pss[:],
                                       fhqT[:, k * P:(k + 1) * P],
                                       fw2[:, k * D + n * 512:k * D + (n + 1) * 512],
                                       start=(k == 0), stop=(k == FK - 1))
                    s_pe.inc(ins)
                _p(pmm2)
                pw = s_pe.n

                def vev2(v, n=n, pss=pss, pw=pw):
                    v.wait_ge(s_pe.h, pw)
                    s_ve.inc(v.scalar_tensor_tensor(
                        fy2[:, n * 512:(n + 1) * 512], pss[:], fns[:, 3:4],
                        b2rep[:, n * 512:(n + 1) * 512], ALU.mult, ALU.add))
                _v(vev2)
            y2w = s_ve.n

            s_ae.reserve()

            def a_st3(a, y2w=y2w):
                a.wait_ge(s_ve.h, y2w)
                a.activation(flnb[:], fy2[:], AF.Copy, accum_out=fst[:, 0:1])
                a.drain()
                s_ae.incr(a.activation(flnb[:], fy2[:], AF.Square,
                                       accum_out=fst[:, 1:2]))
            _a(a_st3)
            aw5 = s_ae.n

            def p_fin(gp, aw5=aw5):
                gp.wait_ge(s_ae.h, aw5)
                gp.tensor_scalar(fst[:, 2:3], fst[:, 0:1], 1.0 / D, None,
                                 ALU.mult)
                gp.tensor_scalar(fst[:, 3:4], fst[:, 1:2], 1.0 / D, None,
                                 ALU.mult)
                gp.tensor_tensor(fns[:, 0:1], fst[:, 2:3], fst[:, 2:3],
                                 ALU.mult)
                gp.tensor_tensor(fst[:, 3:4], fst[:, 3:4], fns[:, 0:1],
                                 ALU.subtract)
                s_pq.inc(gp.tensor_scalar(fst[:, 3:4], fst[:, 3:4], EPS, None,
                                          ALU.add))
            _g(p_fin)
            pqfa = s_pq.n

            def v_sdf(v, pqfa=pqfa):
                v.wait_ge(s_pq.h, pqfa)
                rsqrt_seed(v, fst[:, 4:5], fst[:, 3:4], f16s)
                s_ve.inc(v.engine_nop())
            _v(v_sdf)
            vsdf = s_ve.n

            def p_finb(gp, vsdf=vsdf):
                gp.wait_ge(s_ve.h, vsdf)
                rsqrt_newton(gp, fst[:, 4:5], fst[:, 3:4], fns[:, 2:3])
                gp.tensor_tensor(fns[:, 0:1], fst[:, 2:3], fst[:, 4:5],
                                 ALU.mult)
                s_pq.inc(gp.tensor_scalar(fns[:, 0:1], fns[:, 0:1], -1.0,
                                          None, ALU.mult))
            _g(p_finb)
            pqf = s_pq.n

            def v_fin(v, pqf=pqf):
                v.wait_ge(s_pq.h, pqf)
                v.tensor_scalar(flnb[:], fy2[:], fst[:, 4:5], fns[:, 0:1],
                                ALU.mult, ALU.add)
                v.drain()
                s_ve.inc(v.tensor_tensor(fys[:], fxm[:], flnb[:], ALU.add))
            _v(v_fin)
            fw = s_ve.n

            def g_store(gp, i=i, fw=fw):
                gp.wait_ge(s_ve.h, fw)
                s_out.inc(gp.dma_start(out=dyo[i * P:(i + 1) * P, :],
                                       in_=fys[:]), 16)
            _g(g_store)

        def act_flushD(a):
            a.preload_act_table(AF.Gelu)
            for fn in act_defer:
                fn(a)
        blk.scalar(act_flushD)

        def g_final(gp):
            gp.wait_ge(s_out.h, 16 * NTL)
        blk.gpsimd(g_final)

    dctx.close()
    cctx.close()
    ctx.close()
    return nc


_NC_CACHE = None


def kernel(L0_attn_x, L0_attn_u, L0_attn_W, L0_attn_b, L0_attn_gamma,
           L0_attn_beta, L0_ffn_w1, L0_ffn_b1, L0_ffn_w2, L0_ffn_b2):
    global _NC_CACHE
    if _NC_CACHE is None:
        _NC_CACHE = build_nc()
    nc = _NC_CACHE

    x = np.ascontiguousarray(np.asarray(L0_attn_x, dtype=np.float32))
    W = np.ascontiguousarray(np.asarray(L0_attn_W, dtype=np.float32))
    bv = np.ascontiguousarray(np.asarray(L0_attn_b, dtype=np.float32)).reshape(1, F3)
    gam = np.ascontiguousarray(np.asarray(L0_attn_gamma, dtype=np.float32)).reshape(1, D)
    bet = np.ascontiguousarray(np.asarray(L0_attn_beta, dtype=np.float32)).reshape(1, D)
    u = np.ascontiguousarray(np.asarray(L0_attn_u, dtype=np.float32)).reshape(1, D)
    w1 = np.ascontiguousarray(np.asarray(L0_ffn_w1, dtype=np.float32))
    b1 = np.ascontiguousarray(np.asarray(L0_ffn_b1, dtype=np.float32)).reshape(1, FF)
    w2 = np.ascontiguousarray(np.asarray(L0_ffn_w2, dtype=np.float32))
    b2 = np.ascontiguousarray(np.asarray(L0_ffn_b2, dtype=np.float32)).reshape(1, D)
    idf = np.eye(P, dtype=np.float32)
    import ml_dtypes
    idb = np.eye(P, dtype=ml_dtypes.bfloat16)

    in_maps = []
    for c in range(8):
        b = c // 2
        H = c % 2
        xb = np.ascontiguousarray(x[b])
        xh = np.ascontiguousarray(xb[H * (T // 2):(H + 1) * (T // 2)])
        yidx = (H * (T // 2)
                + np.arange(NTL, dtype=np.int32)[None, :] * TILE
                + np.arange(P, dtype=np.int32)[:, None]).astype(np.int32)
        in_maps.append({
            "x": xb, "xh": xh, "W": W, "bvec": bv, "gamma": gam, "beta": bet,
            "u": u, "w1": w1, "b1": b1, "w2": w2, "b2": b2,
            "identf": idf, "identb": idb, "yidx": np.ascontiguousarray(yidx),
        })

    res = run_bass_kernel_spmd(nc, in_maps, core_ids=list(range(8)))
    out = np.empty((B, T, D), dtype=np.float32)
    for c in range(8):
        b = c // 2
        H = c % 2
        out[b, H * (T // 2):(H + 1) * (T // 2), :] = np.asarray(res.results[c]["yo"])
    return out


# version 15 (final)
# speedup vs baseline: 1.1561x; 1.1561x over previous
import contextlib
import os
import numpy as np

import concourse.bass as bass
import concourse.mybir as mybir
from concourse.bass_utils import run_bass_kernel_spmd

F32 = mybir.dt.float32
BF16 = mybir.dt.bfloat16
I32 = mybir.dt.int32
I8 = mybir.dt.int8
AF = mybir.ActivationFunctionType
ALU = mybir.AluOpType
AX = mybir.AxisListType.X

P = 128
B, T, D = 4, 8192, 1024
DK = D // P
F3 = 3 * D
FF = 4096
FK = FF // P
SUP = 256
NSUP = T // SUP
TC = 32
NCH = T // TC
CIRC = 2
YCIRC = 256
TILE = 128
NTILE = T // TILE
NTL = NTILE // 2
EPS = 1e-5
NEWT = 11


class Sem:
    def __init__(self, ctx, nc, name):
        self.h = ctx.enter_context(nc.semaphore(name))
        self.n = 0

    def inc(self, instr, k=1):
        instr.then_inc(self.h, k)
        self.n += k
        return instr

    def reserve(self, k=1):
        self.n += k
        return self.n

    def incr(self, instr, k=1):
        instr.then_inc(self.h, k)
        return instr


def build_nc():
    nc = bass.Bass("TRN2", target_bir_lowering=False)

    dx = nc.dram_tensor("x", [T, D], F32, kind="ExternalInput").ap()
    dxh = nc.dram_tensor("xh", [T // 2, D], F32, kind="ExternalInput").ap()
    dW = nc.dram_tensor("W", [F3, D], F32, kind="ExternalInput").ap()
    dbv = nc.dram_tensor("bvec", [1, F3], F32, kind="ExternalInput").ap()
    dgam = nc.dram_tensor("gamma", [1, D], F32, kind="ExternalInput").ap()
    dbet = nc.dram_tensor("beta", [1, D], F32, kind="ExternalInput").ap()
    du = nc.dram_tensor("u", [1, D], F32, kind="ExternalInput").ap()
    dw1 = nc.dram_tensor("w1", [FF, D], F32, kind="ExternalInput").ap()
    db1 = nc.dram_tensor("b1", [1, FF], F32, kind="ExternalInput").ap()
    dw2 = nc.dram_tensor("w2", [D, FF], F32, kind="ExternalInput").ap()
    db2 = nc.dram_tensor("b2", [1, D], F32, kind="ExternalInput").ap()
    didf = nc.dram_tensor("identf", [P, P], F32, kind="ExternalInput").ap()
    didb = nc.dram_tensor("identb", [P, P], BF16, kind="ExternalInput").ap()
    dyidx = nc.dram_tensor("yidx", [P, NTL], I32, kind="ExternalInput").ap()
    dyo = nc.dram_tensor("yo", [T // 2, D], F32, kind="ExternalOutput").ap()

    SCW = (T + 2) * DK
    _dbg = "Internal" if not os.environ.get("KDBG") else "ExternalOutput"
    sQ2 = nc.dram_tensor("sQ2", [P, T * 2 * DK], F32, kind=_dbg).ap()
    sP = nc.dram_tensor("sP", [P, SCW], F32, kind=_dbg).ap()
    sZ = nc.dram_tensor("sZ", [P, SCW], F32, kind=_dbg).ap()
    sC = nc.dram_tensor("sC", [P, SCW], F32, kind=_dbg).ap()
    sOZ = nc.dram_tensor("sOZ", [P, SCW], F32, kind="Internal").ap()
    ytm = nc.dram_tensor("ytm", [T, D], F32, kind=_dbg).ap()
    w1qt = nc.dram_tensor("w1qt", [P, DK * FF], BF16, kind="Internal").ap()
    w2qt = nc.dram_tensor("w2qt", [P, FK * D], BF16, kind="Internal").ap()

    ctx = contextlib.ExitStack()
    osb = lambda n_, s, d: ctx.enter_context(nc.sbuf_tensor(n_, s, d))
    psT = lambda n_, s, d: ctx.enter_context(nc.psum_tensor(n_, s, d))
    mksem = lambda n_: Sem(ctx, nc, n_)

    idf = osb("idf", [P, P], F32)
    idb = osb("idb", [P, P], BF16)
    ones1 = osb("ones1", [1, P], F32)
    onesc = osb("onesc", [P, 1], F32)
    btile = osb("btile", [P, 24], F32)
    utile = osb("utile", [P, DK], F32)
    bprime = osb("bprime", [P, 24], F32)
    b1rep = osb("b1rep", [P, FF], BF16)
    b2rep = osb("b2rep", [P, D], F32)
    swrep = osb("swrep", [P, 4], F32)
    yidxt = osb("yidxt", [P, NTL], I32)
    rowbuf = osb("rowbuf", [1, FF], F32)

    s_misc = mksem("s_misc")
    s_wld2 = [mksem("s_wld0"), mksem("s_wld1")]
    s_pe = mksem("s_pe")
    s_ve = mksem("s_ve")
    s_ae = mksem("s_ae")
    s_po = mksem("s_po")
    s_st = mksem("s_st")
    s_sg2 = [mksem("s_sg0"), mksem("s_sg1")]
    s_xl2 = [mksem("s_xl0"), mksem("s_xl1")]
    s_ldc2 = [mksem("s_ldc0"), mksem("s_ldc1")]
    s_ha = mksem("s_ha")
    s_vg = mksem("s_vg")
    s_ytr = mksem("s_ytr")
    s_yse = mksem("s_yse")
    s_yst2 = [mksem("s_yst0"), mksem("s_yst1"), mksem("s_yst2")]
    s_out = mksem("s_out")
    s_rt = mksem("s_rt")
    s_vs = mksem("s_vs")
    s_pes = mksem("s_pes")
    s_pz = mksem("s_pz")
    s_pm = mksem("s_pm")
    s_pb = mksem("s_pb")
    s_bar = mksem("s_bar")
    s_pq = mksem("s_pq")

    ps_a = psT("ps_a", [P, 512], F32)
    ps_b = psT("ps_b", [P, 512], F32)
    ps_c = psT("ps_c", [P, 512], F32)
    ps_d = psT("ps_d", [P, 512], F32)
    ps_tb = psT("ps_tb", [P, P], BF16)
    ps_tb2 = psT("ps_tb2", [P, P], BF16)
    ps_tf = psT("ps_tf", [P, P], F32)
    ps_tf2 = psT("ps_tf2", [P, P], F32)

    def rt(v):
        v.drain()

    def rsqrt_seed(v, dst, src, scr16):
        import math
        for k in range(16):
            v0 = 3.0e4 * (0.25 ** k)
            c2 = 1.5 / math.sqrt(v0)
            c1 = -0.5 / v0 ** 1.5
            v.tensor_scalar(scr16[:, k:k + 1], src, c1, c2, ALU.mult, ALU.add)
        v.drain()
        v.tensor_reduce(dst, scr16[:], AX, ALU.max)
        v.drain()

    def rsqrt_newton(gp, dst, src, scr):
        for _ in range(7):
            gp.tensor_tensor(scr, dst, dst, ALU.mult)
            gp.tensor_tensor(scr, src, scr, ALU.mult)
            gp.tensor_scalar(scr, scr, -0.5, 1.5, ALU.mult, ALU.add)
            gp.tensor_tensor(dst, dst, scr, ALU.mult)

    def heron_rsqrt(v, dst, src, scr, scr2):
        v.tensor_scalar(scr, src, 1.0, 0.5, ALU.add, ALU.mult)
        for _ in range(NEWT):
            v.drain()
            v.reciprocal(scr2, scr)
            v.drain()
            v.tensor_tensor(scr2, src, scr2, ALU.mult)
            v.drain()
            v.tensor_tensor(scr, scr, scr2, ALU.add)
            v.drain()
            v.tensor_scalar(scr, scr, 0.5, None, ALU.mult)
        v.drain()
        v.reciprocal(dst, scr)
        v.drain()

    wtctx = contextlib.ExitStack()
    wT = wtctx.enter_context(nc.sbuf_tensor("wT", [P, 24 * DK * P], BF16))
    actx = contextlib.ExitStack()
    asb = lambda n_, s, d: actx.enter_context(nc.sbuf_tensor(n_, s, d))
    wld = [asb(f"wld{i}", [P, FF], F32) for i in range(2)]
    wbuf = asb("wbuf", [P, D], BF16)
    grep = asb("grep", [P, D], F32)
    betrep = asb("betrep", [P, D], F32)
    urep = asb("urep", [P, D], F32)
    brep = asb("brep", [P, F3], F32)
    wtmp = asb("wtmp", [P, D], F32)
    wtmp2 = asb("wtmp2", [P, FF], F32)
    wsgn = asb("wsgn", [P, FF], F32)
    wstat = asb("wstat", [P, 48], F32)
    stg = [asb(f"stg{i}", [P, P], BF16) for i in range(2)]
    qint = asb("qint", [P, FF], I32)
    qtile = asb("qtile", [P, FF], BF16)
    sc1 = asb("sc1", [1, 8], F32)

    with nc.Block() as blk:
        def _s(f):
            blk.sync(f)

        def _v(f):
            blk.vector(f)

        def _a(f):
            blk.scalar(f)

        def _p(f):
            blk.tensor(f)

        def _g(f):
            blk.gpsimd(f)

        def sync0(sync):
            for dst, src in [(idf, didf), (idb, didb), (yidxt, dyidx)]:
                s_misc.inc(sync.dma_start(out=dst[:], in_=src[:]), 16)
            s_misc.inc(sync.dma_start(out=rowbuf[0:1, 0:D], in_=dgam[:]), 16)
        _s(sync0)

        def v0(v):
            v.memset(ones1[:], 1.0)
            v.memset(onesc[:], 1.0)
            s_ve.inc(v.engine_nop())
        _v(v0)

        def brow(dst, N):
            nch = (N + 511) // 512
            for j in range(nch):
                n0, n1 = j * 512, min(N, (j + 1) * 512)
                pss = [ps_a, ps_b][j % 2]
                mw, vw = s_misc.n, s_ve.n

                def pef(t, pss=pss, n0=n0, n1=n1, mw=mw, vw=vw):
                    t.wait_ge(s_misc.h, mw)
                    t.wait_ge(s_ve.h, vw)
                    s_pe.inc(t.matmul(pss[:, 0:n1 - n0], ones1[:],
                                      rowbuf[0:1, n0:n1], start=True, stop=True))
                _p(pef)
                pw = s_pe.n

                def vef(v, pss=pss, n0=n0, n1=n1, pw=pw, dst=dst):
                    v.wait_ge(s_pe.h, pw)
                    s_ve.inc(v.tensor_copy(dst[:, n0:n1], pss[:, 0:n1 - n0]))
                _v(vef)

        def reload_row(src, n):
            vw = s_ve.n

            def syr(sync, src=src, n=n, vw=vw):
                sync.wait_ge(s_ve.h, vw)
                s_misc.inc(sync.dma_start(out=rowbuf[0:1, 0:n], in_=src[:]), 16)
            _s(syr)

        brow(grep, D)
        reload_row(dbet, D)
        brow(betrep, D)
        reload_row(du, D)
        brow(urep, D)
        reload_row(dbv, F3)
        brow(brep, F3)
        reload_row(db1, FF)
        brow(b1rep, FF)
        reload_row(db2, D)
        brow(b2rep, D)

        def mk_tile(dst, rep, nk):
            for k in range(nk):
                pst = [ps_tf, ps_tf2][k % 2]
                vw = s_ve.n

                def pef(t, pst=pst, k=k, rep=rep, vw=vw):
                    t.wait_ge(s_ve.h, vw)
                    s_pe.inc(t.transpose(pst[:], rep[:, k * P:(k + 1) * P], idf[:]))
                _p(pef)
                pw = s_pe.n

                def vef(v, pst=pst, k=k, dst=dst, pw=pw):
                    v.wait_ge(s_pe.h, pw)
                    s_ve.inc(v.tensor_copy(dst[:, k:k + 1], pst[:, 0:1]))
                _v(vef)

        mk_tile(btile, brep, 24)
        mk_tile(utile, urep, DK)

        wev_marks = []
        for r in range(24):
            vw = s_ve.n

            def syw(sync, r=r, vw=vw):
                if r >= 2:
                    sync.wait_ge(s_ve.h, vw)
                s_wld2[r % 2].inc(sync.dma_start(
                    out=wld[r % 2][:, 0:D // 2],
                    in_=dW[r * P:(r + 1) * P, 0:D // 2]), 16)
            _s(syw)

            def syw2(a, r=r, vw=vw):
                if r >= 2:
                    a.wait_ge(s_ve.h, vw)
                s_wld2[r % 2].inc(a.dma_start(
                    out=wld[r % 2][:, D // 2:D],
                    in_=dW[r * P:(r + 1) * P, D // 2:D]), 16)
            _a(syw2)
            ww = s_wld2[r % 2].n

            pw0 = s_pe.n

            def vw_(v, r=r, ww=ww, pw0=pw0):
                v.wait_ge(s_wld2[r % 2].h, ww)
                v.wait_ge(s_pe.h, pw0)
                v.tensor_tensor(wbuf[:], wld[r % 2][:, 0:D], grep[:],
                                ALU.mult)
                v.tensor_tensor(wtmp[:], wld[r % 2][:, 0:D], betrep[:], ALU.mult)
                v.drain()
                v.tensor_reduce(wstat[:, 42:43], wtmp[:], AX, ALU.add)
                v.drain()
                v.tensor_tensor(bprime[:, r:r + 1], wstat[:, 42:43],
                                btile[:, r:r + 1], ALU.add)
                s_ve.inc(v.engine_nop())
            _v(vw_)
            tw = s_ve.n

            evmk = wev_marks
            for k in range(DK):
                pst = [ps_tb, ps_tb2][k % 2]
                bank_free = evmk[-2] if len(evmk) >= 2 else 0

                def pew(t, pst=pst, k=k, tw=tw, bank_free=bank_free):
                    t.wait_ge(s_ve.h, max(tw, bank_free))
                    s_pe.inc(t.transpose(pst[:], wbuf[:, k * P:(k + 1) * P], idb[:]))
                _p(pew)
                pw = s_pe.n

                def vew(v, pst=pst, k=k, r=r, pw=pw):
                    v.wait_ge(s_pe.h, pw)
                    col = (k * 24 + r) * P
                    s_ve.inc(v.tensor_copy(wT[:, col:col + P], pst[:]))
                _v(vew)
                evmk.append(s_ve.n)

        wtiles = [(dw1, r, D) for r in range(FF // P)] + \
                 [(dw2, r, FF) for r in range(D // P)]
        for i, (src, r, cols) in enumerate(wtiles):
            vw = s_ve.n

            def syl(sync, src=src, r=r, cols=cols, i=i, vw=vw):
                sync.wait_ge(s_ve.h, vw)
                s_wld2[i % 2].inc(sync.dma_start(
                    out=wld[i % 2][:, 0:cols // 2],
                    in_=src[r * P:(r + 1) * P, 0:cols // 2]), 16)
            _s(syl)

            def syl2(a, src=src, r=r, cols=cols, i=i, vw=vw):
                a.wait_ge(s_ve.h, vw)
                s_wld2[i % 2].inc(a.dma_start(
                    out=wld[i % 2][:, cols // 2:cols],
                    in_=src[r * P:(r + 1) * P, cols // 2:cols]), 16)
            _a(syl2)
            ww = s_wld2[i % 2].n

            def vst(v, i=i, cols=cols, ww=ww):
                v.wait_ge(s_wld2[i % 2].h, ww)
                v.tensor_reduce(wstat[:, i:i + 1], wld[i % 2][:, 0:cols], AX,
                                ALU.add, apply_absolute_value=True)
                s_ve.inc(v.engine_nop())
            _v(vst)

        def vsum(v):
            rt(v)
            v.tensor_reduce(wstat[:, 40:41], wstat[:, 0:32], AX, ALU.add)
            v.tensor_reduce(wstat[:, 41:42], wstat[:, 32:40], AX, ALU.add)
            rt(v)
            s_ve.inc(v.engine_nop())
        _v(vsum)

        for which in range(2):
            vw = s_ve.n

            def pesum(t, which=which, vw=vw):
                t.wait_ge(s_ve.h, vw)
                s_pe.inc(t.matmul(ps_c[0:1, 0:1], wstat[:, 40 + which:41 + which],
                                  onesc[:], start=True, stop=True))
            _p(pesum)
            pw = s_pe.n

            def vscal(v, which=which, pw=pw):
                v.wait_ge(s_pe.h, pw)
                denom = 1.0 / (FF * D)
                v.tensor_scalar(sc1[0:1, 0:1], ps_c[0:1, 0:1], denom, EPS,
                                ALU.mult, ALU.max)
                rt(v)
                v.reciprocal(sc1[0:1, 1:2], sc1[0:1, 0:1])
                rt(v)
                s_ve.inc(v.engine_nop())
            _v(vscal)

            for c2 in range(2):
                vw2 = s_ve.n

                def perep(t, c2=c2, vw2=vw2):
                    t.wait_ge(s_ve.h, vw2)
                    s_pe.inc(t.matmul(ps_c[:, 1:2], ones1[:],
                                      sc1[0:1, c2:c2 + 1], start=True, stop=True))
                _p(perep)
                pw2 = s_pe.n

                def vrep(v, which=which, c2=c2, pw2=pw2):
                    v.wait_ge(s_pe.h, pw2)
                    s_ve.inc(v.tensor_copy(
                        swrep[:, which * 2 + c2:which * 2 + c2 + 1], ps_c[:, 1:2]))
                _v(vrep)

        qev_marks = []
        qst_marks = []
        for i, (src, r, cols) in enumerate(wtiles):
            which = 0 if cols == D else 1
            vw = s_ve.n

            def syq(sync, src=src, r=r, cols=cols, i=i, vw=vw):
                sync.wait_ge(s_ve.h, vw)
                s_wld2[i % 2].inc(sync.dma_start(
                    out=wld[i % 2][:, 0:cols // 2],
                    in_=src[r * P:(r + 1) * P, 0:cols // 2]), 16)
            _s(syq)

            def syq2(a, src=src, r=r, cols=cols, i=i, vw=vw):
                a.wait_ge(s_ve.h, vw)
                s_wld2[i % 2].inc(a.dma_start(
                    out=wld[i % 2][:, cols // 2:cols],
                    in_=src[r * P:(r + 1) * P, cols // 2:cols]), 16)
            _a(syq2)
            ww = s_wld2[i % 2].n

            pw0 = s_pe.n

            def vq(v, i=i, cols=cols, which=which, ww=ww, pw0=pw0):
                v.wait_ge(s_wld2[i % 2].h, ww)
                v.wait_ge(s_pe.h, pw0)
                c = slice(0, cols)
                v.tensor_scalar(wtmp2[:, c], wld[i % 2][:, c],
                                swrep[:, which * 2 + 1:which * 2 + 2], 0.5,
                                ALU.mult, ALU.add)
                v.drain()
                v.tensor_scalar(qint[:, c], wtmp2[:, c], 1.0, None, ALU.mult)
                v.drain()
                v.tensor_scalar(wsgn[:, c], qint[:, c], 1, None, ALU.mult)
                v.drain()
                v.tensor_tensor(wtmp2[:, c], wtmp2[:, c], wsgn[:, c],
                                ALU.subtract)
                v.drain()
                v.tensor_scalar(wtmp2[:, c], wtmp2[:, c], 0.0, 1.0,
                                ALU.is_ge, ALU.subtract)
                v.drain()
                v.tensor_tensor(wsgn[:, c], wsgn[:, c], wtmp2[:, c], ALU.add)
                v.drain()
                v.tensor_scalar(qint[:, c], wsgn[:, c], 1.0, None, ALU.mult)
                v.drain()
                v.tensor_scalar(qint[:, c], qint[:, c], -1, 1,
                                ALU.max, ALU.min)
                v.drain()
                s_ve.inc(v.tensor_scalar(qtile[:, c], qint[:, c], 1, None,
                                         ALU.mult))
            _v(vq)
            qdone = s_ve.n

            nk = cols // P
            for k in range(nk):
                pst = [ps_tb, ps_tb2][k % 2]
                slot = len(qev_marks) % 2
                bank_free = qev_marks[-2] if len(qev_marks) >= 2 else 0
                stg_free = s_sg2[slot].n

                def peq(t, pst=pst, k=k, qdone=qdone, bank_free=bank_free):
                    t.wait_ge(s_ve.h, max(qdone, bank_free))
                    s_pe.inc(t.transpose(pst[:], qtile[:, k * P:(k + 1) * P],
                                         idb[:]))
                _p(peq)
                pw = s_pe.n

                def veq(v, pst=pst, k=k, pw=pw, slot=slot, stg_free=stg_free):
                    v.wait_ge(s_pe.h, pw)
                    v.wait_ge(s_sg2[slot].h, stg_free)
                    s_ve.inc(v.tensor_copy(stg[slot][:], pst[:]))
                _v(veq)
                ew = s_ve.n
                qev_marks.append(ew)

                def syst(sync, k=k, r=r, cols=cols, ew=ew, slot=slot):
                    sync.wait_ge(s_ve.h, ew)
                    if cols == D:
                        dst = w1qt[:, k * FF + r * P: k * FF + (r + 1) * P]
                    else:
                        dst = w2qt[:, k * D + r * P: k * D + (r + 1) * P]
                    s_sg2[slot].inc(sync.dma_start(out=dst, in_=stg[slot][:]), 16)
                _s(syst)

        stw = s_st.n
        sg0w, sg1w = s_sg2[0].n, s_sg2[1].n

        def vbarA(v, stw=stw, sg0w=sg0w, sg1w=sg1w):
            v.wait_ge(s_st.h, stw)
            v.wait_ge(s_sg2[0].h, sg0w)
            v.wait_ge(s_sg2[1].h, sg1w)
            s_bar.inc(v.engine_nop())
        _v(vbarA)
    barA = s_bar.n
    actx.close()

    bctx = contextlib.ExitStack()
    bsb = lambda n_, s, d: bctx.enter_context(nc.sbuf_tensor(n_, s, d))
    xld = [bsb(f"xld{i}", [P, D], F32) for i in range(2)]
    vdump = bsb("vdump", [P, D], F32)
    xnb = bsb("xnb", [P, D], BF16)
    xnT = bsb("xnT", [P, DK * SUP], BF16)
    stat = bsb("stat", [P, 16], F32)
    nsc = bsb("nsc", [P, 4], F32)
    nsc2 = bsb("nsc2", [P, 4], F32)
    asup = [bsb(f"asup{i}", [P, SUP * DK], F32) for i in range(2)]
    zsup = [bsb(f"zsup{i}", [P, SUP * DK], F32) for i in range(2)]
    csup = bsb("csup", [P, SUP * DK], F32)
    q2sup = bsb("q2sup", [P, 2 * SUP * DK], F32)
    psup = bsb("psup", [P, SUP * DK], F32)
    rscr = bsb("rscr", [P, SUP], F32)
    zpad = bsb("zpad", [P, 2 * DK], F32)
    ozsup = bsb("ozsup", [P, SUP * DK], F32)
    b16s = bsb("b16s", [P, 16], F32)

    with nc.Block() as blk:
        act_defer = []

        def _s(f):
            blk.sync(f)

        def _v(f):
            blk.vector(f)

        def _a(f):
            blk.scalar(f)

        def _p(f):
            blk.tensor(f)

        def vz(v):
            v.wait_ge(s_bar.h, barA)
            v.memset(zpad[:], 0.0)
            v.memset(q2sup[:], 0.0)
            s_ve.inc(v.engine_nop())
        _v(vz)

        def stream_build(v, sp, halo):
            n = SUP * DK
            v.tensor_tensor(psup[:, 0:n - DK], asup[sp % 2][:, DK:n],
                            zsup[sp % 2][:, 0:n - DK], ALU.mult)
            v.tensor_tensor(psup[:, n - DK:n], halo,
                            zsup[sp % 2][:, n - DK:n], ALU.mult)
            v.tensor_scalar(ozsup[:], zsup[sp % 2][:], -1.0, 1.0,
                            ALU.mult, ALU.add)
            v.drain()
            q2a = bass.AP(q2sup, 1, [[2 * n, P], [2, n - DK]])
            q2b = bass.AP(q2sup, 1 + 2 * (n - DK), [[2 * n, P], [2, DK]])
            v.tensor_tensor(q2a, asup[sp % 2][:, DK:n],
                            psup[:, 0:n - DK], ALU.subtract)
            v.tensor_tensor(q2b, halo,
                            psup[:, n - DK:n], ALU.subtract)
            s_ve.inc(v.engine_nop())

        prev_xev = []
        xnb_mark = 0
        for s in range(NSUP):
            vwA = s_ve.n
            aw2, vred2, bpq_, bsd2, bpq2_ = {}, {}, {}, {}, {}
            for half in range(2):
                tau = 2 * s + half
                off = 8 * half

                def syx(sync, tau=tau, s=s, vwA=vwA):
                    sync.wait_ge(s_bar.h, barA)
                    if s >= 1:
                        sync.wait_ge(s_ve.h, vwA)
                    s_xl2[tau % 2].inc(sync.dma_start(
                        out=xld[tau % 2][:, 0:D // 2],
                        in_=dx[tau * P:(tau + 1) * P, 0:D // 2]), 16)
                _s(syx)

                def syxg(gp, tau=tau, s=s, vwA=vwA):
                    gp.wait_ge(s_bar.h, barA)
                    if s >= 1:
                        gp.wait_ge(s_ve.h, vwA)
                    s_xl2[tau % 2].inc(gp.dma_start(
                        out=xld[tau % 2][:, D // 2:D],
                        in_=dx[tau * P:(tau + 1) * P, D // 2:D]), 16)
                blk.gpsimd(syxg)
                xw = s_xl2[tau % 2].n
                s_ae.reserve()

                def act_sq(a, tau=tau, xw=xw, off=off):
                    a.wait_ge(s_xl2[tau % 2].h, xw)
                    s_ae.incr(a.activation(vdump[:], xld[tau % 2][:], AF.Square,
                                           accum_out=stat[:, 1 + off:2 + off]))
                act_defer.append(act_sq)
                aw2[half] = s_ae.n

                def vln_a(v, tau=tau, xw=xw, off=off):
                    v.wait_ge(s_xl2[tau % 2].h, xw)
                    s_ve.inc(v.tensor_reduce(stat[:, off:1 + off],
                                             xld[tau % 2][:], AX, ALU.add))
                _v(vln_a)
                vred2[half] = s_ve.n

            for half in range(2):
                off = 8 * half
                o2 = 2 * half

                def p_bln(gp, aw=aw2[half], vred=vred2[half], off=off, o2=o2):
                    gp.wait_ge(s_ae.h, aw)
                    gp.wait_ge(s_ve.h, vred)
                    gp.tensor_scalar(stat[:, 2 + off:3 + off],
                                     stat[:, off:1 + off], 1.0 / D, None,
                                     ALU.mult)
                    gp.tensor_scalar(stat[:, 3 + off:4 + off],
                                     stat[:, 1 + off:2 + off], 1.0 / D, None,
                                     ALU.mult)
                    gp.tensor_tensor(nsc[:, o2:o2 + 1], stat[:, 2 + off:3 + off],
                                     stat[:, 2 + off:3 + off], ALU.mult)
                    gp.tensor_tensor(stat[:, 3 + off:4 + off],
                                     stat[:, 3 + off:4 + off], nsc[:, o2:o2 + 1],
                                     ALU.subtract)
                    s_pq.inc(gp.tensor_scalar(stat[:, 3 + off:4 + off],
                                              stat[:, 3 + off:4 + off], EPS,
                                              None, ALU.add))
                blk.gpsimd(p_bln)
                bpq_[half] = s_pq.n
            for half in range(2):
                off = 8 * half

                def v_bsd(v, bpq=bpq_[half], off=off):
                    v.wait_ge(s_pq.h, bpq)
                    rsqrt_seed(v, stat[:, 4 + off:5 + off],
                               stat[:, 3 + off:4 + off], b16s)
                    s_ve.inc(v.engine_nop())
                _v(v_bsd)
                bsd2[half] = s_ve.n
            for half in range(2):
                off = 8 * half
                o2 = 2 * half

                def p_blnb(gp, bsd=bsd2[half], off=off, o2=o2, half=half):
                    gp.wait_ge(s_ve.h, bsd)
                    rsqrt_newton(gp, stat[:, 4 + off:5 + off],
                                 stat[:, 3 + off:4 + off], nsc[:, o2 + 1:o2 + 2])
                    gp.tensor_tensor(nsc2[:, half:half + 1],
                                     stat[:, 2 + off:3 + off],
                                     stat[:, 4 + off:5 + off], ALU.mult)
                    s_pq.inc(gp.tensor_scalar(nsc2[:, half:half + 1],
                                              nsc2[:, half:half + 1], -1.0,
                                              None, ALU.mult))
                blk.gpsimd(p_blnb)
                bpq2_[half] = s_pq.n

            for half in range(2):
                tau = 2 * s + half
                off = 8 * half

                def vln(v, tau=tau, bpq2=bpq2_[half], off=off, half=half,
                        xm=xnb_mark):
                    v.wait_ge(s_pq.h, bpq2)
                    v.wait_ge(s_pe.h, xm)
                    v.tensor_scalar(xnb[:], xld[tau % 2][:],
                                    stat[:, 4 + off:5 + off],
                                    nsc2[:, half:half + 1], ALU.mult, ALU.add)
                    s_ve.inc(v.engine_nop())
                _v(vln)
                lnw = s_ve.n

                xev_marks = []
                for k in range(DK):
                    pst = [ps_tb, ps_tb2][k % 2]
                    if k >= 2:
                        bank_free = xev_marks[k - 2]
                    elif len(prev_xev) >= 8:
                        bank_free = prev_xev[k + 6]
                    else:
                        bank_free = None

                    def pex(t, pst=pst, k=k, bank_free=bank_free, lnw=lnw):
                        t.wait_ge(s_ve.h, lnw)
                        if bank_free is not None:
                            t.wait_ge(s_ae.h, bank_free)
                        s_pe.inc(t.transpose(pst[:], xnb[:, k * P:(k + 1) * P],
                                             idb[:]))
                    _p(pex)
                    pw = s_pe.n
                    s_ae.reserve()

                    def aex(a, pst=pst, k=k, half=half, pw=pw):
                        a.wait_ge(s_pe.h, pw)
                        col = k * SUP + half * P
                        s_ae.incr(a.copy(xnT[:, col:col + P], pst[:]))
                    act_defer.append(aex)
                    xev_marks.append(s_ae.n)
                prev_xev = xev_marks
                xnb_mark = s_pe.n

            xnT_ready = s_ae.n
            evac_marks = {}

            for m in range(24):
                pss = [ps_a, ps_b, ps_c, ps_d][m % 4]
                bank_free = evac_marks.get(m - 4)

                def pem(t, m=m, pss=pss, xr=xnT_ready, bank_free=bank_free):
                    t.wait_ge(s_ae.h, xr)
                    if bank_free is not None:
                        t.wait_ge(bank_free[0].h, bank_free[1])
                    ins = None
                    for k in range(DK):
                        col = (k * 24 + m) * P
                        ins = t.matmul(pss[:, 0:SUP], wT[:, col:col + P],
                                       xnT[:, k * SUP:(k + 1) * SUP],
                                       start=(k == 0), stop=(k == DK - 1))
                    s_pe.inc(ins)
                _p(pem)
                pw = s_pe.n
                j = m % 8

                if m < 8:
                    def vevc(v, m=m, j=j, pss=pss, pw=pw):
                        v.wait_ge(s_pe.h, pw)
                        dst = bass.AP(csup, j, [[SUP * DK, P], [DK, SUP]])
                        s_ve.inc(v.tensor_scalar(dst, pss[:, 0:SUP],
                                                 bprime[:, m:m + 1], None,
                                                 ALU.add))
                    _v(vevc)
                    evac_marks[m] = (s_ve, s_ve.n)
                elif m < 16:
                    vbw = s_ve.n
                    s_ae.reserve()

                    def aevz(a, m=m, j=j, s=s, pss=pss, pw=pw, vbw=vbw):
                        a.wait_ge(s_pe.h, pw)
                        a.wait_ge(s_ve.h, vbw)
                        dst = bass.AP(zsup[s % 2], j, [[SUP * DK, P], [DK, SUP]])
                        s_ae.incr(a.activation(dst, pss[:, 0:SUP], AF.Sigmoid,
                                               bias=bprime[:, m:m + 1]))
                    act_defer.append(aevz)
                    evac_marks[m] = (s_ae, s_ae.n)
                else:
                    vaw = s_ve.n
                    s_ae.reserve()

                    def aevr(a, m=m, pss=pss, pw=pw, vaw=vaw):
                        a.wait_ge(s_pe.h, pw)
                        a.wait_ge(s_ve.h, vaw)
                        s_ae.incr(a.activation(rscr[:], pss[:, 0:SUP], AF.Sigmoid,
                                               bias=bprime[:, m:m + 1]))
                    act_defer.append(aevr)
                    evac_marks[m] = (s_ae, s_ae.n)
                    aw2 = s_ae.n

                    def veva(v, m=m, j=j, s=s, aw2=aw2):
                        v.wait_ge(s_ae.h, aw2)
                        dst = bass.AP(asup[s % 2], j, [[SUP * DK, P], [DK, SUP]])
                        s_ve.inc(v.tensor_scalar(dst, rscr[:],
                                                 utile[:, j:j + 1], None,
                                                 ALU.mult))
                    _v(veva)

            vws, aws = s_ve.n, s_ae.n

            def syzc(sync, s=s, vws=vws, aws=aws):
                sync.wait_ge(s_ve.h, vws)
                sync.wait_ge(s_ae.h, aws)
                c0 = s * SUP * DK
                s_st.inc(sync.dma_start(out=sZ[:, c0:c0 + SUP * DK],
                                        in_=zsup[s % 2][:]), 16)
                s_st.inc(sync.dma_start(out=sC[:, c0:c0 + SUP * DK],
                                        in_=csup[:]), 16)
            _s(syzc)
            zcw = s_st.n

            def vcw(v, zcw=zcw):
                v.wait_ge(s_st.h, zcw)
                s_ve.inc(v.engine_nop())
            _v(vcw)

            if s >= 1:
                stm = s_st.n

                def vbuild(v, s=s, stm=stm):
                    v.wait_ge(s_st.h, stm)
                    stream_build(v, s - 1, asup[s % 2][:, 0:DK])
                _v(vbuild)
                bw = s_ve.n

                def syqp(sync, s=s, bw=bw):
                    sync.wait_ge(s_ve.h, bw)
                    c0 = (s - 1) * SUP * DK
                    s_st.inc(sync.dma_start(
                        out=sQ2[:, 2 * c0:2 * c0 + 2 * SUP * DK],
                        in_=q2sup[:]), 16)
                    s_st.inc(sync.dma_start(out=sP[:, c0:c0 + SUP * DK],
                                            in_=psup[:]), 16)
                    s_st.inc(sync.dma_start(out=sOZ[:, c0:c0 + SUP * DK],
                                            in_=ozsup[:]), 16)
                _s(syqp)

        stm = s_st.n

        def vbuildl(v, stm=stm):
            v.wait_ge(s_st.h, stm)
            stream_build(v, NSUP - 1, zpad[:, 0:DK])
        _v(vbuildl)
        bw = s_ve.n

        def syl(sync, bw=bw):
            sync.wait_ge(s_ve.h, bw)
            c0 = (NSUP - 1) * SUP * DK
            s_st.inc(sync.dma_start(
                out=sQ2[:, 2 * c0:2 * c0 + 2 * SUP * DK], in_=q2sup[:]), 16)
            s_st.inc(sync.dma_start(out=sP[:, c0:c0 + SUP * DK],
                                    in_=psup[:]), 16)
            s_st.inc(sync.dma_start(out=sOZ[:, c0:c0 + SUP * DK],
                                    in_=ozsup[:]), 16)
            s_st.inc(sync.dma_start(out=sC[:, T * DK:(T + 2) * DK],
                                    in_=zpad[:]), 16)
        _s(syl)
        stw = s_st.n

        def act_flush(a):
            a.preload_act_table(AF.Sigmoid)
            for fn in act_defer:
                fn(a)
        blk.scalar(act_flush)

        def vbarB(v, stw=stw):
            v.wait_ge(s_st.h, stw)
            s_bar.inc(v.engine_nop())
        _v(vbarB)
    barB = s_bar.n
    bctx.close()
    wtctx.close()

    cctx = contextlib.ExitStack()
    csb = lambda n_, s, d: cctx.enter_context(nc.sbuf_tensor(n_, s, d))
    fw1 = csb("fw1", [P, DK * FF], BF16)
    fw2 = csb("fw2", [P, FK * D], BF16)

    chctx = contextlib.ExitStack()
    hsb = lambda n_, s, d: chctx.enter_context(nc.sbuf_tensor(n_, s, d))
    W = CIRC * TC
    W16 = W * 2 * DK
    std0 = hsb("std0", [P, W16], F32)
    stp = hsb("stp", [P, CIRC * TC * DK], F32)
    stz = hsb("stz", [P, CIRC * TC * DK], F32)
    stc = hsb("stc", [P, CIRC * TC * DK], F32)
    stoz = hsb("stoz", [P, CIRC * TC * DK], F32)
    cinit = hsb("cinit", [P, 2 * DK], F32)
    ysc = hsb("ysc", [P, YCIRC * DK], F32)
    hd1 = hsb("hd1", [P, W16], F32)
    garg = hsb("garg", [P, 32], F32)
    zinit = hsb("zinit", [P, 1], F32)
    ohb = hsb("ohb", [P, DK], F32)
    zsb = hsb("zsb", [P, DK], F32)
    nbb = hsb("nbb", [P, DK], F32)
    ysT = [hsb(f"ysT{i}", [P, D], F32) for i in range(3)]

    with nc.Block() as blk:
        def sy_w(sync):
            sync.wait_ge(s_bar.h, barB)
            for jj in range(4):
                s_misc.inc(sync.dma_start(
                    out=fw1[:, jj * 8192:(jj + 1) * 8192],
                    in_=w1qt[:, jj * 8192:(jj + 1) * 8192]), 16)
                s_misc.inc(sync.dma_start(
                    out=fw2[:, jj * 8192:(jj + 1) * 8192],
                    in_=w2qt[:, jj * 8192:(jj + 1) * 8192]), 16)
            s_misc.inc(sync.dma_start(out=cinit[:], in_=sC[:, 0:2 * DK]), 16)
        blk.sync(sy_w)
        w_loaded = s_misc.n

        def sy_loader(sync):
            for c in range(NCH):
                if c >= CIRC:
                    sync.wait_ge(s_vg.h, (c - 1) * TC)
                    sync.wait_ge(s_vs.h, (c - 1) * TC)
                lo = (c % CIRC) * TC * DK
                c0 = c * TC * DK
                sl = s_ldc2[c % 2]
                sl.inc(sync.dma_start(out=std0[:, 2 * lo:2 * lo + 2 * TC * DK],
                                      in_=sQ2[:, 2 * c0:2 * c0 + 2 * TC * DK]),
                       16)
                sl.inc(sync.dma_start(out=stp[:, lo:lo + TC * DK],
                                      in_=sP[:, c0:c0 + TC * DK]), 16)
                sl.inc(sync.dma_start(out=stoz[:, lo:lo + TC * DK],
                                      in_=sOZ[:, c0:c0 + TC * DK]), 16)
                sl.inc(sync.dma_start(out=stz[:, lo:lo + TC * DK],
                                      in_=sZ[:, c0:c0 + TC * DK]), 16)
                sl.inc(sync.dma_start(out=stc[:, lo:lo + TC * DK],
                                      in_=sC[:, c0 + DK:c0 + (TC + 1) * DK]), 16)
                if c % 4 == 3 and c >= 11:
                    tau = (c - 11) // 4
                    sync.wait_ge(s_yse.h, 8 * (tau + 1))
                    s_yst2[tau % 3].inc(sync.dma_start(
                        out=ytm[tau * P:(tau + 1) * P, :],
                        in_=ysT[tau % 3][:]), 16)
            for tau in range(62, NTILE):
                sync.wait_ge(s_yse.h, 8 * (tau + 1))
                s_yst2[tau % 3].inc(sync.dma_start(
                    out=ytm[tau * P:(tau + 1) * P, :],
                    in_=ysT[tau % 3][:]), 16)
        blk.sync(sy_loader)
        assert sum(s.n for s in s_yst2) == 16 * NTILE

        def v_scan(v):
            v.memset(zinit[:], 0.0)
            v.drain()
            for t in range(T):
                if t % TC == 0:
                    ch = t // TC
                    v.wait_ge(s_ldc2[ch % 2].h, 80 * (ch // 2 + 1))
                v.wait_ge(s_ha.h, t + 1)
                v.wait_ge(s_pm.h, t + 1)
                sl16 = (t % W) * 2 * DK
                out = garg[:, (t % 2) * 16:(t % 2) * 16 + 16]
                s_vg.inc(v.tensor_tensor_scan(
                    out, std0[:, sl16:sl16 + 16], hd1[:, sl16:sl16 + 16],
                    zinit[:], ALU.mult, ALU.add))
        blk.vector(v_scan)
        assert s_vg.n == T

        def act_chain(a):
            a.preload_act_table(AF.Tanh)
            a.wait_ge(s_misc.h, w_loaded)
            for t in range(T):
                sl16 = (t % W) * 2 * DK
                hout = bass.AP(hd1, sl16, [[W16, P], [2, DK]])
                if t == 0:
                    gin = cinit[:, 0:DK]
                else:
                    a.wait_ge(s_vg.h, t)
                    gin = bass.AP(garg, ((t - 1) % 2) * 16 + 1,
                                  [[32, P], [2, DK]])
                s_ha.inc(a.activation(hout, gin, AF.Tanh))
                if t >= TILE and t % 16 == 8:
                    tau = t // TILE - 1
                    k = (t % TILE) // 16
                    a.wait_ge(s_ytr.h, 8 * tau + k + 1)
                    if tau >= 3 and k == 0:
                        a.wait_ge(s_yst2[tau % 3].h, 16 * ((tau - 3) // 3 + 1))
                    pst = [ps_tf, ps_tf2][k % 2]
                    s_yse.inc(a.copy(ysT[tau % 3][:, k * P:(k + 1) * P], pst[:]))
            for tau in (62, 63):
                for k in range(DK):
                    done = 8 * tau + k + 1
                    if tau == 62 and k < 8:
                        if s_yse.n >= done:
                            continue
                    a.wait_ge(s_ytr.h, done)
                    a.wait_ge(s_yst2[tau % 3].h, 16 * ((tau - 3) // 3 + 1))
                    pst = [ps_tf, ps_tf2][k % 2]
                    s_yse.inc(a.copy(ysT[tau % 3][:, k * P:(k + 1) * P], pst[:]))
        blk.scalar(act_chain)
        assert s_yse.n == 8 * NTILE, s_yse.n
        assert s_ha.n == T

        def pool_chain(gp):
            gp.wait_ge(s_misc.h, w_loaded)
            gp.memset(ysc[:, (YCIRC - 1) * DK:YCIRC * DK], 0.0)
            for t in range(T):
                cs = (t % (CIRC * TC)) * DK
                sl16 = (t % W) * 2 * DK
                sprev = ((t - 1) % YCIRC) * DK
                scur = (t % YCIRC) * DK
                if t % TC == 0:
                    ch = t // TC
                    gp.wait_ge(s_ldc2[ch % 2].h, 80 * (ch // 2 + 1))
                if t % TILE == 0 and t >= 2 * TILE:
                    gp.wait_ge(s_ytr.h, 8 * (t // TILE - 1))
                gp.tensor_tensor(nbb[:], stp[:, cs:cs + DK],
                                 ysc[:, sprev:sprev + DK], ALU.mult)
                nout = bass.AP(hd1, sl16 + 1, [[W16, P], [2, DK]])
                s_pm.inc(gp.tensor_tensor(nout, nbb[:], stc[:, cs:cs + DK],
                                          ALU.add))
                gp.tensor_tensor(zsb[:], stz[:, cs:cs + DK],
                                 ysc[:, sprev:sprev + DK], ALU.mult)
                gp.wait_ge(s_ha.h, t + 1)
                hin = bass.AP(hd1, sl16, [[W16, P], [2, DK]])
                gp.tensor_tensor(ohb[:], stoz[:, cs:cs + DK], hin, ALU.mult)
                s_vs.inc(gp.tensor_tensor(ysc[:, scur:scur + DK], zsb[:],
                                          ohb[:], ALU.add))
        blk.gpsimd(pool_chain)
        assert s_pm.n == T and s_vs.n == T

        def pe_ys(t):
            for tau in range(NTILE):
                for k in range(DK):
                    pst = [ps_tf, ps_tf2][k % 2]
                    if k == 0:
                        t.wait_ge(s_vs.h, TILE * (tau + 1))
                    cnt = 8 * tau + k
                    if cnt >= 2:
                        t.wait_ge(s_yse.h, cnt - 1)
                    src = bass.AP(ysc, ((tau * TILE) % YCIRC) * DK + k,
                                  [[YCIRC * DK, P], [DK, TILE]])
                    s_ytr.inc(t.transpose(pst[:], src, idf[:]))
        blk.tensor(pe_ys)
        assert s_ytr.n == 8 * NTILE

    chctx.close()

    dctx = contextlib.ExitStack()
    dsb = lambda n_, s, d: dctx.enter_context(nc.sbuf_tensor(n_, s, d))
    fys = dsb("fys", [P, D], F32)
    fxh = dsb("fxh", [P, D], F32)
    fxm = dsb("fxm", [P, D], F32)
    flnb = dsb("flnb", [P, D], F32)
    fst = dsb("fst", [P, 32], F32)
    fns = dsb("fns", [P, 4], F32)
    fns2 = dsb("fns2", [P, 4], F32)
    fi32 = dsb("fi32", [P, D], I32)
    fxqb = dsb("fxqb", [P, D], BF16)
    fxqT = dsb("fxqT", [P, DK * P], BF16)
    fvb = dsb("fvb", [P, 512], F32)
    feb = dsb("feb", [P, 512], F32)
    fyb = dsb("fyb", [P, FF], BF16)
    fhqT = dsb("fhqT", [P, FK * P], BF16)
    fy2 = dsb("fy2", [P, D], F32)
    f16s = dsb("f16s", [P, 16], F32)

    with nc.Block() as blk:
        act_defer = []

        def _v(f):
            blk.vector(f)

        def _a(f):
            act_defer.append(f)

        def _p(f):
            blk.tensor(f)

        def _g(f):
            blk.gpsimd(f)

        for i in range(NTL):
            vprev = s_ve.n

            def g_load(gp, i=i, vprev=vprev):
                if i == 0:
                    for _sl in range(3):
                        gp.wait_ge(s_yst2[_sl].h, 16 * len(
                            [t_ for t_ in range(NTILE) if t_ % 3 == _sl]))
                    gp.wait_ge(s_misc.h, s_misc.n)
                else:
                    gp.wait_ge(s_ve.h, vprev)
                    gp.wait_ge(s_out.h, 16 * i)
                s_po.inc(gp.indirect_dma_start(
                    fys[:], None, ytm[0:T, :],
                    bass.IndirectOffsetOnAxis(yidxt[:, i:i + 1], 0)), 16)
                s_po.inc(gp.dma_start(out=fxh[:],
                                      in_=dxh[i * P:(i + 1) * P, :]), 16)
            _g(g_load)
            gw = s_po.n

            s_ae.reserve()

            def a_st1(a, gw=gw):
                a.wait_ge(s_po.h, gw)
                a.activation(flnb[:], fys[:], AF.Copy, accum_out=fst[:, 0:1])
                a.drain()
                s_ae.incr(a.activation(flnb[:], fys[:], AF.Square,
                                       accum_out=fst[:, 1:2]))
            _a(a_st1)
            aw = s_ae.n

            def p_ln1(gp, aw=aw):
                gp.wait_ge(s_ae.h, aw)
                gp.tensor_scalar(fst[:, 2:3], fst[:, 0:1], 1.0 / D, None,
                                 ALU.mult)
                gp.tensor_scalar(fst[:, 3:4], fst[:, 1:2], 1.0 / D, None,
                                 ALU.mult)
                gp.tensor_tensor(fns[:, 0:1], fst[:, 2:3], fst[:, 2:3],
                                 ALU.mult)
                gp.tensor_tensor(fst[:, 3:4], fst[:, 3:4], fns[:, 0:1],
                                 ALU.subtract)
                s_pq.inc(gp.tensor_scalar(fst[:, 3:4], fst[:, 3:4], EPS, None,
                                          ALU.add))
            _g(p_ln1)
            pqa = s_pq.n

            def v_sd1(v, pqa=pqa):
                v.wait_ge(s_pq.h, pqa)
                rsqrt_seed(v, fst[:, 4:5], fst[:, 3:4], f16s)
                s_ve.inc(v.engine_nop())
            _v(v_sd1)
            vsd = s_ve.n

            def p_ln1b(gp, vsd=vsd):
                gp.wait_ge(s_ve.h, vsd)
                rsqrt_newton(gp, fst[:, 4:5], fst[:, 3:4], fns[:, 2:3])
                gp.tensor_tensor(fns[:, 1:2], fst[:, 2:3], fst[:, 4:5],
                                 ALU.mult)
                s_pq.inc(gp.tensor_scalar(fns[:, 1:2], fns[:, 1:2], -1.0,
                                          None, ALU.mult))
            _g(p_ln1b)
            pqw = s_pq.n

            def v_ln1(v, pqw=pqw):
                v.wait_ge(s_pq.h, pqw)
                v.tensor_scalar(flnb[:], fys[:], fst[:, 4:5], fns[:, 1:2],
                                ALU.mult, ALU.add)
                v.drain()
                s_ve.inc(v.tensor_tensor(fxm[:], flnb[:], fxh[:], ALU.add))
            _v(v_ln1)
            vw = s_ve.n

            s_ae.reserve()

            def a_st2(a, vw=vw):
                a.wait_ge(s_ve.h, vw)
                a.activation(flnb[:], fxm[:], AF.Copy, accum_out=fst[:, 5:6])
                a.drain()
                s_ae.incr(a.activation(flnb[:], fxm[:], AF.Square,
                                       accum_out=fst[:, 6:7]))
            _a(a_st2)
            aw = s_ae.n

            pwq = s_pe.n

            def p_q1(gp, aw=aw):
                gp.wait_ge(s_ae.h, aw)
                gp.tensor_scalar(fst[:, 7:8], fst[:, 5:6], 1.0 / D, None,
                                 ALU.mult)
                gp.tensor_scalar(fst[:, 8:9], fst[:, 6:7], 1.0 / D, None,
                                 ALU.mult)
                gp.tensor_tensor(fns[:, 0:1], fst[:, 7:8], fst[:, 7:8],
                                 ALU.mult)
                gp.tensor_tensor(fst[:, 8:9], fst[:, 8:9], fns[:, 0:1],
                                 ALU.subtract)
                gp.tensor_scalar(fst[:, 9:10], fst[:, 8:9], 1.0 + EPS,
                                 EPS * EPS, ALU.mult, ALU.add)
                s_pq.inc(gp.tensor_scalar(fst[:, 7:8], fst[:, 7:8], -1.0,
                                          None, ALU.mult))
            _g(p_q1)
            pq1a = s_pq.n

            def v_sdq1(v, pq1a=pq1a):
                v.wait_ge(s_pq.h, pq1a)
                rsqrt_seed(v, fst[:, 10:11], fst[:, 9:10], f16s)
                s_ve.inc(v.engine_nop())
            _v(v_sdq1)
            vsdq = s_ve.n

            def p_q1b(gp, vsdq=vsdq):
                gp.wait_ge(s_ve.h, vsdq)
                rsqrt_newton(gp, fst[:, 10:11], fst[:, 9:10], fns[:, 2:3])
                s_pq.inc(gp.engine_nop())
            _g(p_q1b)
            pq1 = s_pq.n

            def v_quant1(v, pq1=pq1, pwq=pwq):
                v.wait_ge(s_pq.h, pq1)
                v.wait_ge(s_pe.h, pwq)
                v.tensor_scalar(fys[:], fxm[:], fst[:, 7:8], None, ALU.add)
                v.drain()
                v.tensor_reduce(fst[:, 12:13], fys[:], AX, ALU.max,
                                apply_absolute_value=True)
                v.drain()
                v.tensor_tensor(fns[:, 1:2], fst[:, 12:13], fst[:, 10:11],
                                ALU.mult)
                v.drain()
                v.tensor_scalar(fns[:, 1:2], fns[:, 1:2], EPS, None, ALU.max)
                rt(v)
                v.drain()
                v.reciprocal(fns[:, 2:3], fns[:, 1:2])
                rt(v)
                v.drain()
                v.tensor_scalar(fns[:, 3:4], fns[:, 2:3], 127.0, None, ALU.mult)
                v.drain()
                v.tensor_tensor(fst[:, 13:14], fst[:, 10:11], fns[:, 3:4],
                                ALU.mult)
                v.drain()
                v.tensor_scalar(fns[:, 1:2], fns[:, 1:2], 1.0 / 127.0, None,
                                ALU.mult)
                v.drain()
                v.tensor_tensor(fst[:, 14:15], fns[:, 1:2], swrep[:, 0:1],
                                ALU.mult)
                v.drain()
                v.tensor_scalar(flnb[:], fys[:], fst[:, 13:14], 0.5,
                                ALU.mult, ALU.add)
                v.drain()
                v.tensor_scalar(fi32[:], flnb[:], 1.0, None, ALU.mult)
                v.drain()
                v.tensor_scalar(fys[:], fi32[:], 1, None, ALU.mult)
                v.drain()
                v.tensor_tensor(flnb[:], flnb[:], fys[:], ALU.subtract)
                v.drain()
                v.tensor_scalar(flnb[:], flnb[:], 0.0, 1.0, ALU.is_ge,
                                ALU.subtract)
                v.drain()
                v.tensor_tensor(fys[:], fys[:], flnb[:], ALU.add)
                v.drain()
                s_ve.inc(v.tensor_copy(fxqb[:], fys[:]))
            _v(v_quant1)
            qw = s_ve.n

            xqev_marks = []
            for k in range(DK):
                pst = [ps_tb, ps_tb2][k % 2]
                bank_free = xqev_marks[k - 2] if k >= 2 else qw

                def pext(t, pst=pst, k=k, bank_free=bank_free):
                    t.wait_ge(s_ve.h, bank_free)
                    s_pe.inc(t.transpose(pst[:], fxqb[:, k * P:(k + 1) * P],
                                         idb[:]))
                _p(pext)
                pw = s_pe.n

                def vext(v, pst=pst, k=k, pw=pw):
                    v.wait_ge(s_pe.h, pw)
                    s_ve.inc(v.tensor_copy(fxqT[:, k * P:(k + 1) * P], pst[:]))
                _v(vext)
                xqev_marks.append(s_ve.n)
            xtw = s_ve.n

            sq_cols = []
            mx_cols = []
            for n in range(8):
                pss = [ps_a, ps_b][n % 2]
                vw2 = s_ve.n

                def pmm1(t, n=n, pss=pss, xtw=xtw, vw2=vw2):
                    t.wait_ge(s_ve.h, vw2 if n >= 2 else xtw)
                    ins = None
                    for k in range(DK):
                        ins = t.matmul(pss[:],
                                       fxqT[:, k * P:(k + 1) * P],
                                       fw1[:, k * FF + n * 512:k * FF + (n + 1) * 512],
                                       start=(k == 0), stop=(k == DK - 1))
                    s_pe.inc(ins)
                _p(pmm1)
                pw = s_pe.n

                def vev1(v, n=n, pss=pss, pw=pw):
                    v.wait_ge(s_pe.h, pw)
                    s_ve.inc(v.scalar_tensor_tensor(
                        fvb[:], pss[:], fst[:, 14:15],
                        b1rep[:, n * 512:(n + 1) * 512], ALU.mult, ALU.add))
                _v(vev1)
                vw3 = s_ve.n

                s_ae.reserve()

                def agelu(a, n=n, vw3=vw3):
                    a.wait_ge(s_ve.h, vw3)
                    s_ae.incr(a.activation(fyb[:, n * 512:(n + 1) * 512], fvb[:],
                                           AF.Gelu))
                _a(agelu)
                aw3 = s_ae.n

                s_ae.reserve()

                def asq(a, n=n, aw3=aw3):
                    a.drain()
                    s_ae.incr(a.activation(feb[:], fyb[:, n * 512:(n + 1) * 512],
                                           AF.Square, accum_out=fst[:, 16 + n:17 + n]))
                _a(asq)
                sq_cols.append(16 + n)

                def vmx(v, n=n, aw3=aw3):
                    v.wait_ge(s_ae.h, aw3)
                    v.tensor_reduce(fst[:, 24 + n:25 + n],
                                    fyb[:, n * 512:(n + 1) * 512], AX, ALU.max,
                                    apply_absolute_value=True)
                    s_ve.inc(v.engine_nop())
                _v(vmx)
                mx_cols.append(24 + n)

            aw4 = s_ae.n

            def v_red8(v, aw4=aw4):
                v.wait_ge(s_ae.h, aw4)
                v.drain()
                v.tensor_reduce(fns[:, 0:1], fst[:, 16:24], AX, ALU.add)
                v.tensor_reduce(fns[:, 1:2], fst[:, 24:32], AX, ALU.max)
                v.drain()
                v.tensor_scalar(fst[:, 9:10], fns[:, 0:1], 1.0 / FF, EPS,
                                ALU.mult, ALU.add)
                v.drain()
                rsqrt_seed(v, fst[:, 15:16], fst[:, 9:10], f16s)
                s_ve.inc(v.engine_nop())
            _v(v_red8)
            vr8 = s_ve.n

            def p_q2(gp, vr8=vr8):
                gp.wait_ge(s_ve.h, vr8)
                rsqrt_newton(gp, fst[:, 15:16], fst[:, 9:10], fns[:, 2:3])
                gp.tensor_scalar(fns2[:, 0:1], fst[:, 15:16], 1.0, None,
                                 ALU.mult)
                s_pq.inc(gp.tensor_tensor(fns2[:, 1:2], fns2[:, 0:1],
                                          fns[:, 1:2], ALU.mult))
            _g(p_q2)
            pq2 = s_pq.n

            def v_quant2(v, pq2=pq2):
                v.wait_ge(s_pq.h, pq2)
                v.tensor_scalar(fns2[:, 1:2], fns2[:, 1:2], EPS, None,
                                ALU.max)
                rt(v)
                v.drain()
                v.reciprocal(fns2[:, 2:3], fns2[:, 1:2])
                rt(v)
                v.drain()
                v.tensor_scalar(fns2[:, 3:4], fns2[:, 2:3], 127.0, None, ALU.mult)
                v.drain()
                v.tensor_tensor(fns[:, 2:3], fns2[:, 0:1], fns2[:, 3:4], ALU.mult)
                v.drain()
                v.tensor_scalar(fns[:, 3:4], fns2[:, 1:2], 1.0 / 127.0, None,
                                ALU.mult)
                v.drain()
                v.tensor_tensor(fns[:, 3:4], fns[:, 3:4], swrep[:, 2:3],
                                ALU.mult)
                v.drain()
                for g in range(4):
                    ys_ = [fys[:, 0:512], fys[:, 512:1024]]
                    fs_ = [feb[:], fvb[:]]
                    is_ = [fi32[:, j * 512:(j + 1) * 512] for j in range(2)]
                    segs = [slice((2 * g + j) * 512, (2 * g + j + 1) * 512)
                            for j in range(2)]
                    for j in range(2):
                        v.tensor_scalar(ys_[j], fyb[:, segs[j]], fns[:, 2:3],
                                        0.5, ALU.mult, ALU.add)
                    v.drain()
                    for j in range(2):
                        v.tensor_scalar(is_[j], ys_[j], 1.0, None, ALU.mult)
                    v.drain()
                    for j in range(2):
                        v.tensor_scalar(fs_[j], is_[j], 1, None, ALU.mult)
                    v.drain()
                    for j in range(2):
                        v.tensor_tensor(ys_[j], ys_[j], fs_[j], ALU.subtract)
                    v.drain()
                    for j in range(2):
                        v.tensor_scalar(ys_[j], ys_[j], 0.0, 1.0, ALU.is_ge,
                                        ALU.subtract)
                    v.drain()
                    for j in range(2):
                        v.tensor_tensor(fs_[j], fs_[j], ys_[j], ALU.add)
                    v.drain()
                    for j in range(2):
                        v.tensor_copy(fyb[:, segs[j]], fs_[j])
                    v.drain()
                s_ve.inc(v.engine_nop())
            _v(v_quant2)
            q2w = s_ve.n

            hqev_marks = []
            for k in range(FK):
                pst = [ps_tb, ps_tb2][k % 2]
                bank_free = hqev_marks[k - 2] if k >= 2 else q2w

                def peht(t, pst=pst, k=k, bank_free=bank_free):
                    t.wait_ge(s_ve.h, bank_free)
                    s_pe.inc(t.transpose(pst[:], fyb[:, k * P:(k + 1) * P], idb[:]))
                _p(peht)
                pw = s_pe.n

                def veht(v, pst=pst, k=k, pw=pw):
                    v.wait_ge(s_pe.h, pw)
                    s_ve.inc(v.tensor_copy(fhqT[:, k * P:(k + 1) * P], pst[:]))
                _v(veht)
                hqev_marks.append(s_ve.n)
            htw = s_ve.n

            for n in range(2):
                pss = [ps_c, ps_d][n]

                def pmm2(t, n=n, pss=pss, htw=htw):
                    t.wait_ge(s_ve.h, htw)
                    ins = None
                    for k in range(FK):
                        ins = t.matmul(pss[:],
                                       fhqT[:, k * P:(k + 1) * P],
                                       fw2[:, k * D + n * 512:k * D + (n + 1) * 512],
                                       start=(k == 0), stop=(k == FK - 1))
                    s_pe.inc(ins)
                _p(pmm2)
                pw = s_pe.n

                def vev2(v, n=n, pss=pss, pw=pw):
                    v.wait_ge(s_pe.h, pw)
                    s_ve.inc(v.scalar_tensor_tensor(
                        fy2[:, n * 512:(n + 1) * 512], pss[:], fns[:, 3:4],
                        b2rep[:, n * 512:(n + 1) * 512], ALU.mult, ALU.add))
                _v(vev2)
            y2w = s_ve.n

            s_ae.reserve()

            def a_st3(a, y2w=y2w):
                a.wait_ge(s_ve.h, y2w)
                a.activation(flnb[:], fy2[:], AF.Copy, accum_out=fst[:, 0:1])
                a.drain()
                s_ae.incr(a.activation(flnb[:], fy2[:], AF.Square,
                                       accum_out=fst[:, 1:2]))
            _a(a_st3)
            aw5 = s_ae.n

            def p_fin(gp, aw5=aw5):
                gp.wait_ge(s_ae.h, aw5)
                gp.tensor_scalar(fst[:, 2:3], fst[:, 0:1], 1.0 / D, None,
                                 ALU.mult)
                gp.tensor_scalar(fst[:, 3:4], fst[:, 1:2], 1.0 / D, None,
                                 ALU.mult)
                gp.tensor_tensor(fns[:, 0:1], fst[:, 2:3], fst[:, 2:3],
                                 ALU.mult)
                gp.tensor_tensor(fst[:, 3:4], fst[:, 3:4], fns[:, 0:1],
                                 ALU.subtract)
                s_pq.inc(gp.tensor_scalar(fst[:, 3:4], fst[:, 3:4], EPS, None,
                                          ALU.add))
            _g(p_fin)
            pqfa = s_pq.n

            def v_sdf(v, pqfa=pqfa):
                v.wait_ge(s_pq.h, pqfa)
                rsqrt_seed(v, fst[:, 4:5], fst[:, 3:4], f16s)
                s_ve.inc(v.engine_nop())
            _v(v_sdf)
            vsdf = s_ve.n

            def p_finb(gp, vsdf=vsdf):
                gp.wait_ge(s_ve.h, vsdf)
                rsqrt_newton(gp, fst[:, 4:5], fst[:, 3:4], fns[:, 2:3])
                gp.tensor_tensor(fns[:, 0:1], fst[:, 2:3], fst[:, 4:5],
                                 ALU.mult)
                s_pq.inc(gp.tensor_scalar(fns[:, 0:1], fns[:, 0:1], -1.0,
                                          None, ALU.mult))
            _g(p_finb)
            pqf = s_pq.n

            def v_fin(v, pqf=pqf):
                v.wait_ge(s_pq.h, pqf)
                v.tensor_scalar(flnb[:], fy2[:], fst[:, 4:5], fns[:, 0:1],
                                ALU.mult, ALU.add)
                v.drain()
                s_ve.inc(v.tensor_tensor(fys[:], fxm[:], flnb[:], ALU.add))
            _v(v_fin)
            fw = s_ve.n

            def g_store(gp, i=i, fw=fw):
                gp.wait_ge(s_ve.h, fw)
                s_out.inc(gp.dma_start(out=dyo[i * P:(i + 1) * P, :],
                                       in_=fys[:]), 16)
            _g(g_store)

        def act_flushD(a):
            a.preload_act_table(AF.Gelu)
            for fn in act_defer:
                fn(a)
        blk.scalar(act_flushD)

        def g_final(gp):
            gp.wait_ge(s_out.h, 16 * NTL)
        blk.gpsimd(g_final)

    dctx.close()
    cctx.close()
    ctx.close()
    return nc


_NC_CACHE = None


def kernel(L0_attn_x, L0_attn_u, L0_attn_W, L0_attn_b, L0_attn_gamma,
           L0_attn_beta, L0_ffn_w1, L0_ffn_b1, L0_ffn_w2, L0_ffn_b2):
    global _NC_CACHE
    if _NC_CACHE is None:
        _NC_CACHE = build_nc()
    nc = _NC_CACHE

    x = np.ascontiguousarray(np.asarray(L0_attn_x, dtype=np.float32))
    W = np.ascontiguousarray(np.asarray(L0_attn_W, dtype=np.float32))
    bv = np.ascontiguousarray(np.asarray(L0_attn_b, dtype=np.float32)).reshape(1, F3)
    gam = np.ascontiguousarray(np.asarray(L0_attn_gamma, dtype=np.float32)).reshape(1, D)
    bet = np.ascontiguousarray(np.asarray(L0_attn_beta, dtype=np.float32)).reshape(1, D)
    u = np.ascontiguousarray(np.asarray(L0_attn_u, dtype=np.float32)).reshape(1, D)
    w1 = np.ascontiguousarray(np.asarray(L0_ffn_w1, dtype=np.float32))
    b1 = np.ascontiguousarray(np.asarray(L0_ffn_b1, dtype=np.float32)).reshape(1, FF)
    w2 = np.ascontiguousarray(np.asarray(L0_ffn_w2, dtype=np.float32))
    b2 = np.ascontiguousarray(np.asarray(L0_ffn_b2, dtype=np.float32)).reshape(1, D)
    idf = np.eye(P, dtype=np.float32)
    import ml_dtypes
    idb = np.eye(P, dtype=ml_dtypes.bfloat16)

    in_maps = []
    for c in range(8):
        b = c // 2
        H = c % 2
        xb = np.ascontiguousarray(x[b])
        xh = np.ascontiguousarray(xb[H * (T // 2):(H + 1) * (T // 2)])
        yidx = (H * (T // 2)
                + np.arange(NTL, dtype=np.int32)[None, :] * TILE
                + np.arange(P, dtype=np.int32)[:, None]).astype(np.int32)
        in_maps.append({
            "x": xb, "xh": xh, "W": W, "bvec": bv, "gamma": gam, "beta": bet,
            "u": u, "w1": w1, "b1": b1, "w2": w2, "b2": b2,
            "identf": idf, "identb": idb, "yidx": np.ascontiguousarray(yidx),
        })

    res = run_bass_kernel_spmd(nc, in_maps, core_ids=list(range(8)))
    out = np.empty((B, T, D), dtype=np.float32)
    for c in range(8):
        b = c // 2
        H = c % 2
        out[b, H * (T // 2):(H + 1) * (T // 2), :] = np.asarray(res.results[c]["yo"])
    return out


# version 16
# speedup vs baseline: 1.1563x; 1.0002x over previous
import contextlib
import os
import numpy as np

import concourse.bass as bass
import concourse.mybir as mybir
from concourse.bass_utils import run_bass_kernel_spmd

F32 = mybir.dt.float32
BF16 = mybir.dt.bfloat16
I32 = mybir.dt.int32
I8 = mybir.dt.int8
AF = mybir.ActivationFunctionType
ALU = mybir.AluOpType
AX = mybir.AxisListType.X

P = 128
B, T, D = 4, 8192, 1024
DK = D // P
F3 = 3 * D
FF = 4096
FK = FF // P
SUP = 256
NSUP = T // SUP
TC = 32
NCH = T // TC
CIRC = 2
YCIRC = 256
TILE = 128
NTILE = T // TILE
NTL = NTILE // 2
EPS = 1e-5
NEWT = 11


class Sem:
    def __init__(self, ctx, nc, name):
        self.h = ctx.enter_context(nc.semaphore(name))
        self.n = 0

    def inc(self, instr, k=1):
        instr.then_inc(self.h, k)
        self.n += k
        return instr

    def reserve(self, k=1):
        self.n += k
        return self.n

    def incr(self, instr, k=1):
        instr.then_inc(self.h, k)
        return instr


def build_nc():
    nc = bass.Bass("TRN2", target_bir_lowering=False)

    dx = nc.dram_tensor("x", [T, D], F32, kind="ExternalInput").ap()
    dxh = nc.dram_tensor("xh", [T // 2, D], F32, kind="ExternalInput").ap()
    dW = nc.dram_tensor("W", [F3, D], F32, kind="ExternalInput").ap()
    dbv = nc.dram_tensor("bvec", [1, F3], F32, kind="ExternalInput").ap()
    dgam = nc.dram_tensor("gamma", [1, D], F32, kind="ExternalInput").ap()
    dbet = nc.dram_tensor("beta", [1, D], F32, kind="ExternalInput").ap()
    du = nc.dram_tensor("u", [1, D], F32, kind="ExternalInput").ap()
    dw1 = nc.dram_tensor("w1", [FF, D], F32, kind="ExternalInput").ap()
    db1 = nc.dram_tensor("b1", [1, FF], F32, kind="ExternalInput").ap()
    dw2 = nc.dram_tensor("w2", [D, FF], F32, kind="ExternalInput").ap()
    db2 = nc.dram_tensor("b2", [1, D], F32, kind="ExternalInput").ap()
    didf = nc.dram_tensor("identf", [P, P], F32, kind="ExternalInput").ap()
    didb = nc.dram_tensor("identb", [P, P], BF16, kind="ExternalInput").ap()
    dyidx = nc.dram_tensor("yidx", [P, NTL], I32, kind="ExternalInput").ap()
    dyo = nc.dram_tensor("yo", [T // 2, D], F32, kind="ExternalOutput").ap()

    SCW = (T + 2) * DK
    _dbg = "Internal" if not os.environ.get("KDBG") else "ExternalOutput"
    sQ2 = nc.dram_tensor("sQ2", [P, T * 2 * DK], F32, kind=_dbg).ap()
    sP = nc.dram_tensor("sP", [P, SCW], F32, kind=_dbg).ap()
    sZ = nc.dram_tensor("sZ", [P, SCW], F32, kind=_dbg).ap()
    sC = nc.dram_tensor("sC", [P, SCW], F32, kind=_dbg).ap()
    sOZ = nc.dram_tensor("sOZ", [P, SCW], F32, kind="Internal").ap()
    ytm = nc.dram_tensor("ytm", [T, D], F32, kind=_dbg).ap()
    w1qt = nc.dram_tensor("w1qt", [P, DK * FF], BF16, kind="Internal").ap()
    w2qt = nc.dram_tensor("w2qt", [P, FK * D], BF16, kind="Internal").ap()

    ctx = contextlib.ExitStack()
    osb = lambda n_, s, d: ctx.enter_context(nc.sbuf_tensor(n_, s, d))
    psT = lambda n_, s, d: ctx.enter_context(nc.psum_tensor(n_, s, d))
    mksem = lambda n_: Sem(ctx, nc, n_)

    idf = osb("idf", [P, P], F32)
    idb = osb("idb", [P, P], BF16)
    ones1 = osb("ones1", [1, P], F32)
    onesc = osb("onesc", [P, 1], F32)
    btile = osb("btile", [P, 24], F32)
    utile = osb("utile", [P, DK], F32)
    bprime = osb("bprime", [P, 24], F32)
    b1rep = osb("b1rep", [P, FF], BF16)
    b2rep = osb("b2rep", [P, D], F32)
    swrep = osb("swrep", [P, 4], F32)
    yidxt = osb("yidxt", [P, NTL], I32)
    rowbuf = osb("rowbuf", [1, FF], F32)

    s_misc = mksem("s_misc")
    s_wld2 = [mksem("s_wld0"), mksem("s_wld1")]
    s_pe = mksem("s_pe")
    s_ve = mksem("s_ve")
    s_ae = mksem("s_ae")
    s_po = mksem("s_po")
    s_st = mksem("s_st")
    s_sg2 = [mksem("s_sg0"), mksem("s_sg1")]
    s_xl2 = [mksem("s_xl0"), mksem("s_xl1")]
    s_ldc2 = [mksem("s_ldc0"), mksem("s_ldc1")]
    s_ha = mksem("s_ha")
    s_vg = mksem("s_vg")
    s_ytr = mksem("s_ytr")
    s_yse = mksem("s_yse")
    s_yst2 = [mksem("s_yst0"), mksem("s_yst1"), mksem("s_yst2")]
    s_out = mksem("s_out")
    s_rt = mksem("s_rt")
    s_vs = mksem("s_vs")
    s_pes = mksem("s_pes")
    s_pz = mksem("s_pz")
    s_pm = mksem("s_pm")
    s_pb = mksem("s_pb")
    s_bar = mksem("s_bar")
    s_pq = mksem("s_pq")

    ps_a = psT("ps_a", [P, 512], F32)
    ps_b = psT("ps_b", [P, 512], F32)
    ps_c = psT("ps_c", [P, 512], F32)
    ps_d = psT("ps_d", [P, 512], F32)
    ps_tb = psT("ps_tb", [P, P], BF16)
    ps_tb2 = psT("ps_tb2", [P, P], BF16)
    ps_tf = psT("ps_tf", [P, P], F32)
    ps_tf2 = psT("ps_tf2", [P, P], F32)

    def rt(v):
        v.drain()

    def rsqrt_seed(v, dst, src, scr16):
        import math
        for k in range(16):
            v0 = 3.0e4 * (0.25 ** k)
            c2 = 1.5 / math.sqrt(v0)
            c1 = -0.5 / v0 ** 1.5
            v.tensor_scalar(scr16[:, k:k + 1], src, c1, c2, ALU.mult, ALU.add)
        v.drain()
        v.tensor_reduce(dst, scr16[:], AX, ALU.max)
        v.drain()

    def rsqrt_newton(gp, dst, src, scr):
        for _ in range(7):
            gp.tensor_tensor(scr, dst, dst, ALU.mult)
            gp.tensor_tensor(scr, src, scr, ALU.mult)
            gp.tensor_scalar(scr, scr, -0.5, 1.5, ALU.mult, ALU.add)
            gp.tensor_tensor(dst, dst, scr, ALU.mult)

    def heron_rsqrt(v, dst, src, scr, scr2):
        v.tensor_scalar(scr, src, 1.0, 0.5, ALU.add, ALU.mult)
        for _ in range(NEWT):
            v.drain()
            v.reciprocal(scr2, scr)
            v.drain()
            v.tensor_tensor(scr2, src, scr2, ALU.mult)
            v.drain()
            v.tensor_tensor(scr, scr, scr2, ALU.add)
            v.drain()
            v.tensor_scalar(scr, scr, 0.5, None, ALU.mult)
        v.drain()
        v.reciprocal(dst, scr)
        v.drain()

    wtctx = contextlib.ExitStack()
    wT = wtctx.enter_context(nc.sbuf_tensor("wT", [P, 24 * DK * P], BF16))
    actx = contextlib.ExitStack()
    asb = lambda n_, s, d: actx.enter_context(nc.sbuf_tensor(n_, s, d))
    wld = [asb(f"wld{i}", [P, FF], F32) for i in range(2)]
    wbuf = asb("wbuf", [P, D], BF16)
    grep = asb("grep", [P, D], F32)
    betrep = asb("betrep", [P, D], F32)
    urep = asb("urep", [P, D], F32)
    brep = asb("brep", [P, F3], F32)
    wtmp = asb("wtmp", [P, D], F32)
    wtmp2 = asb("wtmp2", [P, FF], F32)
    wsgn = asb("wsgn", [P, FF], F32)
    wstat = asb("wstat", [P, 48], F32)
    stg = [asb(f"stg{i}", [P, P], BF16) for i in range(2)]
    qint = asb("qint", [P, FF], I32)
    qtile = asb("qtile", [P, FF], BF16)
    sc1 = asb("sc1", [1, 8], F32)

    with nc.Block() as blk:
        def _s(f):
            blk.sync(f)

        def _v(f):
            blk.vector(f)

        def _a(f):
            blk.scalar(f)

        def _p(f):
            blk.tensor(f)

        def _g(f):
            blk.gpsimd(f)

        def sync0(sync):
            for dst, src in [(idf, didf), (idb, didb), (yidxt, dyidx)]:
                s_misc.inc(sync.dma_start(out=dst[:], in_=src[:]), 16)
            s_misc.inc(sync.dma_start(out=rowbuf[0:1, 0:D], in_=dgam[:]), 16)
        _s(sync0)

        def v0(v):
            v.memset(ones1[:], 1.0)
            v.memset(onesc[:], 1.0)
            s_ve.inc(v.engine_nop())
        _v(v0)

        def brow(dst, N):
            nch = (N + 511) // 512
            for j in range(nch):
                n0, n1 = j * 512, min(N, (j + 1) * 512)
                pss = [ps_a, ps_b][j % 2]
                mw, vw = s_misc.n, s_ve.n

                def pef(t, pss=pss, n0=n0, n1=n1, mw=mw, vw=vw):
                    t.wait_ge(s_misc.h, mw)
                    t.wait_ge(s_ve.h, vw)
                    s_pe.inc(t.matmul(pss[:, 0:n1 - n0], ones1[:],
                                      rowbuf[0:1, n0:n1], start=True, stop=True))
                _p(pef)
                pw = s_pe.n

                def vef(v, pss=pss, n0=n0, n1=n1, pw=pw, dst=dst):
                    v.wait_ge(s_pe.h, pw)
                    s_ve.inc(v.tensor_copy(dst[:, n0:n1], pss[:, 0:n1 - n0]))
                _v(vef)

        def reload_row(src, n):
            vw = s_ve.n

            def syr(sync, src=src, n=n, vw=vw):
                sync.wait_ge(s_ve.h, vw)
                s_misc.inc(sync.dma_start(out=rowbuf[0:1, 0:n], in_=src[:]), 16)
            _s(syr)

        brow(grep, D)
        reload_row(dbet, D)
        brow(betrep, D)
        reload_row(du, D)
        brow(urep, D)
        reload_row(dbv, F3)
        brow(brep, F3)
        reload_row(db1, FF)
        brow(b1rep, FF)
        reload_row(db2, D)
        brow(b2rep, D)

        def mk_tile(dst, rep, nk):
            for k in range(nk):
                pst = [ps_tf, ps_tf2][k % 2]
                vw = s_ve.n

                def pef(t, pst=pst, k=k, rep=rep, vw=vw):
                    t.wait_ge(s_ve.h, vw)
                    s_pe.inc(t.transpose(pst[:], rep[:, k * P:(k + 1) * P], idf[:]))
                _p(pef)
                pw = s_pe.n

                def vef(v, pst=pst, k=k, dst=dst, pw=pw):
                    v.wait_ge(s_pe.h, pw)
                    s_ve.inc(v.tensor_copy(dst[:, k:k + 1], pst[:, 0:1]))
                _v(vef)

        mk_tile(btile, brep, 24)
        mk_tile(utile, urep, DK)

        wev_marks = []
        for r in range(24):
            vw = s_ve.n

            def syw(sync, r=r, vw=vw):
                if r >= 2:
                    sync.wait_ge(s_ve.h, vw)
                s_wld2[r % 2].inc(sync.dma_start(
                    out=wld[r % 2][:, 0:D // 2],
                    in_=dW[r * P:(r + 1) * P, 0:D // 2]), 16)
            _s(syw)

            def syw2(a, r=r, vw=vw):
                if r >= 2:
                    a.wait_ge(s_ve.h, vw)
                s_wld2[r % 2].inc(a.dma_start(
                    out=wld[r % 2][:, D // 2:D],
                    in_=dW[r * P:(r + 1) * P, D // 2:D]), 16)
            _a(syw2)
            ww = s_wld2[r % 2].n

            pw0 = s_pe.n

            def vw_(v, r=r, ww=ww, pw0=pw0):
                v.wait_ge(s_wld2[r % 2].h, ww)
                v.wait_ge(s_pe.h, pw0)
                v.tensor_tensor(wbuf[:], wld[r % 2][:, 0:D], grep[:],
                                ALU.mult)
                v.tensor_tensor(wtmp[:], wld[r % 2][:, 0:D], betrep[:], ALU.mult)
                v.drain()
                v.tensor_reduce(wstat[:, 42:43], wtmp[:], AX, ALU.add)
                v.drain()
                v.tensor_tensor(bprime[:, r:r + 1], wstat[:, 42:43],
                                btile[:, r:r + 1], ALU.add)
                s_ve.inc(v.engine_nop())
            _v(vw_)
            tw = s_ve.n

            evmk = wev_marks
            for k in range(DK):
                pst = [ps_tb, ps_tb2][k % 2]
                bank_free = evmk[-2] if len(evmk) >= 2 else 0

                def pew(t, pst=pst, k=k, tw=tw, bank_free=bank_free):
                    t.wait_ge(s_ve.h, max(tw, bank_free))
                    s_pe.inc(t.transpose(pst[:], wbuf[:, k * P:(k + 1) * P], idb[:]))
                _p(pew)
                pw = s_pe.n

                def vew(v, pst=pst, k=k, r=r, pw=pw):
                    v.wait_ge(s_pe.h, pw)
                    col = (k * 24 + r) * P
                    s_ve.inc(v.tensor_copy(wT[:, col:col + P], pst[:]))
                _v(vew)
                evmk.append(s_ve.n)

        wtiles = [(dw1, r, D) for r in range(FF // P)] + \
                 [(dw2, r, FF) for r in range(D // P)]
        for i, (src, r, cols) in enumerate(wtiles):
            vw = s_ve.n

            def syl(sync, src=src, r=r, cols=cols, i=i, vw=vw):
                sync.wait_ge(s_ve.h, vw)
                s_wld2[i % 2].inc(sync.dma_start(
                    out=wld[i % 2][:, 0:cols // 2],
                    in_=src[r * P:(r + 1) * P, 0:cols // 2]), 16)
            _s(syl)

            def syl2(a, src=src, r=r, cols=cols, i=i, vw=vw):
                a.wait_ge(s_ve.h, vw)
                s_wld2[i % 2].inc(a.dma_start(
                    out=wld[i % 2][:, cols // 2:cols],
                    in_=src[r * P:(r + 1) * P, cols // 2:cols]), 16)
            _a(syl2)
            ww = s_wld2[i % 2].n

            def vst(v, i=i, cols=cols, ww=ww):
                v.wait_ge(s_wld2[i % 2].h, ww)
                v.tensor_reduce(wstat[:, i:i + 1], wld[i % 2][:, 0:cols], AX,
                                ALU.add, apply_absolute_value=True)
                s_ve.inc(v.engine_nop())
            _v(vst)

        def vsum(v):
            rt(v)
            v.tensor_reduce(wstat[:, 40:41], wstat[:, 0:32], AX, ALU.add)
            v.tensor_reduce(wstat[:, 41:42], wstat[:, 32:40], AX, ALU.add)
            rt(v)
            s_ve.inc(v.engine_nop())
        _v(vsum)

        for which in range(2):
            vw = s_ve.n

            def pesum(t, which=which, vw=vw):
                t.wait_ge(s_ve.h, vw)
                s_pe.inc(t.matmul(ps_c[0:1, 0:1], wstat[:, 40 + which:41 + which],
                                  onesc[:], start=True, stop=True))
            _p(pesum)
            pw = s_pe.n

            def vscal(v, which=which, pw=pw):
                v.wait_ge(s_pe.h, pw)
                denom = 1.0 / (FF * D)
                v.tensor_scalar(sc1[0:1, 0:1], ps_c[0:1, 0:1], denom, EPS,
                                ALU.mult, ALU.max)
                rt(v)
                v.reciprocal(sc1[0:1, 1:2], sc1[0:1, 0:1])
                rt(v)
                s_ve.inc(v.engine_nop())
            _v(vscal)

            for c2 in range(2):
                vw2 = s_ve.n

                def perep(t, c2=c2, vw2=vw2):
                    t.wait_ge(s_ve.h, vw2)
                    s_pe.inc(t.matmul(ps_c[:, 1:2], ones1[:],
                                      sc1[0:1, c2:c2 + 1], start=True, stop=True))
                _p(perep)
                pw2 = s_pe.n

                def vrep(v, which=which, c2=c2, pw2=pw2):
                    v.wait_ge(s_pe.h, pw2)
                    s_ve.inc(v.tensor_copy(
                        swrep[:, which * 2 + c2:which * 2 + c2 + 1], ps_c[:, 1:2]))
                _v(vrep)

        qev_marks = []
        qst_marks = []
        for i, (src, r, cols) in enumerate(wtiles):
            which = 0 if cols == D else 1
            vw = s_ve.n

            def syq(sync, src=src, r=r, cols=cols, i=i, vw=vw):
                sync.wait_ge(s_ve.h, vw)
                s_wld2[i % 2].inc(sync.dma_start(
                    out=wld[i % 2][:, 0:cols // 2],
                    in_=src[r * P:(r + 1) * P, 0:cols // 2]), 16)
            _s(syq)

            def syq2(a, src=src, r=r, cols=cols, i=i, vw=vw):
                a.wait_ge(s_ve.h, vw)
                s_wld2[i % 2].inc(a.dma_start(
                    out=wld[i % 2][:, cols // 2:cols],
                    in_=src[r * P:(r + 1) * P, cols // 2:cols]), 16)
            _a(syq2)
            ww = s_wld2[i % 2].n

            pw0 = s_pe.n

            def vq(v, i=i, cols=cols, which=which, ww=ww, pw0=pw0):
                v.wait_ge(s_wld2[i % 2].h, ww)
                v.wait_ge(s_pe.h, pw0)
                c = slice(0, cols)
                v.tensor_scalar(wtmp2[:, c], wld[i % 2][:, c],
                                swrep[:, which * 2 + 1:which * 2 + 2], 0.5,
                                ALU.mult, ALU.add)
                v.drain()
                v.tensor_scalar(qint[:, c], wtmp2[:, c], 1.0, None, ALU.mult)
                v.drain()
                v.tensor_scalar(wsgn[:, c], qint[:, c], 1, None, ALU.mult)
                v.drain()
                v.tensor_tensor(wtmp2[:, c], wtmp2[:, c], wsgn[:, c],
                                ALU.subtract)
                v.drain()
                v.tensor_scalar(wtmp2[:, c], wtmp2[:, c], 0.0, 1.0,
                                ALU.is_ge, ALU.subtract)
                v.drain()
                v.tensor_tensor(wsgn[:, c], wsgn[:, c], wtmp2[:, c], ALU.add)
                v.drain()
                v.tensor_scalar(qint[:, c], wsgn[:, c], 1.0, None, ALU.mult)
                v.drain()
                v.tensor_scalar(qint[:, c], qint[:, c], -1, 1,
                                ALU.max, ALU.min)
                v.drain()
                s_ve.inc(v.tensor_scalar(qtile[:, c], qint[:, c], 1, None,
                                         ALU.mult))
            _v(vq)
            qdone = s_ve.n

            nk = cols // P
            for k in range(nk):
                pst = [ps_tb, ps_tb2][k % 2]
                slot = len(qev_marks) % 2
                bank_free = qev_marks[-2] if len(qev_marks) >= 2 else 0
                stg_free = s_sg2[slot].n

                def peq(t, pst=pst, k=k, qdone=qdone, bank_free=bank_free):
                    t.wait_ge(s_ve.h, max(qdone, bank_free))
                    s_pe.inc(t.transpose(pst[:], qtile[:, k * P:(k + 1) * P],
                                         idb[:]))
                _p(peq)
                pw = s_pe.n

                def veq(v, pst=pst, k=k, pw=pw, slot=slot, stg_free=stg_free):
                    v.wait_ge(s_pe.h, pw)
                    v.wait_ge(s_sg2[slot].h, stg_free)
                    s_ve.inc(v.tensor_copy(stg[slot][:], pst[:]))
                _v(veq)
                ew = s_ve.n
                qev_marks.append(ew)

                def syst(sync, k=k, r=r, cols=cols, ew=ew, slot=slot):
                    sync.wait_ge(s_ve.h, ew)
                    if cols == D:
                        dst = w1qt[:, k * FF + r * P: k * FF + (r + 1) * P]
                    else:
                        dst = w2qt[:, k * D + r * P: k * D + (r + 1) * P]
                    s_sg2[slot].inc(sync.dma_start(out=dst, in_=stg[slot][:]), 16)
                _s(syst)

        stw = s_st.n
        sg0w, sg1w = s_sg2[0].n, s_sg2[1].n

        def vbarA(v, stw=stw, sg0w=sg0w, sg1w=sg1w):
            v.wait_ge(s_st.h, stw)
            v.wait_ge(s_sg2[0].h, sg0w)
            v.wait_ge(s_sg2[1].h, sg1w)
            s_bar.inc(v.engine_nop())
        _v(vbarA)
    barA = s_bar.n
    actx.close()

    bctx = contextlib.ExitStack()
    bsb = lambda n_, s, d: bctx.enter_context(nc.sbuf_tensor(n_, s, d))
    xld = [bsb(f"xld{i}", [P, D], F32) for i in range(2)]
    vdump = bsb("vdump", [P, D], F32)
    xnb = bsb("xnb", [P, D], BF16)
    xnT = bsb("xnT", [P, DK * SUP], BF16)
    stat = bsb("stat", [P, 16], F32)
    nsc = bsb("nsc", [P, 4], F32)
    nsc2 = bsb("nsc2", [P, 4], F32)
    asup = [bsb(f"asup{i}", [P, SUP * DK], F32) for i in range(2)]
    zsup = [bsb(f"zsup{i}", [P, SUP * DK], F32) for i in range(2)]
    csup = bsb("csup", [P, SUP * DK], F32)
    q2sup = bsb("q2sup", [P, 2 * SUP * DK], F32)
    psup = bsb("psup", [P, SUP * DK], F32)
    rscr = bsb("rscr", [P, SUP], F32)
    zpad = bsb("zpad", [P, 2 * DK], F32)
    ozsup = bsb("ozsup", [P, SUP * DK], F32)
    b16s = bsb("b16s", [P, 16], F32)

    with nc.Block() as blk:
        act_defer = []

        def _s(f):
            blk.sync(f)

        def _v(f):
            blk.vector(f)

        def _a(f):
            blk.scalar(f)

        def _p(f):
            blk.tensor(f)

        def vz(v):
            v.wait_ge(s_bar.h, barA)
            v.memset(zpad[:], 0.0)
            v.memset(q2sup[:], 0.0)
            s_ve.inc(v.engine_nop())
        _v(vz)

        def stream_build(v, sp, halo):
            n = SUP * DK
            v.tensor_tensor(psup[:, 0:n - DK], asup[sp % 2][:, DK:n],
                            zsup[sp % 2][:, 0:n - DK], ALU.mult)
            v.tensor_tensor(psup[:, n - DK:n], halo,
                            zsup[sp % 2][:, n - DK:n], ALU.mult)
            v.tensor_scalar(ozsup[:], zsup[sp % 2][:], -1.0, 1.0,
                            ALU.mult, ALU.add)
            v.drain()
            q2a = bass.AP(q2sup, 1, [[2 * n, P], [2, n - DK]])
            q2b = bass.AP(q2sup, 1 + 2 * (n - DK), [[2 * n, P], [2, DK]])
            v.tensor_tensor(q2a, asup[sp % 2][:, DK:n],
                            psup[:, 0:n - DK], ALU.subtract)
            v.tensor_tensor(q2b, halo,
                            psup[:, n - DK:n], ALU.subtract)
            s_ve.inc(v.engine_nop())

        prev_xev = []
        xnb_mark = 0
        for s in range(NSUP):
            vwA = s_ve.n
            aw2, vred2, bpq_, bsd2, bpq2_ = {}, {}, {}, {}, {}
            for half in range(2):
                tau = 2 * s + half
                off = 8 * half

                def syx(sync, tau=tau, s=s, vwA=vwA):
                    sync.wait_ge(s_bar.h, barA)
                    if s >= 1:
                        sync.wait_ge(s_ve.h, vwA)
                    s_xl2[tau % 2].inc(sync.dma_start(
                        out=xld[tau % 2][:, 0:D // 2],
                        in_=dx[tau * P:(tau + 1) * P, 0:D // 2]), 16)
                _s(syx)

                def syxg(gp, tau=tau, s=s, vwA=vwA):
                    gp.wait_ge(s_bar.h, barA)
                    if s >= 1:
                        gp.wait_ge(s_ve.h, vwA)
                    s_xl2[tau % 2].inc(gp.dma_start(
                        out=xld[tau % 2][:, D // 2:D],
                        in_=dx[tau * P:(tau + 1) * P, D // 2:D]), 16)
                blk.gpsimd(syxg)
                xw = s_xl2[tau % 2].n
                s_ae.reserve()

                def act_sq(a, tau=tau, xw=xw, off=off):
                    a.wait_ge(s_xl2[tau % 2].h, xw)
                    s_ae.incr(a.activation(vdump[:], xld[tau % 2][:], AF.Square,
                                           accum_out=stat[:, 1 + off:2 + off]))
                act_defer.append(act_sq)
                aw2[half] = s_ae.n

                def vln_a(v, tau=tau, xw=xw, off=off):
                    v.wait_ge(s_xl2[tau % 2].h, xw)
                    s_ve.inc(v.tensor_reduce(stat[:, off:1 + off],
                                             xld[tau % 2][:], AX, ALU.add))
                _v(vln_a)
                vred2[half] = s_ve.n

            for half in range(2):
                off = 8 * half
                o2 = 2 * half

                def p_bln(gp, aw=aw2[half], vred=vred2[half], off=off, o2=o2):
                    gp.wait_ge(s_ae.h, aw)
                    gp.wait_ge(s_ve.h, vred)
                    gp.tensor_scalar(stat[:, 2 + off:3 + off],
                                     stat[:, off:1 + off], 1.0 / D, None,
                                     ALU.mult)
                    gp.tensor_scalar(stat[:, 3 + off:4 + off],
                                     stat[:, 1 + off:2 + off], 1.0 / D, None,
                                     ALU.mult)
                    gp.tensor_tensor(nsc[:, o2:o2 + 1], stat[:, 2 + off:3 + off],
                                     stat[:, 2 + off:3 + off], ALU.mult)
                    gp.tensor_tensor(stat[:, 3 + off:4 + off],
                                     stat[:, 3 + off:4 + off], nsc[:, o2:o2 + 1],
                                     ALU.subtract)
                    s_pq.inc(gp.tensor_scalar(stat[:, 3 + off:4 + off],
                                              stat[:, 3 + off:4 + off], EPS,
                                              None, ALU.add))
                blk.gpsimd(p_bln)
                bpq_[half] = s_pq.n
            for half in range(2):
                off = 8 * half

                def v_bsd(v, bpq=bpq_[half], off=off):
                    v.wait_ge(s_pq.h, bpq)
                    rsqrt_seed(v, stat[:, 4 + off:5 + off],
                               stat[:, 3 + off:4 + off], b16s)
                    s_ve.inc(v.engine_nop())
                _v(v_bsd)
                bsd2[half] = s_ve.n
            for half in range(2):
                off = 8 * half
                o2 = 2 * half

                def p_blnb(gp, bsd=bsd2[half], off=off, o2=o2, half=half):
                    gp.wait_ge(s_ve.h, bsd)
                    rsqrt_newton(gp, stat[:, 4 + off:5 + off],
                                 stat[:, 3 + off:4 + off], nsc[:, o2 + 1:o2 + 2])
                    gp.tensor_tensor(nsc2[:, half:half + 1],
                                     stat[:, 2 + off:3 + off],
                                     stat[:, 4 + off:5 + off], ALU.mult)
                    s_pq.inc(gp.tensor_scalar(nsc2[:, half:half + 1],
                                              nsc2[:, half:half + 1], -1.0,
                                              None, ALU.mult))
                blk.gpsimd(p_blnb)
                bpq2_[half] = s_pq.n

            for half in range(2):
                tau = 2 * s + half
                off = 8 * half

                def vln(v, tau=tau, bpq2=bpq2_[half], off=off, half=half,
                        xm=xnb_mark):
                    v.wait_ge(s_pq.h, bpq2)
                    v.wait_ge(s_pe.h, xm)
                    v.tensor_scalar(xnb[:], xld[tau % 2][:],
                                    stat[:, 4 + off:5 + off],
                                    nsc2[:, half:half + 1], ALU.mult, ALU.add)
                    s_ve.inc(v.engine_nop())
                _v(vln)
                lnw = s_ve.n

                xev_marks = []
                for k in range(DK):
                    pst = [ps_tb, ps_tb2][k % 2]
                    if k >= 2:
                        bank_free = xev_marks[k - 2]
                    elif len(prev_xev) >= 8:
                        bank_free = prev_xev[k + 6]
                    else:
                        bank_free = None

                    def pex(t, pst=pst, k=k, bank_free=bank_free, lnw=lnw):
                        t.wait_ge(s_ve.h, lnw)
                        if bank_free is not None:
                            t.wait_ge(s_ae.h, bank_free)
                        s_pe.inc(t.transpose(pst[:], xnb[:, k * P:(k + 1) * P],
                                             idb[:]))
                    _p(pex)
                    pw = s_pe.n
                    s_ae.reserve()

                    def aex(a, pst=pst, k=k, half=half, pw=pw):
                        a.wait_ge(s_pe.h, pw)
                        col = k * SUP + half * P
                        s_ae.incr(a.copy(xnT[:, col:col + P], pst[:]))
                    act_defer.append(aex)
                    xev_marks.append(s_ae.n)
                prev_xev = xev_marks
                xnb_mark = s_pe.n

            xnT_ready = s_ae.n
            evac_marks = {}

            for m in range(24):
                pss = [ps_a, ps_b, ps_c, ps_d][m % 4]
                bank_free = evac_marks.get(m - 4)

                def pem(t, m=m, pss=pss, xr=xnT_ready, bank_free=bank_free):
                    t.wait_ge(s_ae.h, xr)
                    if bank_free is not None:
                        t.wait_ge(bank_free[0].h, bank_free[1])
                    ins = None
                    for k in range(DK):
                        col = (k * 24 + m) * P
                        ins = t.matmul(pss[:, 0:SUP], wT[:, col:col + P],
                                       xnT[:, k * SUP:(k + 1) * SUP],
                                       start=(k == 0), stop=(k == DK - 1))
                    s_pe.inc(ins)
                _p(pem)
                pw = s_pe.n
                j = m % 8

                if m < 8:
                    def vevc(v, m=m, j=j, pss=pss, pw=pw):
                        v.wait_ge(s_pe.h, pw)
                        dst = bass.AP(csup, j, [[SUP * DK, P], [DK, SUP]])
                        s_ve.inc(v.tensor_scalar(dst, pss[:, 0:SUP],
                                                 bprime[:, m:m + 1], None,
                                                 ALU.add))
                    _v(vevc)
                    evac_marks[m] = (s_ve, s_ve.n)
                elif m < 16:
                    vbw = s_ve.n
                    s_ae.reserve()

                    def aevz(a, m=m, j=j, s=s, pss=pss, pw=pw, vbw=vbw):
                        a.wait_ge(s_pe.h, pw)
                        a.wait_ge(s_ve.h, vbw)
                        dst = bass.AP(zsup[s % 2], j, [[SUP * DK, P], [DK, SUP]])
                        s_ae.incr(a.activation(dst, pss[:, 0:SUP], AF.Sigmoid,
                                               bias=bprime[:, m:m + 1]))
                    act_defer.append(aevz)
                    evac_marks[m] = (s_ae, s_ae.n)
                else:
                    vaw = s_ve.n
                    s_ae.reserve()

                    def aevr(a, m=m, pss=pss, pw=pw, vaw=vaw):
                        a.wait_ge(s_pe.h, pw)
                        a.wait_ge(s_ve.h, vaw)
                        s_ae.incr(a.activation(rscr[:], pss[:, 0:SUP], AF.Sigmoid,
                                               bias=bprime[:, m:m + 1]))
                    act_defer.append(aevr)
                    evac_marks[m] = (s_ae, s_ae.n)
                    aw2 = s_ae.n

                    def veva(v, m=m, j=j, s=s, aw2=aw2):
                        v.wait_ge(s_ae.h, aw2)
                        dst = bass.AP(asup[s % 2], j, [[SUP * DK, P], [DK, SUP]])
                        s_ve.inc(v.tensor_scalar(dst, rscr[:],
                                                 utile[:, j:j + 1], None,
                                                 ALU.mult))
                    _v(veva)

            vws, aws = s_ve.n, s_ae.n

            def syzc(sync, s=s, vws=vws, aws=aws):
                sync.wait_ge(s_ve.h, vws)
                sync.wait_ge(s_ae.h, aws)
                c0 = s * SUP * DK
                s_st.inc(sync.dma_start(out=sZ[:, c0:c0 + SUP * DK],
                                        in_=zsup[s % 2][:]), 16)
                s_st.inc(sync.dma_start(out=sC[:, c0:c0 + SUP * DK],
                                        in_=csup[:]), 16)
            _s(syzc)
            zcw = s_st.n

            def vcw(v, zcw=zcw):
                v.wait_ge(s_st.h, zcw)
                s_ve.inc(v.engine_nop())
            _v(vcw)

            if s >= 1:
                stm = s_st.n

                def vbuild(v, s=s, stm=stm):
                    v.wait_ge(s_st.h, stm)
                    stream_build(v, s - 1, asup[s % 2][:, 0:DK])
                _v(vbuild)
                bw = s_ve.n

                def syqp(sync, s=s, bw=bw):
                    sync.wait_ge(s_ve.h, bw)
                    c0 = (s - 1) * SUP * DK
                    s_st.inc(sync.dma_start(
                        out=sQ2[:, 2 * c0:2 * c0 + 2 * SUP * DK],
                        in_=q2sup[:]), 16)
                    s_st.inc(sync.dma_start(out=sP[:, c0:c0 + SUP * DK],
                                            in_=psup[:]), 16)
                    s_st.inc(sync.dma_start(out=sOZ[:, c0:c0 + SUP * DK],
                                            in_=ozsup[:]), 16)
                _s(syqp)

        stm = s_st.n

        def vbuildl(v, stm=stm):
            v.wait_ge(s_st.h, stm)
            stream_build(v, NSUP - 1, zpad[:, 0:DK])
        _v(vbuildl)
        bw = s_ve.n

        def syl(sync, bw=bw):
            sync.wait_ge(s_ve.h, bw)
            c0 = (NSUP - 1) * SUP * DK
            s_st.inc(sync.dma_start(
                out=sQ2[:, 2 * c0:2 * c0 + 2 * SUP * DK], in_=q2sup[:]), 16)
            s_st.inc(sync.dma_start(out=sP[:, c0:c0 + SUP * DK],
                                    in_=psup[:]), 16)
            s_st.inc(sync.dma_start(out=sOZ[:, c0:c0 + SUP * DK],
                                    in_=ozsup[:]), 16)
            s_st.inc(sync.dma_start(out=sC[:, T * DK:(T + 2) * DK],
                                    in_=zpad[:]), 16)
        _s(syl)
        stw = s_st.n

        def act_flush(a):
            a.preload_act_table(AF.Sigmoid)
            for fn in act_defer:
                fn(a)
        blk.scalar(act_flush)

        def vbarB(v, stw=stw):
            v.wait_ge(s_st.h, stw)
            s_bar.inc(v.engine_nop())
        _v(vbarB)
    barB = s_bar.n
    bctx.close()
    wtctx.close()

    cctx = contextlib.ExitStack()
    csb = lambda n_, s, d: cctx.enter_context(nc.sbuf_tensor(n_, s, d))
    fw1 = csb("fw1", [P, DK * FF], BF16)
    fw2 = csb("fw2", [P, FK * D], BF16)

    chctx = contextlib.ExitStack()
    hsb = lambda n_, s, d: chctx.enter_context(nc.sbuf_tensor(n_, s, d))
    W = CIRC * TC
    W16 = W * 2 * DK
    std0 = hsb("std0", [P, W16], F32)
    stp = hsb("stp", [P, CIRC * TC * DK], F32)
    stz = hsb("stz", [P, CIRC * TC * DK], F32)
    stc = hsb("stc", [P, CIRC * TC * DK], F32)
    stoz = hsb("stoz", [P, CIRC * TC * DK], F32)
    cinit = hsb("cinit", [P, 2 * DK], F32)
    ysc = hsb("ysc", [P, YCIRC * DK], F32)
    hd1 = hsb("hd1", [P, W16], F32)
    garg = hsb("garg", [P, 32], F32)
    zinit = hsb("zinit", [P, 1], F32)
    ohb = hsb("ohb", [P, DK], F32)
    zsb = hsb("zsb", [P, DK], F32)
    nbb = hsb("nbb", [P, DK], F32)
    ysT = [hsb(f"ysT{i}", [P, D], F32) for i in range(3)]

    with nc.Block() as blk:
        def sy_w(sync):
            sync.wait_ge(s_bar.h, barB)
            for jj in range(4):
                s_misc.inc(sync.dma_start(
                    out=fw1[:, jj * 8192:(jj + 1) * 8192],
                    in_=w1qt[:, jj * 8192:(jj + 1) * 8192]), 16)
                s_misc.inc(sync.dma_start(
                    out=fw2[:, jj * 8192:(jj + 1) * 8192],
                    in_=w2qt[:, jj * 8192:(jj + 1) * 8192]), 16)
            s_misc.inc(sync.dma_start(out=cinit[:], in_=sC[:, 0:2 * DK]), 16)
        blk.sync(sy_w)
        w_loaded = s_misc.n

        def sy_loader(sync):
            for c in range(NCH):
                if c >= CIRC:
                    sync.wait_ge(s_vg.h, (c - 1) * TC)
                    sync.wait_ge(s_vs.h, (c - 1) * TC)
                lo = (c % CIRC) * TC * DK
                c0 = c * TC * DK
                sl = s_ldc2[c % 2]
                sl.inc(sync.dma_start(out=std0[:, 2 * lo:2 * lo + 2 * TC * DK],
                                      in_=sQ2[:, 2 * c0:2 * c0 + 2 * TC * DK]),
                       16)
                sl.inc(sync.dma_start(out=stp[:, lo:lo + TC * DK],
                                      in_=sP[:, c0:c0 + TC * DK]), 16)
                sl.inc(sync.dma_start(out=stoz[:, lo:lo + TC * DK],
                                      in_=sOZ[:, c0:c0 + TC * DK]), 16)
                sl.inc(sync.dma_start(out=stz[:, lo:lo + TC * DK],
                                      in_=sZ[:, c0:c0 + TC * DK]), 16)
                sl.inc(sync.dma_start(out=stc[:, lo:lo + TC * DK],
                                      in_=sC[:, c0 + DK:c0 + (TC + 1) * DK]), 16)
                if c % 4 == 3 and c >= 11:
                    tau = (c - 11) // 4
                    sync.wait_ge(s_yse.h, 8 * (tau + 1))
                    s_yst2[tau % 3].inc(sync.dma_start(
                        out=ytm[tau * P:(tau + 1) * P, :],
                        in_=ysT[tau % 3][:]), 16)
            for tau in range(62, NTILE):
                sync.wait_ge(s_yse.h, 8 * (tau + 1))
                s_yst2[tau % 3].inc(sync.dma_start(
                    out=ytm[tau * P:(tau + 1) * P, :],
                    in_=ysT[tau % 3][:]), 16)
        blk.sync(sy_loader)
        assert sum(s.n for s in s_yst2) == 16 * NTILE

        def v_scan(v):
            v.memset(zinit[:], 0.0)
            v.drain()
            for t in range(T):
                if t % TC == 0:
                    ch = t // TC
                    v.wait_ge(s_ldc2[ch % 2].h, 80 * (ch // 2 + 1))
                v.wait_ge(s_ha.h, t + 1)
                v.wait_ge(s_pm.h, t + 1)
                sl16 = (t % W) * 2 * DK
                out = garg[:, (t % 2) * 16:(t % 2) * 16 + 16]
                s_vg.inc(v.tensor_tensor_scan(
                    out, std0[:, sl16:sl16 + 16], hd1[:, sl16:sl16 + 16],
                    zinit[:], ALU.mult, ALU.add))
        blk.vector(v_scan)
        assert s_vg.n == T

        def act_chain(a):
            a.preload_act_table(AF.Tanh)
            a.wait_ge(s_misc.h, w_loaded)
            for t in range(T):
                sl16 = (t % W) * 2 * DK
                hout = bass.AP(hd1, sl16, [[W16, P], [2, DK]])
                if t == 0:
                    gin = cinit[:, 0:DK]
                else:
                    a.wait_ge(s_vg.h, t)
                    gin = bass.AP(garg, ((t - 1) % 2) * 16 + 1,
                                  [[32, P], [2, DK]])
                s_ha.inc(a.activation(hout, gin, AF.Tanh))
                if t >= TILE and t % 16 == 8:
                    tau = t // TILE - 1
                    k = (t % TILE) // 16
                    a.wait_ge(s_ytr.h, 8 * tau + k + 1)
                    if tau >= 3 and k == 0:
                        a.wait_ge(s_yst2[tau % 3].h, 16 * ((tau - 3) // 3 + 1))
                    pst = [ps_tf, ps_tf2][k % 2]
                    s_yse.inc(a.copy(ysT[tau % 3][:, k * P:(k + 1) * P], pst[:]))
            for tau in (62, 63):
                for k in range(DK):
                    done = 8 * tau + k + 1
                    if tau == 62 and k < 8:
                        if s_yse.n >= done:
                            continue
                    a.wait_ge(s_ytr.h, done)
                    a.wait_ge(s_yst2[tau % 3].h, 16 * ((tau - 3) // 3 + 1))
                    pst = [ps_tf, ps_tf2][k % 2]
                    s_yse.inc(a.copy(ysT[tau % 3][:, k * P:(k + 1) * P], pst[:]))
        blk.scalar(act_chain)
        assert s_yse.n == 8 * NTILE, s_yse.n
        assert s_ha.n == T

        def pool_chain(gp):
            gp.wait_ge(s_misc.h, w_loaded)
            gp.memset(ysc[:, (YCIRC - 1) * DK:YCIRC * DK], 0.0)
            for t in range(T):
                cs = (t % (CIRC * TC)) * DK
                sl16 = (t % W) * 2 * DK
                sprev = ((t - 1) % YCIRC) * DK
                scur = (t % YCIRC) * DK
                if t % TC == 0:
                    ch = t // TC
                    gp.wait_ge(s_ldc2[ch % 2].h, 80 * (ch // 2 + 1))
                if t % TILE == 0 and t >= 2 * TILE:
                    gp.wait_ge(s_ytr.h, 8 * (t // TILE - 1))
                gp.tensor_tensor(nbb[:], stp[:, cs:cs + DK],
                                 ysc[:, sprev:sprev + DK], ALU.mult)
                nout = bass.AP(hd1, sl16 + 1, [[W16, P], [2, DK]])
                s_pm.inc(gp.tensor_tensor(nout, nbb[:], stc[:, cs:cs + DK],
                                          ALU.add))
                gp.tensor_tensor(zsb[:], stz[:, cs:cs + DK],
                                 ysc[:, sprev:sprev + DK], ALU.mult)
                gp.wait_ge(s_ha.h, t + 1)
                hin = bass.AP(hd1, sl16, [[W16, P], [2, DK]])
                gp.tensor_tensor(ohb[:], stoz[:, cs:cs + DK], hin, ALU.mult)
                s_vs.inc(gp.tensor_tensor(ysc[:, scur:scur + DK], zsb[:],
                                          ohb[:], ALU.add))
        blk.gpsimd(pool_chain)
        assert s_pm.n == T and s_vs.n == T

        def pe_ys(t):
            for tau in range(NTILE):
                for k in range(DK):
                    pst = [ps_tf, ps_tf2][k % 2]
                    if k == 0:
                        t.wait_ge(s_vs.h, TILE * (tau + 1))
                    cnt = 8 * tau + k
                    if cnt >= 2:
                        t.wait_ge(s_yse.h, cnt - 1)
                    src = bass.AP(ysc, ((tau * TILE) % YCIRC) * DK + k,
                                  [[YCIRC * DK, P], [DK, TILE]])
                    s_ytr.inc(t.transpose(pst[:], src, idf[:]))
        blk.tensor(pe_ys)
        assert s_ytr.n == 8 * NTILE

    chctx.close()

    dctx = contextlib.ExitStack()
    dsb = lambda n_, s, d: dctx.enter_context(nc.sbuf_tensor(n_, s, d))
    fys = dsb("fys", [P, D], F32)
    fxh = dsb("fxh", [P, D], F32)
    fxm = dsb("fxm", [P, D], F32)
    flnb = dsb("flnb", [P, D], F32)
    fst = dsb("fst", [P, 32], F32)
    fns = dsb("fns", [P, 4], F32)
    fns2 = dsb("fns2", [P, 4], F32)
    fi32 = dsb("fi32", [P, D], I32)
    fxqb = dsb("fxqb", [P, D], BF16)
    fxqT = dsb("fxqT", [P, DK * P], BF16)
    fvb = dsb("fvb", [P, 512], F32)
    feb = dsb("feb", [P, 512], F32)
    fyb = dsb("fyb", [P, FF], BF16)
    fhqT = dsb("fhqT", [P, FK * P], BF16)
    fy2 = dsb("fy2", [P, D], F32)
    f16s = dsb("f16s", [P, 16], F32)

    with nc.Block() as blk:
        act_defer = []

        def _v(f):
            blk.vector(f)

        def _a(f):
            act_defer.append(f)

        def _p(f):
            blk.tensor(f)

        def _g(f):
            blk.gpsimd(f)

        for i in range(NTL):
            vprev = s_ve.n

            def g_load(gp, i=i, vprev=vprev):
                if i == 0:
                    for _sl in range(3):
                        gp.wait_ge(s_yst2[_sl].h, 16 * len(
                            [t_ for t_ in range(NTILE) if t_ % 3 == _sl]))
                    gp.wait_ge(s_misc.h, s_misc.n)
                else:
                    gp.wait_ge(s_ve.h, vprev)
                    gp.wait_ge(s_out.h, 16 * i)
                s_po.inc(gp.indirect_dma_start(
                    fys[:], None, ytm[0:T, :],
                    bass.IndirectOffsetOnAxis(yidxt[:, i:i + 1], 0)), 16)
                s_po.inc(gp.dma_start(out=fxh[:],
                                      in_=dxh[i * P:(i + 1) * P, :]), 16)
            _g(g_load)
            gw = s_po.n

            s_ae.reserve()

            def a_st1(a, gw=gw):
                a.wait_ge(s_po.h, gw)
                a.activation(flnb[:], fys[:], AF.Copy, accum_out=fst[:, 0:1])
                a.drain()
                s_ae.incr(a.activation(flnb[:], fys[:], AF.Square,
                                       accum_out=fst[:, 1:2]))
            _a(a_st1)
            aw = s_ae.n

            def p_ln1(gp, aw=aw):
                gp.wait_ge(s_ae.h, aw)
                gp.tensor_scalar(fst[:, 2:3], fst[:, 0:1], 1.0 / D, None,
                                 ALU.mult)
                gp.tensor_scalar(fst[:, 3:4], fst[:, 1:2], 1.0 / D, None,
                                 ALU.mult)
                gp.tensor_tensor(fns[:, 0:1], fst[:, 2:3], fst[:, 2:3],
                                 ALU.mult)
                gp.tensor_tensor(fst[:, 3:4], fst[:, 3:4], fns[:, 0:1],
                                 ALU.subtract)
                s_pq.inc(gp.tensor_scalar(fst[:, 3:4], fst[:, 3:4], EPS, None,
                                          ALU.add))
            _g(p_ln1)
            pqa = s_pq.n

            def v_sd1(v, pqa=pqa):
                v.wait_ge(s_pq.h, pqa)
                rsqrt_seed(v, fst[:, 4:5], fst[:, 3:4], f16s)
                s_ve.inc(v.engine_nop())
            _v(v_sd1)
            vsd = s_ve.n

            def p_ln1b(gp, vsd=vsd):
                gp.wait_ge(s_ve.h, vsd)
                rsqrt_newton(gp, fst[:, 4:5], fst[:, 3:4], fns[:, 2:3])
                gp.tensor_tensor(fns[:, 1:2], fst[:, 2:3], fst[:, 4:5],
                                 ALU.mult)
                s_pq.inc(gp.tensor_scalar(fns[:, 1:2], fns[:, 1:2], -1.0,
                                          None, ALU.mult))
            _g(p_ln1b)
            pqw = s_pq.n

            def v_ln1(v, pqw=pqw):
                v.wait_ge(s_pq.h, pqw)
                for j in (slice(0, 512), slice(512, 1024)):
                    v.tensor_scalar(flnb[:, j], fys[:, j], fst[:, 4:5],
                                    fns[:, 1:2], ALU.mult, ALU.add)
                v.drain()
                v.tensor_tensor(fxm[:, 0:512], flnb[:, 0:512], fxh[:, 0:512],
                                ALU.add)
                s_ve.inc(v.tensor_tensor(fxm[:, 512:1024], flnb[:, 512:1024],
                                         fxh[:, 512:1024], ALU.add))
            _v(v_ln1)
            vw = s_ve.n

            s_ae.reserve()

            def a_st2(a, vw=vw):
                a.wait_ge(s_ve.h, vw)
                a.activation(flnb[:], fxm[:], AF.Copy, accum_out=fst[:, 5:6])
                a.drain()
                s_ae.incr(a.activation(flnb[:], fxm[:], AF.Square,
                                       accum_out=fst[:, 6:7]))
            _a(a_st2)
            aw = s_ae.n

            pwq = s_pe.n

            def p_q1(gp, aw=aw):
                gp.wait_ge(s_ae.h, aw)
                gp.tensor_scalar(fst[:, 7:8], fst[:, 5:6], 1.0 / D, None,
                                 ALU.mult)
                gp.tensor_scalar(fst[:, 8:9], fst[:, 6:7], 1.0 / D, None,
                                 ALU.mult)
                gp.tensor_tensor(fns[:, 0:1], fst[:, 7:8], fst[:, 7:8],
                                 ALU.mult)
                gp.tensor_tensor(fst[:, 8:9], fst[:, 8:9], fns[:, 0:1],
                                 ALU.subtract)
                gp.tensor_scalar(fst[:, 9:10], fst[:, 8:9], 1.0 + EPS,
                                 EPS * EPS, ALU.mult, ALU.add)
                s_pq.inc(gp.tensor_scalar(fst[:, 7:8], fst[:, 7:8], -1.0,
                                          None, ALU.mult))
            _g(p_q1)
            pq1a = s_pq.n

            def v_sdq1(v, pq1a=pq1a):
                v.wait_ge(s_pq.h, pq1a)
                rsqrt_seed(v, fst[:, 10:11], fst[:, 9:10], f16s)
                s_ve.inc(v.engine_nop())
            _v(v_sdq1)
            vsdq = s_ve.n

            def p_q1b(gp, vsdq=vsdq):
                gp.wait_ge(s_ve.h, vsdq)
                rsqrt_newton(gp, fst[:, 10:11], fst[:, 9:10], fns[:, 2:3])
                s_pq.inc(gp.engine_nop())
            _g(p_q1b)
            pq1 = s_pq.n

            def v_quant1(v, pq1=pq1, pwq=pwq):
                v.wait_ge(s_pq.h, pq1)
                v.wait_ge(s_pe.h, pwq)
                v.tensor_scalar(fys[:], fxm[:], fst[:, 7:8], None, ALU.add)
                v.drain()
                v.tensor_reduce(fst[:, 12:13], fys[:], AX, ALU.max,
                                apply_absolute_value=True)
                v.drain()
                v.tensor_tensor(fns[:, 1:2], fst[:, 12:13], fst[:, 10:11],
                                ALU.mult)
                v.drain()
                v.tensor_scalar(fns[:, 1:2], fns[:, 1:2], EPS, None, ALU.max)
                rt(v)
                v.drain()
                v.reciprocal(fns[:, 2:3], fns[:, 1:2])
                rt(v)
                v.drain()
                v.tensor_scalar(fns[:, 3:4], fns[:, 2:3], 127.0, None, ALU.mult)
                v.drain()
                v.tensor_tensor(fst[:, 13:14], fst[:, 10:11], fns[:, 3:4],
                                ALU.mult)
                v.drain()
                v.tensor_scalar(fns[:, 1:2], fns[:, 1:2], 1.0 / 127.0, None,
                                ALU.mult)
                v.drain()
                v.tensor_tensor(fst[:, 14:15], fns[:, 1:2], swrep[:, 0:1],
                                ALU.mult)
                v.drain()
                hs = [slice(0, 512), slice(512, 1024)]
                for j in (0, 1):
                    v.tensor_scalar(flnb[:, hs[j]], fys[:, hs[j]],
                                    fst[:, 13:14], 0.5, ALU.mult, ALU.add)
                v.drain()
                for j in (0, 1):
                    v.tensor_scalar(fi32[:, hs[j]], flnb[:, hs[j]], 1.0, None,
                                    ALU.mult)
                v.drain()
                for j in (0, 1):
                    v.tensor_scalar(fys[:, hs[j]], fi32[:, hs[j]], 1, None,
                                    ALU.mult)
                v.drain()
                for j in (0, 1):
                    v.tensor_tensor(flnb[:, hs[j]], flnb[:, hs[j]],
                                    fys[:, hs[j]], ALU.subtract)
                v.drain()
                for j in (0, 1):
                    v.tensor_scalar(flnb[:, hs[j]], flnb[:, hs[j]], 0.0, 1.0,
                                    ALU.is_ge, ALU.subtract)
                v.drain()
                for j in (0, 1):
                    v.tensor_tensor(fys[:, hs[j]], fys[:, hs[j]],
                                    flnb[:, hs[j]], ALU.add)
                v.drain()
                v.tensor_copy(fxqb[:, 0:512], fys[:, 0:512])
                s_ve.inc(v.tensor_copy(fxqb[:, 512:1024], fys[:, 512:1024]))
            _v(v_quant1)
            qw = s_ve.n

            xqev_marks = []
            for k in range(DK):
                pst = [ps_tb, ps_tb2][k % 2]
                bank_free = xqev_marks[k - 2] if k >= 2 else qw

                def pext(t, pst=pst, k=k, bank_free=bank_free):
                    t.wait_ge(s_ve.h, bank_free)
                    s_pe.inc(t.transpose(pst[:], fxqb[:, k * P:(k + 1) * P],
                                         idb[:]))
                _p(pext)
                pw = s_pe.n

                def vext(v, pst=pst, k=k, pw=pw):
                    v.wait_ge(s_pe.h, pw)
                    s_ve.inc(v.tensor_copy(fxqT[:, k * P:(k + 1) * P], pst[:]))
                _v(vext)
                xqev_marks.append(s_ve.n)
            xtw = s_ve.n

            sq_cols = []
            mx_cols = []
            for n in range(8):
                pss = [ps_a, ps_b][n % 2]
                vw2 = s_ve.n

                def pmm1(t, n=n, pss=pss, xtw=xtw, vw2=vw2):
                    t.wait_ge(s_ve.h, vw2 if n >= 2 else xtw)
                    ins = None
                    for k in range(DK):
                        ins = t.matmul(pss[:],
                                       fxqT[:, k * P:(k + 1) * P],
                                       fw1[:, k * FF + n * 512:k * FF + (n + 1) * 512],
                                       start=(k == 0), stop=(k == DK - 1))
                    s_pe.inc(ins)
                _p(pmm1)
                pw = s_pe.n

                def vev1(v, n=n, pss=pss, pw=pw):
                    v.wait_ge(s_pe.h, pw)
                    s_ve.inc(v.scalar_tensor_tensor(
                        fvb[:], pss[:], fst[:, 14:15],
                        b1rep[:, n * 512:(n + 1) * 512], ALU.mult, ALU.add))
                _v(vev1)
                vw3 = s_ve.n

                s_ae.reserve()

                def agelu(a, n=n, vw3=vw3):
                    a.wait_ge(s_ve.h, vw3)
                    s_ae.incr(a.activation(fyb[:, n * 512:(n + 1) * 512], fvb[:],
                                           AF.Gelu))
                _a(agelu)
                aw3 = s_ae.n

                s_ae.reserve()

                def asq(a, n=n, aw3=aw3):
                    a.drain()
                    s_ae.incr(a.activation(feb[:], fyb[:, n * 512:(n + 1) * 512],
                                           AF.Square, accum_out=fst[:, 16 + n:17 + n]))
                _a(asq)
                sq_cols.append(16 + n)

                def vmx(v, n=n, aw3=aw3):
                    v.wait_ge(s_ae.h, aw3)
                    v.tensor_reduce(fst[:, 24 + n:25 + n],
                                    fyb[:, n * 512:(n + 1) * 512], AX, ALU.max,
                                    apply_absolute_value=True)
                    s_ve.inc(v.engine_nop())
                _v(vmx)
                mx_cols.append(24 + n)

            aw4 = s_ae.n

            def v_red8(v, aw4=aw4):
                v.wait_ge(s_ae.h, aw4)
                v.drain()
                v.tensor_reduce(fns[:, 0:1], fst[:, 16:24], AX, ALU.add)
                v.tensor_reduce(fns[:, 1:2], fst[:, 24:32], AX, ALU.max)
                v.drain()
                v.tensor_scalar(fst[:, 9:10], fns[:, 0:1], 1.0 / FF, EPS,
                                ALU.mult, ALU.add)
                v.drain()
                rsqrt_seed(v, fst[:, 15:16], fst[:, 9:10], f16s)
                s_ve.inc(v.engine_nop())
            _v(v_red8)
            vr8 = s_ve.n

            def p_q2(gp, vr8=vr8):
                gp.wait_ge(s_ve.h, vr8)
                rsqrt_newton(gp, fst[:, 15:16], fst[:, 9:10], fns[:, 2:3])
                gp.tensor_scalar(fns2[:, 0:1], fst[:, 15:16], 1.0, None,
                                 ALU.mult)
                s_pq.inc(gp.tensor_tensor(fns2[:, 1:2], fns2[:, 0:1],
                                          fns[:, 1:2], ALU.mult))
            _g(p_q2)
            pq2 = s_pq.n

            def v_quant2(v, pq2=pq2):
                v.wait_ge(s_pq.h, pq2)
                v.tensor_scalar(fns2[:, 1:2], fns2[:, 1:2], EPS, None,
                                ALU.max)
                rt(v)
                v.drain()
                v.reciprocal(fns2[:, 2:3], fns2[:, 1:2])
                rt(v)
                v.drain()
                v.tensor_scalar(fns2[:, 3:4], fns2[:, 2:3], 127.0, None, ALU.mult)
                v.drain()
                v.tensor_tensor(fns[:, 2:3], fns2[:, 0:1], fns2[:, 3:4], ALU.mult)
                v.drain()
                v.tensor_scalar(fns[:, 3:4], fns2[:, 1:2], 1.0 / 127.0, None,
                                ALU.mult)
                v.drain()
                v.tensor_tensor(fns[:, 3:4], fns[:, 3:4], swrep[:, 2:3],
                                ALU.mult)
                v.drain()
                for g in range(4):
                    ys_ = [fys[:, 0:512], fys[:, 512:1024]]
                    fs_ = [feb[:], fvb[:]]
                    is_ = [fi32[:, j * 512:(j + 1) * 512] for j in range(2)]
                    segs = [slice((2 * g + j) * 512, (2 * g + j + 1) * 512)
                            for j in range(2)]
                    for j in range(2):
                        v.tensor_scalar(ys_[j], fyb[:, segs[j]], fns[:, 2:3],
                                        0.5, ALU.mult, ALU.add)
                    v.drain()
                    for j in range(2):
                        v.tensor_scalar(is_[j], ys_[j], 1.0, None, ALU.mult)
                    v.drain()
                    for j in range(2):
                        v.tensor_scalar(fs_[j], is_[j], 1, None, ALU.mult)
                    v.drain()
                    for j in range(2):
                        v.tensor_tensor(ys_[j], ys_[j], fs_[j], ALU.subtract)
                    v.drain()
                    for j in range(2):
                        v.tensor_scalar(ys_[j], ys_[j], 0.0, 1.0, ALU.is_ge,
                                        ALU.subtract)
                    v.drain()
                    for j in range(2):
                        v.tensor_tensor(fs_[j], fs_[j], ys_[j], ALU.add)
                    v.drain()
                    for j in range(2):
                        v.tensor_copy(fyb[:, segs[j]], fs_[j])
                    v.drain()
                s_ve.inc(v.engine_nop())
            _v(v_quant2)
            q2w = s_ve.n

            hqev_marks = []
            for k in range(FK):
                pst = [ps_tb, ps_tb2][k % 2]
                bank_free = hqev_marks[k - 2] if k >= 2 else q2w

                def peht(t, pst=pst, k=k, bank_free=bank_free):
                    t.wait_ge(s_ve.h, bank_free)
                    s_pe.inc(t.transpose(pst[:], fyb[:, k * P:(k + 1) * P], idb[:]))
                _p(peht)
                pw = s_pe.n

                def veht(v, pst=pst, k=k, pw=pw):
                    v.wait_ge(s_pe.h, pw)
                    s_ve.inc(v.tensor_copy(fhqT[:, k * P:(k + 1) * P], pst[:]))
                _v(veht)
                hqev_marks.append(s_ve.n)
            htw = s_ve.n

            for n in range(2):
                pss = [ps_c, ps_d][n]

                def pmm2(t, n=n, pss=pss, htw=htw):
                    t.wait_ge(s_ve.h, htw)
                    ins = None
                    for k in range(FK):
                        ins = t.matmul(pss[:],
                                       fhqT[:, k * P:(k + 1) * P],
                                       fw2[:, k * D + n * 512:k * D + (n + 1) * 512],
                                       start=(k == 0), stop=(k == FK - 1))
                    s_pe.inc(ins)
                _p(pmm2)
                pw = s_pe.n

                def vev2(v, n=n, pss=pss, pw=pw):
                    v.wait_ge(s_pe.h, pw)
                    s_ve.inc(v.scalar_tensor_tensor(
                        fy2[:, n * 512:(n + 1) * 512], pss[:], fns[:, 3:4],
                        b2rep[:, n * 512:(n + 1) * 512], ALU.mult, ALU.add))
                _v(vev2)
            y2w = s_ve.n

            s_ae.reserve()

            def a_st3(a, y2w=y2w):
                a.wait_ge(s_ve.h, y2w)
                a.activation(flnb[:], fy2[:], AF.Copy, accum_out=fst[:, 0:1])
                a.drain()
                s_ae.incr(a.activation(flnb[:], fy2[:], AF.Square,
                                       accum_out=fst[:, 1:2]))
            _a(a_st3)
            aw5 = s_ae.n

            def p_fin(gp, aw5=aw5):
                gp.wait_ge(s_ae.h, aw5)
                gp.tensor_scalar(fst[:, 2:3], fst[:, 0:1], 1.0 / D, None,
                                 ALU.mult)
                gp.tensor_scalar(fst[:, 3:4], fst[:, 1:2], 1.0 / D, None,
                                 ALU.mult)
                gp.tensor_tensor(fns[:, 0:1], fst[:, 2:3], fst[:, 2:3],
                                 ALU.mult)
                gp.tensor_tensor(fst[:, 3:4], fst[:, 3:4], fns[:, 0:1],
                                 ALU.subtract)
                s_pq.inc(gp.tensor_scalar(fst[:, 3:4], fst[:, 3:4], EPS, None,
                                          ALU.add))
            _g(p_fin)
            pqfa = s_pq.n

            def v_sdf(v, pqfa=pqfa):
                v.wait_ge(s_pq.h, pqfa)
                rsqrt_seed(v, fst[:, 4:5], fst[:, 3:4], f16s)
                s_ve.inc(v.engine_nop())
            _v(v_sdf)
            vsdf = s_ve.n

            def p_finb(gp, vsdf=vsdf):
                gp.wait_ge(s_ve.h, vsdf)
                rsqrt_newton(gp, fst[:, 4:5], fst[:, 3:4], fns[:, 2:3])
                gp.tensor_tensor(fns[:, 0:1], fst[:, 2:3], fst[:, 4:5],
                                 ALU.mult)
                s_pq.inc(gp.tensor_scalar(fns[:, 0:1], fns[:, 0:1], -1.0,
                                          None, ALU.mult))
            _g(p_finb)
            pqf = s_pq.n

            def v_fin(v, pqf=pqf):
                v.wait_ge(s_pq.h, pqf)
                for j in (slice(0, 512), slice(512, 1024)):
                    v.tensor_scalar(flnb[:, j], fy2[:, j], fst[:, 4:5],
                                    fns[:, 0:1], ALU.mult, ALU.add)
                v.drain()
                v.tensor_tensor(fys[:, 0:512], fxm[:, 0:512], flnb[:, 0:512],
                                ALU.add)
                s_ve.inc(v.tensor_tensor(fys[:, 512:1024], fxm[:, 512:1024],
                                         flnb[:, 512:1024], ALU.add))
            _v(v_fin)
            fw = s_ve.n

            def g_store(gp, i=i, fw=fw):
                gp.wait_ge(s_ve.h, fw)
                s_out.inc(gp.dma_start(out=dyo[i * P:(i + 1) * P, :],
                                       in_=fys[:]), 16)
            _g(g_store)

        def act_flushD(a):
            a.preload_act_table(AF.Gelu)
            for fn in act_defer:
                fn(a)
        blk.scalar(act_flushD)

        def g_final(gp):
            gp.wait_ge(s_out.h, 16 * NTL)
        blk.gpsimd(g_final)

    dctx.close()
    cctx.close()
    ctx.close()
    return nc


_NC_CACHE = None


def kernel(L0_attn_x, L0_attn_u, L0_attn_W, L0_attn_b, L0_attn_gamma,
           L0_attn_beta, L0_ffn_w1, L0_ffn_b1, L0_ffn_w2, L0_ffn_b2):
    global _NC_CACHE
    if _NC_CACHE is None:
        _NC_CACHE = build_nc()
    nc = _NC_CACHE

    x = np.ascontiguousarray(np.asarray(L0_attn_x, dtype=np.float32))
    W = np.ascontiguousarray(np.asarray(L0_attn_W, dtype=np.float32))
    bv = np.ascontiguousarray(np.asarray(L0_attn_b, dtype=np.float32)).reshape(1, F3)
    gam = np.ascontiguousarray(np.asarray(L0_attn_gamma, dtype=np.float32)).reshape(1, D)
    bet = np.ascontiguousarray(np.asarray(L0_attn_beta, dtype=np.float32)).reshape(1, D)
    u = np.ascontiguousarray(np.asarray(L0_attn_u, dtype=np.float32)).reshape(1, D)
    w1 = np.ascontiguousarray(np.asarray(L0_ffn_w1, dtype=np.float32))
    b1 = np.ascontiguousarray(np.asarray(L0_ffn_b1, dtype=np.float32)).reshape(1, FF)
    w2 = np.ascontiguousarray(np.asarray(L0_ffn_w2, dtype=np.float32))
    b2 = np.ascontiguousarray(np.asarray(L0_ffn_b2, dtype=np.float32)).reshape(1, D)
    idf = np.eye(P, dtype=np.float32)
    import ml_dtypes
    idb = np.eye(P, dtype=ml_dtypes.bfloat16)

    in_maps = []
    for c in range(8):
        b = c // 2
        H = c % 2
        xb = np.ascontiguousarray(x[b])
        xh = np.ascontiguousarray(xb[H * (T // 2):(H + 1) * (T // 2)])
        yidx = (H * (T // 2)
                + np.arange(NTL, dtype=np.int32)[None, :] * TILE
                + np.arange(P, dtype=np.int32)[:, None]).astype(np.int32)
        in_maps.append({
            "x": xb, "xh": xh, "W": W, "bvec": bv, "gamma": gam, "beta": bet,
            "u": u, "w1": w1, "b1": b1, "w2": w2, "b2": b2,
            "identf": idf, "identb": idb, "yidx": np.ascontiguousarray(yidx),
        })

    res = run_bass_kernel_spmd(nc, in_maps, core_ids=list(range(8)))
    out = np.empty((B, T, D), dtype=np.float32)
    for c in range(8):
        b = c // 2
        H = c % 2
        out[b, H * (T // 2):(H + 1) * (T // 2), :] = np.asarray(res.results[c]["yo"])
    return out
